# version 13
# baseline (speedup 1.0000x reference)
import numpy as np
from contextlib import ExitStack

import concourse.bass as bass
import concourse.mybir as mybir
from concourse.bass_utils import run_bass_kernel_spmd

F32 = mybir.dt.float32
BF16 = mybir.dt.bfloat16
F8 = mybir.dt.float8e4
AF = mybir.ActivationFunctionType
OP = mybir.AluOpType
AXX = mybir.AxisListType.X

B_, S_, D = 4, 8192, 2048
H, DH = 16, 128
NCH = D // 128
OWN = 4096
PRE = 4096
NT = PRE + OWN
T = 512
NC_ALL = NT // T
NC_OWN = OWN // T
HID = 8192
NHC = HID // 128
EPS = 1e-5
INV_D = 1.0 / D

C1BF = 0.796875
C2BF = 0.044677734375

_DBG = False
_STOP = None
_A_GROUPS = None
_A_MODE = "full"


class Ctr:
    __slots__ = ("h", "n")

    def __init__(self, h):
        self.h = h
        self.n = 0

    def inc(self, k=1):
        self.n += k
        return self.n


def build():
    nc = bass.Bass("TRN2", target_bir_lowering=False)

    xp = nc.dram_tensor("xp", [NT, D], F32, kind="ExternalInput").ap()
    wq = nc.dram_tensor("wq", [D, D], F32, kind="ExternalInput").ap()
    wv = nc.dram_tensor("wv", [D, D], F32, kind="ExternalInput").ap()
    wg = nc.dram_tensor("wg", [D, D], F32, kind="ExternalInput").ap()
    wo = nc.dram_tensor("wo", [D, D], F32, kind="ExternalInput").ap()
    beta = nc.dram_tensor("beta", [128, H], F32, kind="ExternalInput").ap()
    lng = nc.dram_tensor("lng", [128, NCH], F32, kind="ExternalInput").ap()
    lnb = nc.dram_tensor("lnb", [128, NCH], F32, kind="ExternalInput").ap()
    wgt = nc.dram_tensor("wgt", [2, D, HID], F32, kind="ExternalInput").ap()
    wlin = nc.dram_tensor("wlin", [HID, D], F32, kind="ExternalInput").ap()
    cid32 = nc.dram_tensor("cid32", [128, 128], F32, kind="ExternalInput").ap()
    cid16 = nc.dram_tensor("cid16", [128, 128], BF16, kind="ExternalInput").ap()
    conc = nc.dram_tensor("conc", [128, 1], F32, kind="ExternalInput").ap()
    conr = nc.dram_tensor("conr", [1, 128], F32, kind="ExternalInput").ap()
    out = nc.dram_tensor("out", [OWN, D], F32, kind="ExternalOutput").ap()

    xnbf_s = nc.dram_tensor("xnbf_s", [D, NT], BF16, kind="Internal").ap()
    xn8_s = nc.dram_tensor("xn8_s", [D, OWN], F8, kind="Internal").ap()
    st_s = nc.dram_tensor("st_s", [D, OWN], BF16, kind="Internal").ap()
    z1_s = nc.dram_tensor("z1_s", [D, OWN], BF16, kind="Internal").ap()
    z_s = nc.dram_tensor("z_s", [D, OWN], BF16, kind="Internal").ap()
    act8_s = nc.dram_tensor("act8_s", [HID, OWN], F8, kind="Internal").ap()
    par_s = nc.dram_tensor("par_s", [OWN, D], F32, kind="Internal").ap()
    wl8_s = nc.dram_tensor("wl8_s", [HID, D], F8, kind="Internal").ap()
    ffnT_s = nc.dram_tensor("ffnT_s", [D, OWN], BF16, kind="Internal").ap()

    dbg = {}
    if _DBG:
        for nm, sh, dt_ in [
            ("d_xnbf", [D, NT], BF16), ("d_xn8", [D, OWN], F8),
            ("d_st", [D, OWN], BF16), ("d_z1", [D, OWN], BF16),
            ("d_z", [D, OWN], BF16), ("d_par", [OWN, D], F32),
            ("d_act8", [HID, OWN], F8), ("d_lam", [128, H], F32),
        ]:
            dbg[nm] = nc.dram_tensor(nm, sh, dt_, kind="ExternalOutput").ap()

    class _EngWrap:

        def __init__(self, eng):
            self._e = eng
            self._w = {}
            self._r = {}

        @staticmethod
        def _iv(ap):
            lo = ap.offset
            hi = lo + 1
            for step, count in list(ap.ap)[1:]:
                hi += step * (count - 1)
            return (ap.tensor.name, lo, hi)

        @staticmethod
        def _hit(iv, table):
            spans = table.get(iv[0])
            if not spans:
                return False
            for lo, hi in spans:
                if iv[1] < hi and lo < iv[2]:
                    return True
            return False

        @staticmethod
        def _add(iv, table):
            table.setdefault(iv[0], []).append((iv[1], iv[2]))

        def drain(self, *a, **k):
            self._w.clear()
            self._r.clear()
            return self._e.drain(*a, **k)

        def __getattr__(self, name):
            attr = getattr(self._e, name)
            if name in ("wait_ge", "sem_inc", "dma_start"):
                return attr
            if not callable(attr):
                return attr

            def call(*args, **kwargs):
                aps = [a for a in args if isinstance(a, bass.AP)]
                outs = aps[:1]
                ins = aps[1:]
                for kwn, kv in kwargs.items():
                    if isinstance(kv, bass.AP):
                        (outs if kwn == "accum_out" else ins).append(kv)
                ivs_o = [self._iv(a) for a in outs]
                ivs_i = [self._iv(a) for a in ins]
                if (any(self._hit(iv, self._w) for iv in ivs_i)
                        or any(self._hit(iv, self._w) or self._hit(iv, self._r)
                               for iv in ivs_o)):
                    self._e.drain()
                    self._w.clear()
                    self._r.clear()
                for iv in ivs_o:
                    self._add(iv, self._w)
                for iv in ivs_i:
                    self._add(iv, self._r)
                return attr(*args, **kwargs)

            return call

    TE, GE, SY = nc.tensor, nc.gpsimd, nc.sync
    VE = _EngWrap(nc.vector)
    SE = _EngWrap(nc.scalar)
    GP = _EngWrap(nc.gpsimd)

    es = ExitStack()

    def sbuf(nm, sh, dt_):
        return es.enter_context(nc.sbuf_tensor(nm, sh, dt_))

    def psum(nm, sh, dt_):
        return es.enter_context(nc.psum_tensor(nm, sh, dt_))

    def sem(nm):
        return Ctr(es.enter_context(nc.semaphore(nm)))

    def dstream(nm, n=2):
        return [sem(f"{nm}{i}") for i in range(n)]

    sid32 = sbuf("sid32", [128, 128], F32)
    sid16 = sbuf("sid16", [128, 128], BF16)
    sonc = sbuf("sonc", [128, 1], F32)
    sonr = sbuf("sonr", [1, 128], F32)
    sbeta = sbuf("sbeta", [128, H], F32)
    slam = sbuf("slam", [128, H], F32)
    slng = sbuf("slng", [128, NCH], F32)
    slnb = sbuf("slnb", [128, NCH], F32)
    scarry = sbuf("scarry", [128, H], F32)


    bar = sem("bar")
    ALL_ENG = (TE, VE, SE, GE, SY)

    def barrier(queues=()):
        for eng, q in queues:
            eng.wait_ge(q.h, q.n)
        for eng in (TE, VE, SE):
            eng.drain()
        for eng in ALL_ENG:
            eng.sem_inc(bar.h, 1)
        bar.inc(5)
        for eng in ALL_ENG:
            eng.wait_ge(bar.h, bar.n)

    def emit_dbg_dumps():
        if not _DBG:
            return
        barrier()
        qd = sem("qs_dbg")
        for nm, src_ in [("d_xnbf", xnbf_s), ("d_xn8", xn8_s),
                         ("d_st", st_s), ("d_z1", z1_s), ("d_z", z_s),
                         ("d_par", par_s), ("d_act8", act8_s)]:
            SY.dma_start(dbg[nm], src_).then_inc(qd.h, 16)
            qd.inc(16)
        SY.wait_ge(qd.h, qd.n)

    def lam_bcast(hd, n):
        col = slam[:, hd:hd + 1]
        return bass.AP(col.tensor, col.offset, [[col.ap[0][0], 128], [0, n]])

    qgp = sem("qg_prep")
    for dst, src in [(sid32, cid32), (sid16, cid16), (sonc, conc),
                     (sonr, conr), (sbeta, beta), (slng, lng), (slnb, lnb)]:
        GE.dma_start(dst[:, :], src).then_inc(qgp.h, 16)
        qgp.inc(16)
    SE.wait_ge(qgp.h, qgp.n)
    SE.activation(slam[:, :], sbeta[:, :], AF.Sigmoid)
    if _DBG:
        sep = sem("se_prep")
        SE.drain()
        nc.scalar.sem_inc(sep.h, 1)
        sep.inc()
        qsp = sem("qs_prep")
        SY.wait_ge(sep.h, sep.n)
        SY.dma_start(dbg["d_lam"], slam[:, :]).then_inc(qsp.h, 16)
        qsp.inc(16)
        SY.wait_ge(qsp.h, qsp.n)
    barrier(queues=[(GE, qgp)])

    if _STOP == "P":
        es.close()
        return nc

    ws = ExitStack()
    w_sb = ws.enter_context(nc.sbuf_tensor("w_sb", [128, NCH * D], BF16))
    pMM = [ws.enter_context(nc.psum_tensor(f"pmm{i}", [128, T], F32))
           for i in range(4)]
    qg_w = [sem(f"qg_w{i}") for i in range(4)]

    def load_weight(wap, qgc):
        for k in range(NCH):
            GE.dma_start(w_sb[:, k * D:(k + 1) * D],
                         wap[k * 128:(k + 1) * 128, :]).then_inc(qgc.h, 16)
            qgc.inc(16)

    load_weight(wv, qg_w[0])

    with ExitStack() as ph:
        asb = lambda nm, sh, dt_: ph.enter_context(nc.sbuf_tensor(nm, sh, dt_))
        xbuf = [asb(f"a_x{i}", [128, D], F32) for i in range(4)]
        xntm = [asb(f"a_xn{i}", [128, D], F32) for i in range(8)]
        sqdump = asb("a_sqd", [128, D], F32)
        stt_t = asb("a_stt", [128, 48], F32)
        stgbf = [asb(f"a_gbf{i}", [128, T], BF16) for i in range(2)]
        stg8 = [asb(f"a_g8{i}", [128, T], F8) for i in range(2)]
        pTR = [ph.enter_context(nc.psum_tensor(f"ptra{i}", [128, T], F32))
               for i in range(2)] + pMM[0:2]

        qx = dstream("qx_a", 4)
        qbf = dstream("qbf_a")
        q8 = dstream("q8_a")
        ve = sem("ve_a")
        se = sem("se_a")
        te = sem("te_a")

        NTILE = NT // 128
        ld_done = [0] * NTILE
        xn_done = [0] * NTILE
        grp_tr_done = [0] * NC_ALL
        evac_done = {}

        def wait_xn(eng, i):
            v = xn_done[i]
            if isinstance(v, tuple):
                eng.wait_ge(se.h, v[1])
            else:
                eng.wait_ge(ve.h, v)

        def emit_x_load(i):
            if i >= 4:
                wait_xn(SY, i - 4)
            SY.dma_start(xbuf[i % 4][:, :], xp[i * 128:(i + 1) * 128, :]
                         ).then_inc(qx[i % 4].h, 16)
            ld_done[i] = qx[i % 4].inc(16)

        for i in range(4):
            emit_x_load(i)

        for g in range(NC_ALL):
            own_g = g >= NC_ALL - NC_OWN
            stt = stt_t[:, (g % 2) * 24:(g % 2) * 24 + 24]
            if g >= 2:
                VE.wait_ge(te.h, grp_tr_done[g - 2])
                SE.wait_ge(te.h, grp_tr_done[g - 2])
            for s in range(4):
                i = g * 4 + s
                xb = xbuf[i % 4]
                VE.wait_ge(qx[i % 4].h, ld_done[i])
                SE.wait_ge(qx[i % 4].h, ld_done[i])
                if s < 2:
                    VE.reduce_sum(stt[:, s:s + 1], xb[:, :], axis=AXX)
                else:
                    SE.activation(sqdump[:, :], xb[:, :], AF.Identity,
                                  accum_out=stt[:, s:s + 1])
                SE.activation(sqdump[:, :], xb[:, :], AF.Square,
                              accum_out=stt[:, 4 + s:5 + s])
            SE.drain()
            nc.scalar.sem_inc(se.h, 1)
            sq_v = se.inc()
            VE.wait_ge(se.h, sq_v)
            VE.tensor_scalar_mul(stt[:, 8:12], stt[:, 0:4], -INV_D)
            VE.tensor_scalar_mul(stt[:, 12:16], stt[:, 4:8], INV_D)
            VE.tensor_mul(stt[:, 16:20], stt[:, 8:12], stt[:, 8:12])
            VE.tensor_sub(stt[:, 12:16], stt[:, 12:16], stt[:, 16:20])
            VE.tensor_scalar_add(stt[:, 12:16], stt[:, 12:16], EPS).then_inc(ve.h, 1)
            d_v = ve.inc()
            SE.wait_ge(ve.h, d_v)
            SE.activation(stt[:, 16:20], stt[:, 12:16], AF.Sqrt).then_inc(se.h, 1)
            s0_v = se.inc()
            VE.wait_ge(se.h, s0_v)
            VE.reciprocal(stt[:, 20:24], stt[:, 16:20])
            VE.tensor_mul(stt[:, 16:20], stt[:, 20:24], stt[:, 20:24])
            VE.tensor_mul(stt[:, 16:20], stt[:, 16:20], stt[:, 12:16])
            VE.tensor_scalar(stt[:, 16:20], stt[:, 16:20], -0.5, 1.5,
                             op0=OP.mult, op1=OP.add)
            VE.tensor_mul(stt[:, 20:24], stt[:, 20:24], stt[:, 16:20])
            VE.tensor_mul(stt[:, 12:16], stt[:, 8:12], stt[:, 20:24]).then_inc(ve.h, 1)
            r_v = ve.inc()
            SE.wait_ge(ve.h, r_v)
            for s in range(4):
                i = g * 4 + s
                xb = xbuf[i % 4]
                xt = xntm[(g % 2) * 4 + s]
                if s < 2:
                    VE.tensor_scalar(xt[:, :], xb[:, :], stt[:, 8 + s:9 + s],
                                     stt[:, 20 + s:21 + s],
                                     op0=OP.add, op1=OP.mult).then_inc(ve.h, 1)
                    xn_done[i] = ve.inc()
                else:
                    SE.activation(xt[:, :], xb[:, :], AF.Identity,
                                  bias=stt[:, 12 + s:13 + s],
                                  scale=stt[:, 20 + s:21 + s]).then_inc(se.h, 1)
                    xn_done[i] = ("se", se.inc())
                if i + 4 < NTILE:
                    emit_x_load(i + 4)

            for j in range(NCH):
                bank = pTR[j % 4]
                key = (g, j - 4) if j >= 4 else (g - 1, NCH - 4 + j)
                if key in evac_done:
                    vv, sv = evac_done[key]
                    TE.wait_ge(ve.h, vv)
                    TE.wait_ge(se.h, sv)
                if j == 0:
                    for s4 in range(4):
                        wait_xn(TE, g * 4 + s4)
                for s in range(4):
                    xt = xntm[(g % 2) * 4 + s]
                    _inst = TE.transpose(bank[:, s * 128:(s + 1) * 128],
                                         xt[:, j * 128:(j + 1) * 128],
                                         sid32[:, :])
                _inst.then_inc(te.h, 1)
                tr_v = te.inc()
                if j == NCH - 1:
                    grp_tr_done[g] = tr_v
                stb = stgbf[j % 2]
                st8 = stg8[j % 2]
                VE.wait_ge(te.h, tr_v)
                VE.wait_ge(qbf[j % 2].h, qbf[j % 2].n)
                VE.tensor_copy(stb[:, :], bank[:, :]).then_inc(ve.h, 1)
                bf_v = ve.inc()
                sv_8 = 0
                if own_g:
                    SE.wait_ge(ve.h, bf_v)
                    SE.wait_ge(q8[j % 2].h, q8[j % 2].n)
                    SE.copy(st8[:, :], bank[:, :]).then_inc(se.h, 1)
                    sv_8 = se.inc()
                evac_done[(g, j)] = (bf_v, sv_8)
                SY.wait_ge(ve.h, bf_v)
                SY.dma_start(xnbf_s[j * 128:(j + 1) * 128, g * T:(g + 1) * T],
                             stb[:, :]).then_inc(qbf[j % 2].h, 16)
                qbf[j % 2].inc(16)
                if own_g:
                    go = g - (NC_ALL - NC_OWN)
                    SY.wait_ge(se.h, sv_8)
                    SY.dma_start(
                        xn8_s[j * 128:(j + 1) * 128, go * T:(go + 1) * T],
                        st8[:, :]).then_inc(q8[j % 2].h, 16)
                    q8[j % 2].inc(16)

        barrier(queues=[(SY, qx[0]), (SY, qx[1]), (SY, qx[2]), (SY, qx[3]),
                        (SY, qbf[0]), (SY, qbf[1]),
                        (SY, q8[0]), (SY, q8[1])])

    if _STOP == "A":
        emit_dbg_dumps()
        ws.close()
        es.close()
        return nc

    def _emit_bcde():
      with ExitStack() as ph:
        bsb = lambda nm, sh, dt_: ph.enter_context(nc.sbuf_tensor(nm, sh, dt_))
        xnbuf = [bsb(f"xnb{i}", [128, NCH * T], BF16) for i in range(2)]
        v_sb = bsb("v_sb", [128, NCH * T], F32)
        s2_sb = bsb("s2_sb", [128, NCH * T], F32)

        qpre = dstream("qpre")
        pre_c = {}
        pre_d = {}
        pre_e = {}

        def load_fm_chunk(dst, src, c, qsc, dep=None):
            if dep is not None:
                SY.wait_ge(dep[0].h, dep[1])
            for k in range(NCH):
                SY.dma_start(dst[:, k * T:(k + 1) * T],
                             src[k * 128:(k + 1) * 128, c * T:(c + 1) * T]
                             ).then_inc(qsc.h, 16)
                qsc.inc(16)
            return qsc.n

        def emit_chunk_groups(c, xnb, te_c, free_list, free_sem, emit_evac):
            vals = []
            for m in range(NCH):
                gi = c * NCH + m
                if gi >= 4:
                    TE.wait_ge(free_sem.h, free_list[gi - 4])
                bank = pMM[gi % 4]
                for k in range(NCH):
                    _inst = TE.matmul(
                        bank[:, :],
                        w_sb[:, k * D + m * 128: k * D + (m + 1) * 128],
                        xnb[:, k * T:(k + 1) * T],
                        start=(k == 0), stop=(k == NCH - 1))
                _inst.then_inc(te_c.h, 1)
                v = te_c.inc()
                vals.append(v)
                emit_evac(c, m, bank, v)
            return vals

        qg = qg_w[0]
        qxn = dstream("qxn_b")
        qst = dstream("qst_b")
        te = sem("te_b")
        ve = sem("ve_b")
        se = sem("se_b")
        b_se_last = [0]
        with ExitStack() as pb:
            stg = [pb.enter_context(nc.sbuf_tensor(f"b_stg{i}", [128, T], BF16))
                   for i in range(2)]
            v_sb2 = pb.enter_context(
                nc.sbuf_tensor("v_sb2", [128, NCH * T], F32))
            vbufs = [v_sb, v_sb2]

            TE.wait_ge(qg.h, qg.n)

            free_vals = []
            grp_last = [0] * NC_ALL

            def b_evac(c, m, bank, v):
                VE.wait_ge(te.h, v)
                VE.tensor_copy(vbufs[c % 2][:, m * T:(m + 1) * T],
                               bank[:, :]).then_inc(ve.h, 1)
                free_vals.append(ve.inc())

            def b_scan(c):
                vb = vbufs[c % 2]
                own_c = c >= NC_ALL - NC_OWN
                scan_v = {}
                if b_se_last[0]:
                    VE.wait_ge(se.h, b_se_last[0])
                for hd in range(H):
                    init = scarry[:, hd:hd + 1] if c > 0 else 0.0
                    VE.tensor_tensor_scan(
                        s2_sb[:, hd * T:(hd + 1) * T], lam_bcast(hd, T),
                        vb[:, hd * T:(hd + 1) * T], init,
                        op0=OP.mult, op1=OP.add).then_inc(ve.h, 1)
                    scan_v[hd] = ve.inc()
                lastcols = bass.AP(s2_sb.ap().tensor,
                                   s2_sb.ap().offset + T - 1,
                                   [[s2_sb.ap().ap[0][0], 128], [T, H]])
                VE.tensor_copy(scarry[:, :], lastcols)
                if own_c:
                    for hd in range(H):
                        SE.wait_ge(ve.h, scan_v[hd])
                        SE.wait_ge(qst[hd % 2].h, qst[hd % 2].n)
                        SE.copy(stg[hd % 2][:, :],
                                s2_sb[:, hd * T:(hd + 1) * T]).then_inc(se.h, 1)
                        sv = se.inc()
                        SY.wait_ge(se.h, sv)
                        co = c - (NC_ALL - NC_OWN)
                        SY.dma_start(
                            st_s[hd * 128:(hd + 1) * 128,
                                 co * T:(co + 1) * T],
                            stg[hd % 2][:, :]).then_inc(qst[hd % 2].h, 16)
                        qst[hd % 2].inc(16)
                    b_se_last[0] = se.n

            ld_vals = [(qxn[0], load_fm_chunk(xnbuf[0], xnbf_s, 0, qxn[0]))]
            for c in range(NC_ALL):
                if c + 1 < NC_ALL:
                    dep = (te, grp_last[c - 1]) if c >= 1 else None
                    ld_vals.append((qxn[(c + 1) % 2],
                                    load_fm_chunk(xnbuf[(c + 1) % 2], xnbf_s,
                                                  c + 1, qxn[(c + 1) % 2],
                                                  dep=dep)))
                TE.wait_ge(ld_vals[c][0].h, ld_vals[c][1])
                vals = emit_chunk_groups(c, xnbuf[c % 2], te, free_vals, ve,
                                         b_evac)
                grp_last[c] = vals[-1]
                if c >= 1:
                    b_scan(c - 1)
            b_scan(NC_ALL - 1)
            GE.wait_ge(te.h, grp_last[NC_ALL - 1])
            load_weight(wq, qg_w[1])
            pre_c[0] = (qpre[0], load_fm_chunk(
                xnbuf[0], xnbf_s, NC_ALL - NC_OWN, qpre[0],
                dep=(te, grp_last[NC_ALL - 2])))
            pre_c[1] = (qpre[1], load_fm_chunk(
                xnbuf[1], xnbf_s, NC_ALL - NC_OWN + 1, qpre[1],
                dep=(te, grp_last[NC_ALL - 1])))
            barrier(queues=[(SY, qxn[0]), (SY, qxn[1]), (SY, qst[0]),
                            (SY, qst[1]), (GE, qg)])

        if _STOP == "B":
            return
        qg = qg_w[1]
        qxn = dstream("qxn_c")
        qstl = dstream("qstl_c")
        qz1 = dstream("qz1_c")
        te = sem("te_c")
        ve = sem("ve_c")
        se = sem("se_c")
        ge = sem("ge_c")
        with ExitStack() as pc:
            csb = lambda nm, sh, dt_: pc.enter_context(nc.sbuf_tensor(nm, sh, dt_))
            pst1 = pc.enter_context(nc.psum_tensor("pst1", [1, T], F32))
            pst2 = pc.enter_context(nc.psum_tensor("pst2", [1, T], F32))
            pTR = [pc.enter_context(nc.psum_tensor(f"ptrc{i}", [128, T], F32))
                   for i in range(2)]
            stbuf = [csb(f"stb{i}", [128, NCH * T], BF16) for i in range(2)]
            gztmp = csb("gztmp", [128, 1024], F32)
            musigrep = csb("musigrep", [128, T], F32)
            rsigrep = csb("rsigrep", [128, T], F32)
            ztmp = csb("ztmp", [128, T], F32)
            y2t = [csb("y2t0", [128, T], F32), ztmp]
            z1stg = [csb(f"z1s{i}", [128, T], BF16) for i in range(2)]

            ybufs = [v_sb, s2_sb]

            TE.wait_ge(qg.h, qg.n)

            free_vals = []
            grp_last = [0] * NC_OWN
            y_last = [0] * NC_OWN
            st_vals = [0] * NC_OWN
            st2_v = {}
            row_done = [0] * NC_OWN
            z1_done = [0] * NC_OWN

            def c_evac(c, m, bank, v):
                VE.wait_ge(te.h, v)
                if m == 0:
                    VE.wait_ge(qstl[c % 2].h, st_vals[c])
                    if c >= 2:
                        VE.wait_ge(ge.h, z1_done[c - 2])
                VE.tensor_tensor(ybufs[c % 2][:, m * T:(m + 1) * T], bank[:, :],
                                 stbuf[c % 2][:, m * T:(m + 1) * T],
                                 op=OP.mult).then_inc(ve.h, 1)
                free_vals.append(ve.inc())
                if m == NCH - 1:
                    y_last[c] = free_vals[-1]

            def c_stats_z1(c):
                y_sb = ybufs[c % 2]
                TE.wait_ge(ve.h, y_last[c])
                if c >= 1:
                    TE.wait_ge(ve.h, row_done[c - 1])
                for m in range(NCH):
                    _inst = TE.matmul(pst1[:, :], sonc[:, :],
                                      y_sb[:, m * T:(m + 1) * T],
                                      start=(m == 0), stop=(m == NCH - 1))
                _inst.then_inc(te.h, 1)
                te.inc()
                for m in range(NCH):
                    if m >= 2:
                        VE.wait_ge(te.h, st2_v[(c, m - 2)])
                    VE.tensor_mul(y2t[m % 2][:, :],
                                  y_sb[:, m * T:(m + 1) * T],
                                  y_sb[:, m * T:(m + 1) * T]).then_inc(ve.h, 1)
                    y2v = ve.inc()
                    TE.wait_ge(ve.h, y2v)
                    _inst = TE.matmul(pst2[:, :], sonc[:, :], y2t[m % 2][:, :],
                                      start=(m == 0), stop=(m == NCH - 1))
                    _inst.then_inc(te.h, 1)
                    st2_v[(c, m)] = te.inc()
                st_mm = st2_v[(c, NCH - 1)]
                VE.wait_ge(te.h, st_mm)
                if c >= 1:
                    VE.wait_ge(ge.h, z1_done[c - 1])
                rowb = pst2[0:1, :]
                rowN = gztmp[0:1, 0:T]
                VE.tensor_scalar_mul(rowN, pst1[0:1, :], -INV_D)
                VE.tensor_scalar_mul(rowb, rowb, INV_D)
                VE.tensor_mul(rsigrep[0:1, 0:T], rowN, rowN)
                VE.tensor_sub(rowb, rowb, rsigrep[0:1, 0:T])
                VE.tensor_scalar_add(rowb, rowb, EPS).then_inc(ve.h, 1)
                d_v = ve.inc()
                SE.wait_ge(ve.h, d_v)
                SE.activation(rsigrep[0:1, 0:T], rowb, AF.Sqrt).then_inc(se.h, 1)
                s0_v = se.inc()
                VE.wait_ge(se.h, s0_v)
                VE.reciprocal(musigrep[0:1, 0:T], rsigrep[0:1, 0:T])
                VE.tensor_mul(rsigrep[0:1, 0:T], musigrep[0:1, 0:T],
                              musigrep[0:1, 0:T])
                VE.tensor_mul(rsigrep[0:1, 0:T], rsigrep[0:1, 0:T],
                              rowb)
                VE.tensor_scalar(rsigrep[0:1, 0:T], rsigrep[0:1, 0:T],
                                 -0.5, 1.5, op0=OP.mult, op1=OP.add)
                VE.tensor_mul(rsigrep[0:1, 0:T], musigrep[0:1, 0:T],
                              rsigrep[0:1, 0:T])
                VE.tensor_mul(musigrep[0:1, 0:T], rowN,
                              rsigrep[0:1, 0:T]).then_inc(ve.h, 1)
                rows_v = ve.inc()
                row_done[c] = rows_v
                TE.wait_ge(ve.h, rows_v)
                TE.matmul(pTR[0][:, :], sonr[:, :], musigrep[0:1, 0:T],
                          start=True, stop=True).then_inc(te.h, 1)
                te.inc()
                TE.matmul(pTR[1][:, :], sonr[:, :], rsigrep[0:1, 0:T],
                          start=True, stop=True).then_inc(te.h, 1)
                bc_mm = te.inc()
                VE.wait_ge(te.h, bc_mm)
                VE.tensor_copy(musigrep[:, :], pTR[0][:, :])
                VE.tensor_copy(rsigrep[:, :], pTR[1][:, :]).then_inc(ve.h, 1)
                rep_v = ve.inc()
                GP.wait_ge(ve.h, rep_v)
                rs_ps = rsigrep.ap().ap[0][0]
                rs2 = bass.AP(rsigrep.ap().tensor, rsigrep.ap().offset,
                              [[rs_ps, 128], [0, 2], [1, T]])
                mu2b = bass.AP(musigrep.ap().tensor, musigrep.ap().offset,
                               [[musigrep.ap().ap[0][0], 128], [0, 2], [1, T]])
                ts_done = 0
                for mp in range(NCH // 2):
                    if ts_done:
                        GP.wait_ge(ve.h, ts_done)
                    GP.tensor_mul(gztmp[:, :],
                                  y_sb[:, 2 * mp * T:(2 * mp + 2) * T], rs2)
                    GP.tensor_add(gztmp[:, :], gztmp[:, :], mu2b).then_inc(ge.h, 1)
                    gv = ge.inc()
                    VE.wait_ge(ge.h, gv)
                    for h2 in range(2):
                        m = 2 * mp + h2
                        VE.wait_ge(qz1[m % 2].h, qz1[m % 2].n)
                        VE.tensor_scalar(z1stg[m % 2][:, :],
                                         gztmp[:, h2 * T:(h2 + 1) * T],
                                         slng[:, m:m + 1], slnb[:, m:m + 1],
                                         op0=OP.mult, op1=OP.add).then_inc(ve.h, 1)
                        zv = ve.inc()
                        SY.wait_ge(ve.h, zv)
                        SY.dma_start(
                            z1_s[m * 128:(m + 1) * 128, c * T:(c + 1) * T],
                            z1stg[m % 2][:, :]).then_inc(qz1[m % 2].h, 16)
                        qz1[m % 2].inc(16)
                    ts_done = ve.n
                z1_done[c] = ge.n

            ld_vals = [pre_c[0], pre_c[1]]
            st_vals[0] = load_fm_chunk(stbuf[0], st_s, 0, qstl[0])
            for c in range(NC_OWN):
                if c + 1 < NC_OWN:
                    dep2 = (ve, y_last[c - 1]) if c >= 1 else None
                    st_vals[c + 1] = load_fm_chunk(stbuf[(c + 1) % 2], st_s,
                                                   c + 1, qstl[(c + 1) % 2],
                                                   dep=dep2)
                TE.wait_ge(ld_vals[c][0].h, ld_vals[c][1])
                vals = emit_chunk_groups(c, xnbuf[c % 2], te, free_vals, ve,
                                         c_evac)
                grp_last[c] = vals[-1]
                if c + 2 < NC_OWN:
                    ld_vals.append((qxn[c % 2],
                                    load_fm_chunk(xnbuf[c % 2], xnbf_s,
                                                  NC_ALL - NC_OWN + c + 2,
                                                  qxn[c % 2],
                                                  dep=(te, grp_last[c]))))
                if c >= 1:
                    c_stats_z1(c - 1)
            c_stats_z1(NC_OWN - 1)
            GE.wait_ge(te.h, grp_last[NC_OWN - 1])
            load_weight(wg, qg_w[2])
            pre_d[0] = (qpre[0], load_fm_chunk(
                xnbuf[0], xnbf_s, NC_ALL - NC_OWN, qpre[0],
                dep=(te, grp_last[NC_OWN - 2])))
            pre_d[1] = (qpre[1], load_fm_chunk(
                xnbuf[1], xnbf_s, NC_ALL - NC_OWN + 1, qpre[1],
                dep=(te, grp_last[NC_OWN - 1])))
            barrier(queues=[(SY, qxn[0]), (SY, qxn[1]), (SY, qstl[0]),
                            (SY, qstl[1]), (SY, qz1[0]), (SY, qz1[1]),
                            (GE, qg)])

        if _STOP == "C":
            return
        qg = qg_w[2]
        qxn = dstream("qxn_d")
        qz1l = dstream("qz1l_d")
        qzs = dstream("qzs_d")
        te = sem("te_d")
        se = sem("se_d")
        ve = sem("ve_d")
        with ExitStack() as pd:
            dsb = lambda nm, sh, dt_: pd.enter_context(nc.sbuf_tensor(nm, sh, dt_))
            z1buf = [dsb(f"z1b{i}", [128, NCH * T], BF16) for i in range(2)]
            gstg = [dsb(f"ph_g{i}", [128, T], BF16) for i in range(2)]
            zstg = [dsb(f"ph_z{i}", [128, T], BF16) for i in range(2)]

            TE.wait_ge(qg.h, qg.n)
            free_vals = []
            grp_last = [0] * NC_OWN
            z_last = [0] * NC_OWN
            z1_vals = [0] * NC_OWN
            z_of = {}
            zs_after = {}

            def d_evac(c, m, bank, v):
                SE.wait_ge(te.h, v)
                if (c * NCH + m) >= 2:
                    SE.wait_ge(ve.h, z_of.get(c * NCH + m - 2, 0))
                SE.activation(gstg[m % 2][:, :], bank[:, :], AF.Silu).then_inc(se.h, 1)
                free_vals.append(se.inc())
                VE.wait_ge(se.h, free_vals[-1])
                if m == 0:
                    VE.wait_ge(qz1l[c % 2].h, z1_vals[c])
                VE.wait_ge(qzs[m % 2].h, qzs[m % 2].n)
                VE.tensor_mul(zstg[m % 2][:, :],
                              z1buf[c % 2][:, m * T:(m + 1) * T],
                              gstg[m % 2][:, :]).then_inc(ve.h, 1)
                zv = ve.inc()
                z_of[c * NCH + m] = zv
                if m == NCH - 1:
                    z_last[c] = zv
                SY.wait_ge(ve.h, zv)
                SY.dma_start(z_s[m * 128:(m + 1) * 128, c * T:(c + 1) * T],
                             zstg[m % 2][:, :]).then_inc(qzs[m % 2].h, 16)
                qzs[m % 2].inc(16)

            ld_vals = [pre_d[0], pre_d[1]]
            z1_vals[0] = load_fm_chunk(z1buf[0], z1_s, 0, qz1l[0])
            for c in range(NC_OWN):
                if c + 1 < NC_OWN:
                    dep2 = (ve, z_last[c - 1]) if c >= 1 else None
                    z1_vals[c + 1] = load_fm_chunk(z1buf[(c + 1) % 2], z1_s,
                                                   c + 1, qz1l[(c + 1) % 2],
                                                   dep=dep2)
                TE.wait_ge(ld_vals[c][0].h, ld_vals[c][1])
                vals = emit_chunk_groups(c, xnbuf[c % 2], te, free_vals, se,
                                         d_evac)
                grp_last[c] = vals[-1]
                zs_after[c] = (qzs[0].n, qzs[1].n)
                if c + 2 < NC_OWN:
                    ld_vals.append((qxn[c % 2],
                                    load_fm_chunk(xnbuf[c % 2], xnbf_s,
                                                  NC_ALL - NC_OWN + c + 2,
                                                  qxn[c % 2],
                                                  dep=(te, grp_last[c]))))
            GE.wait_ge(te.h, grp_last[NC_OWN - 1])
            load_weight(wo, qg_w[3])
            SY.wait_ge(qzs[0].h, zs_after[0][0])
            SY.wait_ge(qzs[1].h, zs_after[0][1])
            pre_e[0] = (qpre[0], load_fm_chunk(
                xnbuf[0], z_s, 0, qpre[0],
                dep=(te, grp_last[NC_OWN - 2])))
            SY.wait_ge(qzs[0].h, zs_after[1][0])
            SY.wait_ge(qzs[1].h, zs_after[1][1])
            pre_e[1] = (qpre[1], load_fm_chunk(
                xnbuf[1], z_s, 1, qpre[1],
                dep=(te, grp_last[NC_OWN - 1])))
            barrier(queues=[(SY, qxn[0]), (SY, qxn[1]), (SY, qz1l[0]),
                            (SY, qz1l[1]), (SY, qzs[0]), (SY, qzs[1]),
                            (GE, qg)])

        if _STOP == "D":
            return
        qg = qg_w[3]
        qzl = dstream("qzl_e")
        qxe = dstream("qxe_e")
        qpar = dstream("qpar_e")
        te = sem("te_e")
        ve = sem("ve_e")
        with ExitStack() as pe:
            esb = lambda nm, sh, dt_: pe.enter_context(nc.sbuf_tensor(nm, sh, dt_))
            pTR = [pe.enter_context(nc.psum_tensor(f"ptre{i}", [128, T], F32))
                   for i in range(4)]
            xbe = [esb(f"e_x{i}", [128, D], F32) for i in range(2)]
            oste = [esb(f"e_o{i}", [128, D], F32) for i in range(2)]
            attn_sb = v_sb

            TE.wait_ge(qg.h, qg.n)
            free_vals = []
            grp_last = [0] * NC_OWN
            evv = {}
            x_vals = {}
            out_done = {}
            tr_free = {}

            def e_evac(c, m, bank, v):
                VE.wait_ge(te.h, v)
                VE.tensor_copy(attn_sb[:, m * T:(m + 1) * T], bank[:, :]).then_inc(ve.h, 1)
                free_vals.append(ve.inc())
                evv[(c, m)] = free_vals[-1]

            def emit_xe_load(i):
                if i >= 2:
                    SY.wait_ge(ve.h, out_done.get(i - 2, 0))
                SY.dma_start(xbe[i % 2][:, :],
                             xp[PRE + i * 128: PRE + (i + 1) * 128, :]
                             ).then_inc(qxe[i % 2].h, 16)
                x_vals[i] = qxe[i % 2].inc(16)

            ld_vals = [pre_e[0], pre_e[1]]
            emit_xe_load(0)
            emit_xe_load(1)
            for c in range(NC_OWN):
                TE.wait_ge(ld_vals[c][0].h, ld_vals[c][1])
                vals = emit_chunk_groups(c, xnbuf[c % 2], te, free_vals, ve,
                                         e_evac)
                grp_last[c] = vals[-1]
                if c + 2 < NC_OWN:
                    ld_vals.append((qzl[c % 2],
                                    load_fm_chunk(xnbuf[c % 2], z_s, c + 2,
                                                  qzl[c % 2],
                                                  dep=(te, grp_last[c]))))
                for s in range(4):
                    i = c * 4 + s
                    ob = oste[i % 2]
                    if i >= 2:
                        VE.wait_ge(qpar[i % 2].h, qpar[i % 2].n)
                    for mg in range(4):
                        bank = pTR[mg % 4]
                        fkey = (i - 1, mg)
                        if fkey in tr_free:
                            TE.wait_ge(ve.h, tr_free[fkey])
                        for mi in range(4):
                            m = mg * 4 + mi
                            TE.wait_ge(ve.h, evv[(c, m)])
                            _inst = TE.transpose(
                                bank[:, mi * 128:(mi + 1) * 128],
                                attn_sb[:, m * T + s * 128:
                                        m * T + (s + 1) * 128],
                                sid32[:, :])
                        _inst.then_inc(te.h, 1)
                        trv = te.inc()
                        VE.wait_ge(te.h, trv)
                        VE.wait_ge(qxe[i % 2].h, x_vals[i])
                        VE.tensor_add(ob[:, mg * 512:(mg + 1) * 512],
                                      bank[:, :],
                                      xbe[i % 2][:, mg * 512:(mg + 1) * 512]).then_inc(ve.h, 1)
                        tr_free[(i, mg)] = ve.inc()
                    out_done[i] = tr_free[(i, 3)]
                    SY.wait_ge(ve.h, out_done[i])
                    SY.dma_start(par_s[i * 128:(i + 1) * 128, :], ob[:, :]
                                 ).then_inc(qpar[i % 2].h, 16)
                    qpar[i % 2].inc(16)
                    if i + 2 < OWN // 128:
                        emit_xe_load(i + 2)
            barrier(queues=[(SY, qzl[0]), (SY, qzl[1]), (SY, qxe[0]),
                            (SY, qxe[1]), (SY, qpar[0]), (SY, qpar[1]),
                            (GE, qg)])
    _emit_bcde()
    ws.close()

    if _STOP in ("B", "C", "D", "E"):
        emit_dbg_dumps()
        es.close()
        return nc

    DR = mybir.MatmulPerfMode.DoubleRow
    NJC = NCH // 2
    with ExitStack() as ph:
        fsb = lambda nm, sh, dt_: ph.enter_context(nc.sbuf_tensor(nm, sh, dt_))
        pFF = [ph.enter_context(nc.psum_tensor(f"pff{i}", [128, 1024], F32))
               for i in range(4)]
        xn8 = fsb("xn8", [128, NCH * OWN], F8)
        wg8a = [fsb(f"wg8a{i}", [128, NCH * 128], F8) for i in range(2)]
        wg8b = [fsb(f"wg8b{i}", [128, NCH * 128], F8) for i in range(2)]
        W2 = 1024
        ffg = [fsb(f"ffg{i}", [128, W2], BF16) for i in range(2)]
        fgel = [fsb(f"fgel{i}", [128, W2], BF16) for i in range(2)]
        gt1 = [fsb(f"gt1_{i}", [128, W2], BF16) for i in range(2)]
        gt2 = [fsb(f"gt2_{i}", [128, W2], BF16) for i in range(2)]
        ff1 = [fsb(f"ff1{i}", [128, W2], BF16) for i in range(2)]
        factb = [fsb(f"factb{i}", [128, W2], BF16) for i in range(2)]
        fact8 = [fsb(f"fact8{i}", [128, W2], F8) for i in range(2)]
        wlb = [fsb(f"wlb{i}", [128, D], F8) for i in range(2)]

        qx8 = sem("qx8_f1")
        qwg = dstream("qwg_f1")
        qa8 = dstream("qa8_f1")
        qwl = dstream("qwl_f1")
        qws = dstream("qws_f1")
        te = sem("te_f1")
        ve = sem("ve_f1")
        se = sem("se_f1")

        xn8_ps = xn8.ap().ap[0][0]
        for jc in range(NJC):
            src_ap = bass.AP(xn8_s.tensor, 256 * jc * OWN,
                             [[2 * OWN, 128], [OWN, 2], [1, OWN]])
            SY.dma_start(xn8[:, jc * 2 * OWN:(jc + 1) * 2 * OWN], src_ap
                         ).then_inc(qx8.h, 16)
            qx8.inc(16)

        def xn8_rhs(jc, tc):
            return bass.AP(xn8.ap().tensor,
                           xn8.ap().offset + jc * 2 * OWN + tc * T,
                           [[xn8_ps, 128], [OWN, 2], [1, T]])

        hc_grp_last = [0] * NHC

        def load_wg8(hc, dep_te):
            bi = hc % 2
            if dep_te:
                GE.wait_ge(te.h, dep_te)
            for j, wbuf in ((0, wg8a), (1, wg8b)):
                for i in range(2):
                    src_ap = bass.AP(wgt.tensor,
                                     j * D * HID + i * HID + hc * 128,
                                     [[2 * HID, 128], [256 * HID, NJC],
                                      [1, 128]])
                    dst_ap = bass.AP(wbuf[bi].ap().tensor,
                                     wbuf[bi].ap().offset + i * 128,
                                     [[wbuf[bi].ap().ap[0][0], 128],
                                      [256, NJC], [1, 128]])
                    GE.dma_start(dst_ap, src_ap).then_inc(qwg[bi].h, 16)
                    qwg[bi].inc(16)
            return qwg[bi].n

        def wg_lhsT(wbuf, jc):
            return bass.AP(wbuf.ap().tensor,
                           wbuf.ap().offset + jc * 256,
                           [[wbuf.ap().ap[0][0], 128], [128, 2], [1, 128]])

        wg_vals = [load_wg8(0, 0)]
        chain_used = [0, 0]
        pff_free = [(None, 0)] * 4
        TE.wait_ge(qx8.h, qx8.n)
        for hc in range(NHC):
            if hc + 1 < NHC:
                wg_vals.append(load_wg8(hc + 1,
                                        hc_grp_last[hc - 1] if hc >= 1 else 0))
            p = hc % 2
            GE.wait_ge(qws[p].h, qws[p].n)
            GE.dma_start(wlb[p][:, :], wlin[hc * 128:(hc + 1) * 128, :]
                         ).then_inc(qwl[p].h, 16)
            qwl[p].inc(16)
            GE.wait_ge(qwl[p].h, qwl[p].n)
            GE.dma_start(wl8_s[hc * 128:(hc + 1) * 128, :], wlb[p][:, :]
                         ).then_inc(qws[p].h, 16)
            qws[p].inc(16)

            TE.wait_ge(qwg[hc % 2].h, wg_vals[hc])
            for tcp in range(NC_OWN // 2):
                pp = tcp % 2
                pg = pFF[pp]
                pf = pFF[2 + pp]
                for bank, bank_i in ((pg, pp), (pf, 2 + pp)):
                    ctr, v = pff_free[bank_i]
                    if ctr is not None:
                        TE.wait_ge(ctr.h, v)
                for half in range(2):
                    tc = 2 * tcp + half
                    for bank, wbuf in ((pg, wg8a), (pf, wg8b)):
                        for jc in range(NJC):
                            _inst = TE.matmul(
                                bank[:, half * T:(half + 1) * T],
                                wg_lhsT(wbuf[hc % 2], jc),
                                xn8_rhs(jc, tc),
                                start=(jc == 0), stop=(jc == NJC - 1),
                                perf_mode=DR)
                _inst.then_inc(te.h, 1)
                grp_v = te.inc()
                hc_grp_last[hc] = grp_v
                SE.wait_ge(te.h, grp_v)
                SE.wait_ge(ve.h, chain_used[pp])
                SE.copy(ffg[pp][:, :], pg[:, :]).then_inc(se.h, 1)
                ffg_v = se.inc()
                pff_free[pp] = (se, ffg_v)
                SE.copy(ff1[pp][:, :], pf[:, :]).then_inc(se.h, 1)
                ff1_v = se.inc()
                pff_free[2 + pp] = (se, ff1_v)
                VE.wait_ge(se.h, ffg_v)
                VE.wait_ge(ve.h, chain_used[pp])
                VE.tensor_mul(gt1[pp][:, :], ffg[pp][:, :], ffg[pp][:, :])
                VE.tensor_mul(gt1[pp][:, :], gt1[pp][:, :], ffg[pp][:, :])
                VE.tensor_scalar_mul(gt1[pp][:, :], gt1[pp][:, :], C2BF)
                VE.tensor_add(gt1[pp][:, :], ffg[pp][:, :], gt1[pp][:, :])
                VE.tensor_scalar_mul(gt1[pp][:, :], gt1[pp][:, :], C1BF).then_inc(ve.h, 1)
                g5_v = ve.inc()
                SE.wait_ge(ve.h, g5_v)
                SE.activation(gt2[pp][:, :], gt1[pp][:, :], AF.Tanh).then_inc(se.h, 1)
                tanh_v = se.inc()
                VE.wait_ge(se.h, tanh_v)
                VE.tensor_scalar(gt2[pp][:, :], gt2[pp][:, :], 0.5, 0.5,
                                 op0=OP.mult, op1=OP.add)
                VE.tensor_mul(fgel[pp][:, :], ffg[pp][:, :], gt2[pp][:, :])
                VE.wait_ge(se.h, ff1_v)
                VE.tensor_mul(factb[pp][:, :], fgel[pp][:, :], ff1[pp][:, :])
                VE.wait_ge(qa8[pp].h, qa8[pp].n)
                VE.tensor_copy(fact8[pp][:, :], factb[pp][:, :]).then_inc(ve.h, 1)
                a8v = ve.inc()
                chain_used[pp] = a8v
                SY.wait_ge(ve.h, a8v)
                SY.dma_start(
                    act8_s[hc * 128:(hc + 1) * 128,
                           2 * tcp * T:(2 * tcp + 2) * T],
                    fact8[pp][:, :]).then_inc(qa8[pp].h, 16)
                qa8[pp].inc(16)
        barrier(queues=[(SY, qx8), (SY, qa8[0]), (SY, qa8[1]),
                        (GE, qwg[0]), (GE, qwg[1]), (GE, qwl[0]),
                        (GE, qwl[1]), (GE, qws[0]), (GE, qws[1])])

    if _STOP == "F1":
        emit_dbg_dumps()
        es.close()
        return nc

    with ExitStack() as ph:
        fsb = lambda nm, sh, dt_: ph.enter_context(nc.sbuf_tensor(nm, sh, dt_))
        banks8 = [ph.enter_context(nc.psum_tensor(f"pex{i}", [128, T], F32))
                  for i in range(8)]
        a8 = [[fsb(f"a8_{p}_{i}", [128, (NHC // 2) * T], F8) for i in range(2)]
              for p in range(2)]
        wl2 = [fsb(f"wl2_{i}", [128, 2048], F8) for i in range(4)]
        ffsl = fsb("ffsl", [128, NCH * T], BF16)
        outtm = [fsb(f"outtm{i}", [128, D], BF16) for i in range(2)]
        parb = [fsb(f"parb{i}", [128, D], F32) for i in range(2)]
        outb = [fsb(f"outb{i}", [128, D], F32) for i in range(2)]

        qa8 = dstream("qa8_f2")
        qwl2 = dstream("qwl2_f2", 4)
        qpar2 = dstream("qpar_f2")
        qout = dstream("qout_f2")
        qfT = dstream("qfT_f2")
        qtt = dstream("qtt_f2")
        te = sem("te_f2")
        ve = sem("ve_f2")

        NJC2 = NHC // 2
        def load_a8(tc, dep_te):
            p = tc % 2
            if dep_te:
                GE.wait_ge(te.h, dep_te)
            for jc in range(NJC2):
                half = jc // (NJC2 // 2)
                jj = jc % (NJC2 // 2)
                buf = a8[p][half]
                ps = buf.ap().ap[0][0]
                for i in range(2):
                    src_ap = bass.AP(act8_s.tensor,
                                     (256 * jc + i) * OWN + tc * T,
                                     [[2 * OWN, 128], [1, T]])
                    dst_ap = bass.AP(buf.ap().tensor,
                                     buf.ap().offset + jj * 2 * T + i * T,
                                     [[ps, 128], [1, T]])
                    GE.dma_start(dst_ap, src_ap).then_inc(qa8[p].h, 16)
                    qa8[p].inc(16)
            return qa8[p].n

        def a8_rhs(tc, jc):
            half = jc // (NJC2 // 2)
            jj = jc % (NJC2 // 2)
            buf = a8[tc % 2][half]
            return bass.AP(buf.ap().tensor,
                           buf.ap().offset + jj * 2 * T,
                           [[buf.ap().ap[0][0], 128], [T, 2], [1, T]])

        a8_vals = [load_a8(0, 0)]
        par_vals = {}
        tr_free2 = {}
        bank_busy = [0] * 8
        mm_last = 0
        wl_buf_user = [0, 0, 0, 0]
        wl_vals = {}

        SEr = nc.scalar

        def emit_par_load(i):
            SEr.wait_ge(ve.h, tr_free2.get((i - 2, 0), 0))
            SEr.dma_start(parb[i % 2][:, :], par_s[i * 128:(i + 1) * 128, :]
                          ).then_inc(qpar2[i % 2].h, 16)
            par_vals[i] = qpar2[i % 2].inc(16)

        emit_par_load(0)
        emit_par_load(1)

        def load_wl2(key, buf):
            tck, mh, jc = key
            SY.wait_ge(te.h, wl_buf_user[buf])
            ps = wl2[buf].ap().ap[0][0]
            for i in range(2):
                src_ap = bass.AP(wl8_s.tensor,
                                 (256 * jc + i) * D + mh * 1024,
                                 [[2 * D, 128], [1, 1024]])
                dst_ap = bass.AP(wl2[buf].ap().tensor,
                                 wl2[buf].ap().offset + i * 1024,
                                 [[ps, 128], [1, 1024]])
                SY.dma_start(dst_ap, src_ap).then_inc(qwl2[buf].h, 16)
                qwl2[buf].inc(16)
            wl_vals[key] = qwl2[buf].n

        def wl2_lhsT(buf, mi):
            return bass.AP(wl2[buf].ap().tensor,
                           wl2[buf].ap().offset + mi * 128,
                           [[wl2[buf].ap().ap[0][0], 128], [1024, 2],
                            [1, 128]])

        all_keys = [(tc_, mh_, jc_) for tc_ in range(NC_OWN)
                    for mh_ in range(2) for jc_ in range(NJC2)]
        for idx0 in range(3):
            load_wl2(all_keys[idx0], idx0 % 4)
        gstep = 0
        for tc in range(NC_OWN):
            p = tc % 2
            TE.wait_ge(qa8[p].h, a8_vals[tc])
            if tc + 1 < NC_OWN:
                a8_vals.append(load_a8(tc + 1, mm_last))
            evac_v = {}
            for mh in range(2):
                for jc in range(NJC2):
                    buf = gstep % 4
                    if gstep + 3 < len(all_keys):
                        load_wl2(all_keys[gstep + 3], (gstep + 3) % 4)
                    gstep += 1
                    TE.wait_ge(qwl2[buf].h, wl_vals[(tc, mh, jc)])
                    if jc == 0:
                        for mi in range(8):
                            if bank_busy[mi]:
                                TE.wait_ge(ve.h, bank_busy[mi])
                    rhs = a8_rhs(tc, jc)
                    for mi in range(8):
                        _inst = TE.matmul(
                            banks8[mi][:, :],
                            wl2_lhsT(buf, mi),
                            rhs,
                            start=(jc == 0), stop=(jc == NJC2 - 1),
                            perf_mode=mybir.MatmulPerfMode.DoubleRow)
                    _inst.then_inc(te.h, 1)
                    mm_last = te.inc()
                    wl_buf_user[buf] = mm_last
                for mi in range(8):
                    m = mh * 8 + mi
                    VE.wait_ge(te.h, mm_last)
                    if tc > 0 and mi == 0:
                        VE.wait_ge(qfT[0].h, qfT[0].n)
                        VE.wait_ge(qfT[1].h, qfT[1].n)
                    VE.tensor_copy(ffsl[:, m * T:(m + 1) * T],
                                   banks8[mi][:, :]).then_inc(ve.h, 1)
                    evac_v[m] = ve.inc()
                    bank_busy[mi] = evac_v[m]
                    SY.wait_ge(ve.h, evac_v[m])
                    SY.dma_start(
                        ffnT_s[m * 128:(m + 1) * 128, tc * T:(tc + 1) * T],
                        ffsl[:, m * T:(m + 1) * T]).then_inc(qfT[m % 2].h, 16)
                    qfT[m % 2].inc(16)
            fT_done = (qfT[0].n, qfT[1].n)
            for s in range(4):
                i = tc * 4 + s
                ob = outb[i % 2]
                SEr.wait_ge(ve.h, tr_free2.get((i - 2, 0), 0))
                SEr.wait_ge(qfT[0].h, fT_done[0])
                SEr.wait_ge(qfT[1].h, fT_done[1])
                SEr.dma_start_transpose(
                    outtm[i % 2][:, :],
                    ffnT_s[:, i * 128:(i + 1) * 128]).then_inc(qtt[i % 2].h, 16)
                qtt[i % 2].inc(16)
                if i >= 2:
                    VE.wait_ge(qout[i % 2].h, qout[i % 2].n)
                VE.wait_ge(qtt[i % 2].h, qtt[i % 2].n)
                VE.wait_ge(qpar2[i % 2].h, par_vals[i])
                VE.tensor_add(ob[:, :], outtm[i % 2][:, :],
                              parb[i % 2][:, :]).then_inc(ve.h, 1)
                tr_free2[(i, 0)] = ve.inc()
                SEr.wait_ge(ve.h, tr_free2[(i, 0)])
                SEr.dma_start(out[i * 128:(i + 1) * 128, :], ob[:, :]
                              ).then_inc(qout[i % 2].h, 16)
                qout[i % 2].inc(16)
                if i + 2 < OWN // 128:
                    emit_par_load(i + 2)
        SEr.wait_ge(qout[0].h, qout[0].n)
        SEr.wait_ge(qout[1].h, qout[1].n)
        SY.wait_ge(qfT[0].h, qfT[0].n)
        SY.wait_ge(qfT[1].h, qfT[1].n)
        GE.wait_ge(qa8[0].h, qa8[0].n)
        GE.wait_ge(qa8[1].h, qa8[1].n)
        emit_dbg_dumps()

    es.close()
    return nc


_NC_CACHE = None


def _get_nc():
    global _NC_CACHE
    if _NC_CACHE is None:
        _NC_CACHE = build()
    return _NC_CACHE


def _make_in_maps(x, Wq, Wv, Wg, Wo, beta, ln_g, ln_b, wgt_, wlin_):
    import ml_dtypes
    beta = np.ascontiguousarray(
        np.broadcast_to(beta.reshape(1, H), (128, H)))
    ln_g = np.ascontiguousarray(ln_g.reshape(NCH, 128).T)
    ln_b = np.ascontiguousarray(ln_b.reshape(NCH, 128).T)
    cid32 = np.eye(128, dtype=np.float32)
    cid16 = np.eye(128, dtype=ml_dtypes.bfloat16)
    conc = np.ones((128, 1), np.float32)
    conr = np.ones((1, 128), np.float32)
    in_maps = []
    for c in range(8):
        b, h = c % 4, c // 4
        xpv = np.zeros((NT, D), np.float32)
        if h == 0:
            xpv[PRE:] = x[b, 0:OWN]
        else:
            xpv[:] = x[b]
        in_maps.append({
            "xp": xpv, "wq": Wq, "wv": Wv, "wg": Wg, "wo": Wo,
            "beta": beta, "lng": ln_g, "lnb": ln_b,
            "wgt": wgt_, "wlin": wlin_,
            "cid32": cid32, "cid16": np.asarray(cid16),
            "conc": conc, "conr": conr,
        })
    return in_maps


def kernel(L0_attn_x, L0_attn_Wq, L0_attn_Wv, L0_attn_Wg, L0_attn_Wo,
           L0_attn_beta, L0_attn_ln_g, L0_attn_ln_b,
           L0_ffn_w_gating, L0_ffn_w_linear):
    x = np.ascontiguousarray(np.asarray(L0_attn_x, dtype=np.float32))
    Wq = np.ascontiguousarray(np.asarray(L0_attn_Wq, dtype=np.float32))
    Wv = np.ascontiguousarray(np.asarray(L0_attn_Wv, dtype=np.float32))
    Wg = np.ascontiguousarray(np.asarray(L0_attn_Wg, dtype=np.float32))
    Wo = np.ascontiguousarray(np.asarray(L0_attn_Wo, dtype=np.float32))
    beta = np.asarray(L0_attn_beta, dtype=np.float32)
    ln_g = np.asarray(L0_attn_ln_g, dtype=np.float32)
    ln_b = np.asarray(L0_attn_ln_b, dtype=np.float32)
    wgt_ = np.ascontiguousarray(np.asarray(L0_ffn_w_gating, dtype=np.float32))
    wlin_ = np.ascontiguousarray(np.asarray(L0_ffn_w_linear, dtype=np.float32))

    nc = _get_nc()
    in_maps = _make_in_maps(x, Wq, Wv, Wg, Wo, beta, ln_g, ln_b, wgt_, wlin_)
    res = run_bass_kernel_spmd(nc, in_maps, core_ids=list(range(8)))
    outp = np.empty((B_, S_, D), np.float32)
    for c in range(8):
        b, h = c % 4, c // 4
        outp[b, h * OWN:(h + 1) * OWN] = res.results[c]["out"]
    return outp


# version 14
# speedup vs baseline: 1.0024x; 1.0024x over previous
import numpy as np
from contextlib import ExitStack

import concourse.bass as bass
import concourse.mybir as mybir
from concourse.bass_utils import run_bass_kernel_spmd

F32 = mybir.dt.float32
BF16 = mybir.dt.bfloat16
F8 = mybir.dt.float8e4
AF = mybir.ActivationFunctionType
OP = mybir.AluOpType
AXX = mybir.AxisListType.X

B_, S_, D = 4, 8192, 2048
H, DH = 16, 128
NCH = D // 128
OWN = 4096
PRE = 4096
NT = PRE + OWN
T = 512
NC_ALL = NT // T
NC_OWN = OWN // T
HID = 8192
NHC = HID // 128
EPS = 1e-5
INV_D = 1.0 / D

C1BF = 0.796875
C2BF = 0.044677734375

_DBG = False
_STOP = None
_A_GROUPS = None
_A_MODE = "full"


class Ctr:
    __slots__ = ("h", "n")

    def __init__(self, h):
        self.h = h
        self.n = 0

    def inc(self, k=1):
        self.n += k
        return self.n


def build():
    nc = bass.Bass("TRN2", target_bir_lowering=False)

    xp = nc.dram_tensor("xp", [NT, D], F32, kind="ExternalInput").ap()
    wq = nc.dram_tensor("wq", [D, D], F32, kind="ExternalInput").ap()
    wv = nc.dram_tensor("wv", [D, D], F32, kind="ExternalInput").ap()
    wg = nc.dram_tensor("wg", [D, D], F32, kind="ExternalInput").ap()
    wo = nc.dram_tensor("wo", [D, D], F32, kind="ExternalInput").ap()
    beta = nc.dram_tensor("beta", [128, H], F32, kind="ExternalInput").ap()
    lng = nc.dram_tensor("lng", [128, NCH], F32, kind="ExternalInput").ap()
    lnb = nc.dram_tensor("lnb", [128, NCH], F32, kind="ExternalInput").ap()
    wgt = nc.dram_tensor("wgt", [2, D, HID], F32, kind="ExternalInput").ap()
    wlin = nc.dram_tensor("wlin", [HID, D], F32, kind="ExternalInput").ap()
    cid32 = nc.dram_tensor("cid32", [128, 128], F32, kind="ExternalInput").ap()
    cid16 = nc.dram_tensor("cid16", [128, 128], BF16, kind="ExternalInput").ap()
    conc = nc.dram_tensor("conc", [128, 1], F32, kind="ExternalInput").ap()
    conr = nc.dram_tensor("conr", [1, 128], F32, kind="ExternalInput").ap()
    out = nc.dram_tensor("out", [OWN, D], F32, kind="ExternalOutput").ap()

    xnbf_s = nc.dram_tensor("xnbf_s", [D, NT], BF16, kind="Internal").ap()
    xn8_s = nc.dram_tensor("xn8_s", [D, OWN], F8, kind="Internal").ap()
    st_s = nc.dram_tensor("st_s", [D, OWN], BF16, kind="Internal").ap()
    z1_s = nc.dram_tensor("z1_s", [D, OWN], BF16, kind="Internal").ap()
    z_s = nc.dram_tensor("z_s", [D, OWN], BF16, kind="Internal").ap()
    act8_s = nc.dram_tensor("act8_s", [HID, OWN], F8, kind="Internal").ap()
    par_s = nc.dram_tensor("par_s", [OWN, D], F32, kind="Internal").ap()
    wl8_s = nc.dram_tensor("wl8_s", [HID, D], F8, kind="Internal").ap()
    ffnT_s = nc.dram_tensor("ffnT_s", [D, OWN], BF16, kind="Internal").ap()
    wbf_s = nc.dram_tensor("wbf_s", [3 * D, D], BF16, kind="Internal").ap()

    dbg = {}
    if _DBG:
        for nm, sh, dt_ in [
            ("d_xnbf", [D, NT], BF16), ("d_xn8", [D, OWN], F8),
            ("d_st", [D, OWN], BF16), ("d_z1", [D, OWN], BF16),
            ("d_z", [D, OWN], BF16), ("d_par", [OWN, D], F32),
            ("d_act8", [HID, OWN], F8), ("d_lam", [128, H], F32),
        ]:
            dbg[nm] = nc.dram_tensor(nm, sh, dt_, kind="ExternalOutput").ap()

    class _EngWrap:

        def __init__(self, eng):
            self._e = eng
            self._w = {}
            self._r = {}

        @staticmethod
        def _iv(ap):
            lo = ap.offset
            hi = lo + 1
            for step, count in list(ap.ap)[1:]:
                hi += step * (count - 1)
            return (ap.tensor.name, lo, hi)

        @staticmethod
        def _hit(iv, table):
            spans = table.get(iv[0])
            if not spans:
                return False
            for lo, hi in spans:
                if iv[1] < hi and lo < iv[2]:
                    return True
            return False

        @staticmethod
        def _add(iv, table):
            table.setdefault(iv[0], []).append((iv[1], iv[2]))

        def drain(self, *a, **k):
            self._w.clear()
            self._r.clear()
            return self._e.drain(*a, **k)

        def __getattr__(self, name):
            attr = getattr(self._e, name)
            if name in ("wait_ge", "sem_inc", "dma_start"):
                return attr
            if not callable(attr):
                return attr

            def call(*args, **kwargs):
                aps = [a for a in args if isinstance(a, bass.AP)]
                outs = aps[:1]
                ins = aps[1:]
                for kwn, kv in kwargs.items():
                    if isinstance(kv, bass.AP):
                        (outs if kwn == "accum_out" else ins).append(kv)
                ivs_o = [self._iv(a) for a in outs]
                ivs_i = [self._iv(a) for a in ins]
                if (any(self._hit(iv, self._w) for iv in ivs_i)
                        or any(self._hit(iv, self._w) or self._hit(iv, self._r)
                               for iv in ivs_o)):
                    self._e.drain()
                    self._w.clear()
                    self._r.clear()
                for iv in ivs_o:
                    self._add(iv, self._w)
                for iv in ivs_i:
                    self._add(iv, self._r)
                return attr(*args, **kwargs)

            return call

    TE, GE, SY = nc.tensor, nc.gpsimd, nc.sync
    VE = _EngWrap(nc.vector)
    SE = _EngWrap(nc.scalar)
    GP = _EngWrap(nc.gpsimd)

    es = ExitStack()

    def sbuf(nm, sh, dt_):
        return es.enter_context(nc.sbuf_tensor(nm, sh, dt_))

    def psum(nm, sh, dt_):
        return es.enter_context(nc.psum_tensor(nm, sh, dt_))

    def sem(nm):
        return Ctr(es.enter_context(nc.semaphore(nm)))

    def dstream(nm, n=2):
        return [sem(f"{nm}{i}") for i in range(n)]

    sid32 = sbuf("sid32", [128, 128], F32)
    sid16 = sbuf("sid16", [128, 128], BF16)
    sonc = sbuf("sonc", [128, 1], F32)
    sonr = sbuf("sonr", [1, 128], F32)
    sbeta = sbuf("sbeta", [128, H], F32)
    slam = sbuf("slam", [128, H], F32)
    slng = sbuf("slng", [128, NCH], F32)
    slnb = sbuf("slnb", [128, NCH], F32)
    scarry = sbuf("scarry", [128, H], F32)


    bar = sem("bar")
    ALL_ENG = (TE, VE, SE, GE, SY)

    def barrier(queues=()):
        for eng, q in queues:
            eng.wait_ge(q.h, q.n)
        for eng in (TE, VE, SE):
            eng.drain()
        for eng in ALL_ENG:
            eng.sem_inc(bar.h, 1)
        bar.inc(5)
        for eng in ALL_ENG:
            eng.wait_ge(bar.h, bar.n)

    def emit_dbg_dumps():
        if not _DBG:
            return
        barrier()
        qd = sem("qs_dbg")
        for nm, src_ in [("d_xnbf", xnbf_s), ("d_xn8", xn8_s),
                         ("d_st", st_s), ("d_z1", z1_s), ("d_z", z_s),
                         ("d_par", par_s), ("d_act8", act8_s)]:
            SY.dma_start(dbg[nm], src_).then_inc(qd.h, 16)
            qd.inc(16)
        SY.wait_ge(qd.h, qd.n)

    def lam_bcast(hd, n):
        col = slam[:, hd:hd + 1]
        return bass.AP(col.tensor, col.offset, [[col.ap[0][0], 128], [0, n]])

    qgp = sem("qg_prep")
    for dst, src in [(sid32, cid32), (sid16, cid16), (sonc, conc),
                     (sonr, conr), (sbeta, beta), (slng, lng), (slnb, lnb)]:
        GE.dma_start(dst[:, :], src).then_inc(qgp.h, 16)
        qgp.inc(16)
    SE.wait_ge(qgp.h, qgp.n)
    SE.activation(slam[:, :], sbeta[:, :], AF.Sigmoid)
    if _DBG:
        sep = sem("se_prep")
        SE.drain()
        nc.scalar.sem_inc(sep.h, 1)
        sep.inc()
        qsp = sem("qs_prep")
        SY.wait_ge(sep.h, sep.n)
        SY.dma_start(dbg["d_lam"], slam[:, :]).then_inc(qsp.h, 16)
        qsp.inc(16)
        SY.wait_ge(qsp.h, qsp.n)
    barrier(queues=[(GE, qgp)])

    if _STOP == "P":
        es.close()
        return nc

    ws = ExitStack()
    w_sb = ws.enter_context(nc.sbuf_tensor("w_sb", [128, NCH * D], BF16))
    pMM = [ws.enter_context(nc.psum_tensor(f"pmm{i}", [128, T], F32))
           for i in range(4)]
    qg_w = [sem(f"qg_w{i}") for i in range(4)]

    def load_weight(wap, qgc):
        for k in range(NCH):
            GE.dma_start(w_sb[:, k * D:(k + 1) * D],
                         wap[k * 128:(k + 1) * 128, :]).then_inc(qgc.h, 16)
            qgc.inc(16)

    def load_weight_bf(wi, qgc):
        for k in range(NCH):
            SY.dma_start(w_sb[:, k * D:(k + 1) * D],
                         wbf_s[(wi * NCH + k) * 128:(wi * NCH + k + 1) * 128, :]
                         ).then_inc(qgc.h, 16)
            qgc.inc(16)

    load_weight(wv, qg_w[0])

    with ExitStack() as ph:
        asb = lambda nm, sh, dt_: ph.enter_context(nc.sbuf_tensor(nm, sh, dt_))
        xbuf = [asb(f"a_x{i}", [128, D], F32) for i in range(4)]
        xntm = [asb(f"a_xn{i}", [128, D], F32) for i in range(8)]
        sqdump = asb("a_sqd", [128, D], F32)
        stt_t = asb("a_stt", [128, 48], F32)
        stgbf = [asb(f"a_gbf{i}", [128, T], BF16) for i in range(2)]
        stg8 = [asb(f"a_g8{i}", [128, T], F8) for i in range(2)]
        wcb = [asb(f"a_wcb{i}", [128, D], BF16) for i in range(2)]
        pTR = [ph.enter_context(nc.psum_tensor(f"ptra{i}", [128, T], F32))
               for i in range(2)] + pMM[0:2]

        qwc = dstream("qwc_a")
        qwps = dstream("qwps_a")
        for wi, wap in enumerate((wq, wg, wo)):
            for k in range(NCH):
                p = k % 2
                if wi or k >= 2:
                    GE.wait_ge(qwps[p].h, qwps[p].n)
                GE.dma_start(wcb[p][:, :], wap[k * 128:(k + 1) * 128, :]
                             ).then_inc(qwc[p].h, 16)
                qwc[p].inc(16)
                GE.wait_ge(qwc[p].h, qwc[p].n)
                GE.dma_start(
                    wbf_s[(wi * NCH + k) * 128:(wi * NCH + k + 1) * 128, :],
                    wcb[p][:, :]).then_inc(qwps[p].h, 16)
                qwps[p].inc(16)

        qx = dstream("qx_a", 4)
        qbf = dstream("qbf_a")
        q8 = dstream("q8_a")
        ve = sem("ve_a")
        se = sem("se_a")
        te = sem("te_a")

        NTILE = NT // 128
        ld_done = [0] * NTILE
        xn_done = [0] * NTILE
        grp_tr_done = [0] * NC_ALL
        evac_done = {}

        def wait_xn(eng, i):
            v = xn_done[i]
            if isinstance(v, tuple):
                eng.wait_ge(se.h, v[1])
            else:
                eng.wait_ge(ve.h, v)

        def emit_x_load(i):
            if i >= 4:
                wait_xn(SY, i - 4)
            SY.dma_start(xbuf[i % 4][:, :], xp[i * 128:(i + 1) * 128, :]
                         ).then_inc(qx[i % 4].h, 16)
            ld_done[i] = qx[i % 4].inc(16)

        for i in range(4):
            emit_x_load(i)

        for g in range(NC_ALL):
            own_g = g >= NC_ALL - NC_OWN
            stt = stt_t[:, (g % 2) * 24:(g % 2) * 24 + 24]
            if g >= 2:
                VE.wait_ge(te.h, grp_tr_done[g - 2])
                SE.wait_ge(te.h, grp_tr_done[g - 2])
            for s in range(4):
                i = g * 4 + s
                xb = xbuf[i % 4]
                VE.wait_ge(qx[i % 4].h, ld_done[i])
                SE.wait_ge(qx[i % 4].h, ld_done[i])
                if s < 2:
                    VE.reduce_sum(stt[:, s:s + 1], xb[:, :], axis=AXX)
                else:
                    SE.activation(sqdump[:, :], xb[:, :], AF.Identity,
                                  accum_out=stt[:, s:s + 1])
                SE.activation(sqdump[:, :], xb[:, :], AF.Square,
                              accum_out=stt[:, 4 + s:5 + s])
            SE.drain()
            nc.scalar.sem_inc(se.h, 1)
            sq_v = se.inc()
            VE.wait_ge(se.h, sq_v)
            VE.tensor_scalar_mul(stt[:, 8:12], stt[:, 0:4], -INV_D)
            VE.tensor_scalar_mul(stt[:, 12:16], stt[:, 4:8], INV_D)
            VE.tensor_mul(stt[:, 16:20], stt[:, 8:12], stt[:, 8:12])
            VE.tensor_sub(stt[:, 12:16], stt[:, 12:16], stt[:, 16:20])
            VE.tensor_scalar_add(stt[:, 12:16], stt[:, 12:16], EPS).then_inc(ve.h, 1)
            d_v = ve.inc()
            SE.wait_ge(ve.h, d_v)
            SE.activation(stt[:, 16:20], stt[:, 12:16], AF.Sqrt).then_inc(se.h, 1)
            s0_v = se.inc()
            VE.wait_ge(se.h, s0_v)
            VE.reciprocal(stt[:, 20:24], stt[:, 16:20])
            VE.tensor_mul(stt[:, 16:20], stt[:, 20:24], stt[:, 20:24])
            VE.tensor_mul(stt[:, 16:20], stt[:, 16:20], stt[:, 12:16])
            VE.tensor_scalar(stt[:, 16:20], stt[:, 16:20], -0.5, 1.5,
                             op0=OP.mult, op1=OP.add)
            VE.tensor_mul(stt[:, 20:24], stt[:, 20:24], stt[:, 16:20])
            VE.tensor_mul(stt[:, 12:16], stt[:, 8:12], stt[:, 20:24]).then_inc(ve.h, 1)
            r_v = ve.inc()
            SE.wait_ge(ve.h, r_v)
            for s in range(4):
                i = g * 4 + s
                xb = xbuf[i % 4]
                xt = xntm[(g % 2) * 4 + s]
                if s < 2:
                    VE.tensor_scalar(xt[:, :], xb[:, :], stt[:, 8 + s:9 + s],
                                     stt[:, 20 + s:21 + s],
                                     op0=OP.add, op1=OP.mult).then_inc(ve.h, 1)
                    xn_done[i] = ve.inc()
                else:
                    SE.activation(xt[:, :], xb[:, :], AF.Identity,
                                  bias=stt[:, 12 + s:13 + s],
                                  scale=stt[:, 20 + s:21 + s]).then_inc(se.h, 1)
                    xn_done[i] = ("se", se.inc())
                if i + 4 < NTILE:
                    emit_x_load(i + 4)

            for j in range(NCH):
                bank = pTR[j % 4]
                key = (g, j - 4) if j >= 4 else (g - 1, NCH - 4 + j)
                if key in evac_done:
                    vv, sv = evac_done[key]
                    TE.wait_ge(ve.h, vv)
                    TE.wait_ge(se.h, sv)
                if j == 0:
                    for s4 in range(4):
                        wait_xn(TE, g * 4 + s4)
                for s in range(4):
                    xt = xntm[(g % 2) * 4 + s]
                    _inst = TE.transpose(bank[:, s * 128:(s + 1) * 128],
                                         xt[:, j * 128:(j + 1) * 128],
                                         sid32[:, :])
                _inst.then_inc(te.h, 1)
                tr_v = te.inc()
                if j == NCH - 1:
                    grp_tr_done[g] = tr_v
                stb = stgbf[j % 2]
                st8 = stg8[j % 2]
                VE.wait_ge(te.h, tr_v)
                VE.wait_ge(qbf[j % 2].h, qbf[j % 2].n)
                VE.tensor_copy(stb[:, :], bank[:, :]).then_inc(ve.h, 1)
                bf_v = ve.inc()
                sv_8 = 0
                if own_g:
                    SE.wait_ge(ve.h, bf_v)
                    SE.wait_ge(q8[j % 2].h, q8[j % 2].n)
                    SE.copy(st8[:, :], bank[:, :]).then_inc(se.h, 1)
                    sv_8 = se.inc()
                evac_done[(g, j)] = (bf_v, sv_8)
                SY.wait_ge(ve.h, bf_v)
                SY.dma_start(xnbf_s[j * 128:(j + 1) * 128, g * T:(g + 1) * T],
                             stb[:, :]).then_inc(qbf[j % 2].h, 16)
                qbf[j % 2].inc(16)
                if own_g:
                    go = g - (NC_ALL - NC_OWN)
                    SY.wait_ge(se.h, sv_8)
                    SY.dma_start(
                        xn8_s[j * 128:(j + 1) * 128, go * T:(go + 1) * T],
                        st8[:, :]).then_inc(q8[j % 2].h, 16)
                    q8[j % 2].inc(16)

        barrier(queues=[(SY, qx[0]), (SY, qx[1]), (SY, qx[2]), (SY, qx[3]),
                        (SY, qbf[0]), (SY, qbf[1]),
                        (SY, q8[0]), (SY, q8[1]),
                        (GE, qwc[0]), (GE, qwc[1]),
                        (GE, qwps[0]), (GE, qwps[1])])

    if _STOP == "A":
        emit_dbg_dumps()
        ws.close()
        es.close()
        return nc

    def _emit_bcde():
      with ExitStack() as ph:
        bsb = lambda nm, sh, dt_: ph.enter_context(nc.sbuf_tensor(nm, sh, dt_))
        xnbuf = [bsb(f"xnb{i}", [128, NCH * T], BF16) for i in range(2)]
        v_sb = bsb("v_sb", [128, NCH * T], F32)
        s2_sb = bsb("s2_sb", [128, NCH * T], F32)

        qpre = dstream("qpre")
        pre_c = {}
        pre_d = {}
        pre_e = {}

        def load_fm_chunk(dst, src, c, qsc, dep=None):
            if dep is not None:
                SY.wait_ge(dep[0].h, dep[1])
            for k in range(NCH):
                SY.dma_start(dst[:, k * T:(k + 1) * T],
                             src[k * 128:(k + 1) * 128, c * T:(c + 1) * T]
                             ).then_inc(qsc.h, 16)
                qsc.inc(16)
            return qsc.n

        def emit_chunk_groups(c, xnb, te_c, free_list, free_sem, emit_evac):
            vals = []
            for m in range(NCH):
                gi = c * NCH + m
                if gi >= 4:
                    TE.wait_ge(free_sem.h, free_list[gi - 4])
                bank = pMM[gi % 4]
                for k in range(NCH):
                    _inst = TE.matmul(
                        bank[:, :],
                        w_sb[:, k * D + m * 128: k * D + (m + 1) * 128],
                        xnb[:, k * T:(k + 1) * T],
                        start=(k == 0), stop=(k == NCH - 1))
                _inst.then_inc(te_c.h, 1)
                v = te_c.inc()
                vals.append(v)
                emit_evac(c, m, bank, v)
            return vals

        qg = qg_w[0]
        qxn = dstream("qxn_b")
        qst = dstream("qst_b")
        te = sem("te_b")
        ve = sem("ve_b")
        se = sem("se_b")
        b_se_last = [0]
        with ExitStack() as pb:
            stg = [pb.enter_context(nc.sbuf_tensor(f"b_stg{i}", [128, T], BF16))
                   for i in range(2)]
            v_sb2 = pb.enter_context(
                nc.sbuf_tensor("v_sb2", [128, NCH * T], F32))
            vbufs = [v_sb, v_sb2]

            TE.wait_ge(qg.h, qg.n)

            free_vals = []
            grp_last = [0] * NC_ALL

            def b_evac(c, m, bank, v):
                VE.wait_ge(te.h, v)
                VE.tensor_copy(vbufs[c % 2][:, m * T:(m + 1) * T],
                               bank[:, :]).then_inc(ve.h, 1)
                free_vals.append(ve.inc())

            def b_scan(c):
                vb = vbufs[c % 2]
                own_c = c >= NC_ALL - NC_OWN
                scan_v = {}
                if b_se_last[0]:
                    VE.wait_ge(se.h, b_se_last[0])
                for hd in range(H):
                    init = scarry[:, hd:hd + 1] if c > 0 else 0.0
                    VE.tensor_tensor_scan(
                        s2_sb[:, hd * T:(hd + 1) * T], lam_bcast(hd, T),
                        vb[:, hd * T:(hd + 1) * T], init,
                        op0=OP.mult, op1=OP.add).then_inc(ve.h, 1)
                    scan_v[hd] = ve.inc()
                lastcols = bass.AP(s2_sb.ap().tensor,
                                   s2_sb.ap().offset + T - 1,
                                   [[s2_sb.ap().ap[0][0], 128], [T, H]])
                VE.tensor_copy(scarry[:, :], lastcols)
                if own_c:
                    for hd in range(H):
                        SE.wait_ge(ve.h, scan_v[hd])
                        SE.wait_ge(qst[hd % 2].h, qst[hd % 2].n)
                        SE.copy(stg[hd % 2][:, :],
                                s2_sb[:, hd * T:(hd + 1) * T]).then_inc(se.h, 1)
                        sv = se.inc()
                        SY.wait_ge(se.h, sv)
                        co = c - (NC_ALL - NC_OWN)
                        SY.dma_start(
                            st_s[hd * 128:(hd + 1) * 128,
                                 co * T:(co + 1) * T],
                            stg[hd % 2][:, :]).then_inc(qst[hd % 2].h, 16)
                        qst[hd % 2].inc(16)
                    b_se_last[0] = se.n

            ld_vals = [(qxn[0], load_fm_chunk(xnbuf[0], xnbf_s, 0, qxn[0]))]
            for c in range(NC_ALL):
                if c + 1 < NC_ALL:
                    dep = (te, grp_last[c - 1]) if c >= 1 else None
                    ld_vals.append((qxn[(c + 1) % 2],
                                    load_fm_chunk(xnbuf[(c + 1) % 2], xnbf_s,
                                                  c + 1, qxn[(c + 1) % 2],
                                                  dep=dep)))
                TE.wait_ge(ld_vals[c][0].h, ld_vals[c][1])
                vals = emit_chunk_groups(c, xnbuf[c % 2], te, free_vals, ve,
                                         b_evac)
                grp_last[c] = vals[-1]
                if c >= 1:
                    b_scan(c - 1)
            b_scan(NC_ALL - 1)
            SY.wait_ge(te.h, grp_last[NC_ALL - 1])
            load_weight_bf(0, qg_w[1])
            pre_c[0] = (qpre[0], load_fm_chunk(
                xnbuf[0], xnbf_s, NC_ALL - NC_OWN, qpre[0],
                dep=(te, grp_last[NC_ALL - 2])))
            pre_c[1] = (qpre[1], load_fm_chunk(
                xnbuf[1], xnbf_s, NC_ALL - NC_OWN + 1, qpre[1],
                dep=(te, grp_last[NC_ALL - 1])))
            barrier(queues=[(SY, qxn[0]), (SY, qxn[1]), (SY, qst[0]),
                            (SY, qst[1]), (GE, qg)])

        if _STOP == "B":
            return
        qg = qg_w[1]
        qxn = dstream("qxn_c")
        qstl = dstream("qstl_c")
        qz1 = dstream("qz1_c")
        te = sem("te_c")
        ve = sem("ve_c")
        se = sem("se_c")
        ge = sem("ge_c")
        with ExitStack() as pc:
            csb = lambda nm, sh, dt_: pc.enter_context(nc.sbuf_tensor(nm, sh, dt_))
            pst1 = pc.enter_context(nc.psum_tensor("pst1", [1, T], F32))
            pst2 = pc.enter_context(nc.psum_tensor("pst2", [1, T], F32))
            pTR = [pc.enter_context(nc.psum_tensor(f"ptrc{i}", [128, T], F32))
                   for i in range(2)]
            stbuf = [csb(f"stb{i}", [128, NCH * T], BF16) for i in range(2)]
            gztmp = csb("gztmp", [128, 1024], F32)
            musigrep = csb("musigrep", [128, T], F32)
            rsigrep = csb("rsigrep", [128, T], F32)
            ztmp = csb("ztmp", [128, T], F32)
            y2t = [csb("y2t0", [128, T], F32), ztmp]
            z1stg = [csb(f"z1s{i}", [128, T], BF16) for i in range(2)]

            ybufs = [v_sb, s2_sb]

            TE.wait_ge(qg.h, qg.n)

            free_vals = []
            grp_last = [0] * NC_OWN
            y_last = [0] * NC_OWN
            st_vals = [0] * NC_OWN
            st2_v = {}
            row_done = [0] * NC_OWN
            z1_done = [0] * NC_OWN
            z1_pair = {}

            def c_evac(c, m, bank, v):
                VE.wait_ge(te.h, v)
                if m == 0:
                    VE.wait_ge(qstl[c % 2].h, st_vals[c])
                if c >= 2:
                    VE.wait_ge(ge.h, z1_pair[(c - 2, m // 2)])
                VE.tensor_tensor(ybufs[c % 2][:, m * T:(m + 1) * T], bank[:, :],
                                 stbuf[c % 2][:, m * T:(m + 1) * T],
                                 op=OP.mult).then_inc(ve.h, 1)
                free_vals.append(ve.inc())
                if m == NCH - 1:
                    y_last[c] = free_vals[-1]

            def c_stats_z1(c):
                y_sb = ybufs[c % 2]
                TE.wait_ge(ve.h, y_last[c])
                if c >= 1:
                    TE.wait_ge(ve.h, row_done[c - 1])
                for m in range(NCH):
                    _inst = TE.matmul(pst1[:, :], sonc[:, :],
                                      y_sb[:, m * T:(m + 1) * T],
                                      start=(m == 0), stop=(m == NCH - 1))
                _inst.then_inc(te.h, 1)
                te.inc()
                for m in range(NCH):
                    if m >= 2:
                        VE.wait_ge(te.h, st2_v[(c, m - 2)])
                    VE.tensor_mul(y2t[m % 2][:, :],
                                  y_sb[:, m * T:(m + 1) * T],
                                  y_sb[:, m * T:(m + 1) * T]).then_inc(ve.h, 1)
                    y2v = ve.inc()
                    TE.wait_ge(ve.h, y2v)
                    _inst = TE.matmul(pst2[:, :], sonc[:, :], y2t[m % 2][:, :],
                                      start=(m == 0), stop=(m == NCH - 1))
                    _inst.then_inc(te.h, 1)
                    st2_v[(c, m)] = te.inc()
                st_mm = st2_v[(c, NCH - 1)]
                VE.wait_ge(te.h, st_mm)
                if c >= 1:
                    VE.wait_ge(ge.h, z1_done[c - 1])
                rowb = pst2[0:1, :]
                rowN = gztmp[0:1, 0:T]
                VE.tensor_scalar_mul(rowN, pst1[0:1, :], -INV_D)
                VE.tensor_scalar_mul(rowb, rowb, INV_D)
                VE.tensor_mul(rsigrep[0:1, 0:T], rowN, rowN)
                VE.tensor_sub(rowb, rowb, rsigrep[0:1, 0:T])
                VE.tensor_scalar_add(rowb, rowb, EPS).then_inc(ve.h, 1)
                d_v = ve.inc()
                SE.wait_ge(ve.h, d_v)
                SE.activation(rsigrep[0:1, 0:T], rowb, AF.Sqrt).then_inc(se.h, 1)
                s0_v = se.inc()
                VE.wait_ge(se.h, s0_v)
                VE.reciprocal(musigrep[0:1, 0:T], rsigrep[0:1, 0:T])
                VE.tensor_mul(rsigrep[0:1, 0:T], musigrep[0:1, 0:T],
                              musigrep[0:1, 0:T])
                VE.tensor_mul(rsigrep[0:1, 0:T], rsigrep[0:1, 0:T],
                              rowb)
                VE.tensor_scalar(rsigrep[0:1, 0:T], rsigrep[0:1, 0:T],
                                 -0.5, 1.5, op0=OP.mult, op1=OP.add)
                VE.tensor_mul(rsigrep[0:1, 0:T], musigrep[0:1, 0:T],
                              rsigrep[0:1, 0:T])
                VE.tensor_mul(musigrep[0:1, 0:T], rowN,
                              rsigrep[0:1, 0:T]).then_inc(ve.h, 1)
                rows_v = ve.inc()
                row_done[c] = rows_v
                TE.wait_ge(ve.h, rows_v)
                TE.matmul(pTR[0][:, :], sonr[:, :], musigrep[0:1, 0:T],
                          start=True, stop=True).then_inc(te.h, 1)
                te.inc()
                TE.matmul(pTR[1][:, :], sonr[:, :], rsigrep[0:1, 0:T],
                          start=True, stop=True).then_inc(te.h, 1)
                bc_mm = te.inc()
                VE.wait_ge(te.h, bc_mm)
                VE.tensor_copy(musigrep[:, :], pTR[0][:, :])
                VE.tensor_copy(rsigrep[:, :], pTR[1][:, :]).then_inc(ve.h, 1)
                rep_v = ve.inc()
                GP.wait_ge(ve.h, rep_v)
                rs_ps = rsigrep.ap().ap[0][0]
                rs2 = bass.AP(rsigrep.ap().tensor, rsigrep.ap().offset,
                              [[rs_ps, 128], [0, 2], [1, T]])
                mu2b = bass.AP(musigrep.ap().tensor, musigrep.ap().offset,
                               [[musigrep.ap().ap[0][0], 128], [0, 2], [1, T]])
                ts_done = 0
                for mp in range(NCH // 2):
                    if ts_done:
                        GP.wait_ge(ve.h, ts_done)
                    GP.tensor_mul(gztmp[:, :],
                                  y_sb[:, 2 * mp * T:(2 * mp + 2) * T], rs2)
                    GP.tensor_add(gztmp[:, :], gztmp[:, :], mu2b).then_inc(ge.h, 1)
                    gv = ge.inc()
                    z1_pair[(c, mp)] = gv
                    VE.wait_ge(ge.h, gv)
                    for h2 in range(2):
                        m = 2 * mp + h2
                        VE.wait_ge(qz1[m % 2].h, qz1[m % 2].n)
                        VE.tensor_scalar(z1stg[m % 2][:, :],
                                         gztmp[:, h2 * T:(h2 + 1) * T],
                                         slng[:, m:m + 1], slnb[:, m:m + 1],
                                         op0=OP.mult, op1=OP.add).then_inc(ve.h, 1)
                        zv = ve.inc()
                        SY.wait_ge(ve.h, zv)
                        SY.dma_start(
                            z1_s[m * 128:(m + 1) * 128, c * T:(c + 1) * T],
                            z1stg[m % 2][:, :]).then_inc(qz1[m % 2].h, 16)
                        qz1[m % 2].inc(16)
                    ts_done = ve.n
                z1_done[c] = ge.n

            ld_vals = [pre_c[0], pre_c[1]]
            st_vals[0] = load_fm_chunk(stbuf[0], st_s, 0, qstl[0])
            for c in range(NC_OWN):
                if c + 1 < NC_OWN:
                    dep2 = (ve, y_last[c - 1]) if c >= 1 else None
                    st_vals[c + 1] = load_fm_chunk(stbuf[(c + 1) % 2], st_s,
                                                   c + 1, qstl[(c + 1) % 2],
                                                   dep=dep2)
                TE.wait_ge(ld_vals[c][0].h, ld_vals[c][1])
                vals = emit_chunk_groups(c, xnbuf[c % 2], te, free_vals, ve,
                                         c_evac)
                grp_last[c] = vals[-1]
                if c + 2 < NC_OWN:
                    ld_vals.append((qxn[c % 2],
                                    load_fm_chunk(xnbuf[c % 2], xnbf_s,
                                                  NC_ALL - NC_OWN + c + 2,
                                                  qxn[c % 2],
                                                  dep=(te, grp_last[c]))))
                if c >= 1:
                    c_stats_z1(c - 1)
            c_stats_z1(NC_OWN - 1)
            SY.wait_ge(te.h, grp_last[NC_OWN - 1])
            load_weight_bf(1, qg_w[2])
            pre_d[0] = (qpre[0], load_fm_chunk(
                xnbuf[0], xnbf_s, NC_ALL - NC_OWN, qpre[0],
                dep=(te, grp_last[NC_OWN - 2])))
            pre_d[1] = (qpre[1], load_fm_chunk(
                xnbuf[1], xnbf_s, NC_ALL - NC_OWN + 1, qpre[1],
                dep=(te, grp_last[NC_OWN - 1])))
            barrier(queues=[(SY, qxn[0]), (SY, qxn[1]), (SY, qstl[0]),
                            (SY, qstl[1]), (SY, qz1[0]), (SY, qz1[1]),
                            (SY, qg)])

        if _STOP == "C":
            return
        qg = qg_w[2]
        qxn = dstream("qxn_d")
        qz1l = dstream("qz1l_d")
        qzs = dstream("qzs_d")
        te = sem("te_d")
        se = sem("se_d")
        ve = sem("ve_d")
        with ExitStack() as pd:
            dsb = lambda nm, sh, dt_: pd.enter_context(nc.sbuf_tensor(nm, sh, dt_))
            z1buf = [dsb(f"z1b{i}", [128, NCH * T], BF16) for i in range(2)]
            gstg = [dsb(f"ph_g{i}", [128, T], BF16) for i in range(2)]
            zstg = [dsb(f"ph_z{i}", [128, T], BF16) for i in range(2)]

            TE.wait_ge(qg.h, qg.n)
            free_vals = []
            grp_last = [0] * NC_OWN
            z_last = [0] * NC_OWN
            z1_vals = [0] * NC_OWN
            z_of = {}
            zs_after = {}

            def d_evac(c, m, bank, v):
                SE.wait_ge(te.h, v)
                if (c * NCH + m) >= 2:
                    SE.wait_ge(ve.h, z_of.get(c * NCH + m - 2, 0))
                SE.activation(gstg[m % 2][:, :], bank[:, :], AF.Silu).then_inc(se.h, 1)
                free_vals.append(se.inc())
                VE.wait_ge(se.h, free_vals[-1])
                if m == 0:
                    VE.wait_ge(qz1l[c % 2].h, z1_vals[c])
                VE.wait_ge(qzs[m % 2].h, qzs[m % 2].n)
                VE.tensor_mul(zstg[m % 2][:, :],
                              z1buf[c % 2][:, m * T:(m + 1) * T],
                              gstg[m % 2][:, :]).then_inc(ve.h, 1)
                zv = ve.inc()
                z_of[c * NCH + m] = zv
                if m == NCH - 1:
                    z_last[c] = zv
                SY.wait_ge(ve.h, zv)
                SY.dma_start(z_s[m * 128:(m + 1) * 128, c * T:(c + 1) * T],
                             zstg[m % 2][:, :]).then_inc(qzs[m % 2].h, 16)
                qzs[m % 2].inc(16)

            ld_vals = [pre_d[0], pre_d[1]]
            z1_vals[0] = load_fm_chunk(z1buf[0], z1_s, 0, qz1l[0])
            for c in range(NC_OWN):
                if c + 1 < NC_OWN:
                    dep2 = (ve, z_last[c - 1]) if c >= 1 else None
                    z1_vals[c + 1] = load_fm_chunk(z1buf[(c + 1) % 2], z1_s,
                                                   c + 1, qz1l[(c + 1) % 2],
                                                   dep=dep2)
                TE.wait_ge(ld_vals[c][0].h, ld_vals[c][1])
                vals = emit_chunk_groups(c, xnbuf[c % 2], te, free_vals, se,
                                         d_evac)
                grp_last[c] = vals[-1]
                zs_after[c] = (qzs[0].n, qzs[1].n)
                if c + 2 < NC_OWN:
                    ld_vals.append((qxn[c % 2],
                                    load_fm_chunk(xnbuf[c % 2], xnbf_s,
                                                  NC_ALL - NC_OWN + c + 2,
                                                  qxn[c % 2],
                                                  dep=(te, grp_last[c]))))
            SY.wait_ge(te.h, grp_last[NC_OWN - 1])
            load_weight_bf(2, qg_w[3])
            SY.wait_ge(qzs[0].h, zs_after[0][0])
            SY.wait_ge(qzs[1].h, zs_after[0][1])
            pre_e[0] = (qpre[0], load_fm_chunk(
                xnbuf[0], z_s, 0, qpre[0],
                dep=(te, grp_last[NC_OWN - 2])))
            SY.wait_ge(qzs[0].h, zs_after[1][0])
            SY.wait_ge(qzs[1].h, zs_after[1][1])
            pre_e[1] = (qpre[1], load_fm_chunk(
                xnbuf[1], z_s, 1, qpre[1],
                dep=(te, grp_last[NC_OWN - 1])))
            barrier(queues=[(SY, qxn[0]), (SY, qxn[1]), (SY, qz1l[0]),
                            (SY, qz1l[1]), (SY, qzs[0]), (SY, qzs[1]),
                            (SY, qg)])

        if _STOP == "D":
            return
        qg = qg_w[3]
        qzl = dstream("qzl_e")
        qxe = dstream("qxe_e")
        qpar = dstream("qpar_e")
        te = sem("te_e")
        ve = sem("ve_e")
        with ExitStack() as pe:
            esb = lambda nm, sh, dt_: pe.enter_context(nc.sbuf_tensor(nm, sh, dt_))
            pTR = [pe.enter_context(nc.psum_tensor(f"ptre{i}", [128, T], F32))
                   for i in range(4)]
            xbe = [esb(f"e_x{i}", [128, D], F32) for i in range(2)]
            oste = [esb(f"e_o{i}", [128, D], F32) for i in range(2)]
            attn_sb = v_sb

            TE.wait_ge(qg.h, qg.n)
            free_vals = []
            grp_last = [0] * NC_OWN
            evv = {}
            x_vals = {}
            out_done = {}
            tr_free = {}

            def e_evac(c, m, bank, v):
                VE.wait_ge(te.h, v)
                VE.tensor_copy(attn_sb[:, m * T:(m + 1) * T], bank[:, :]).then_inc(ve.h, 1)
                free_vals.append(ve.inc())
                evv[(c, m)] = free_vals[-1]

            def emit_xe_load(i):
                if i >= 2:
                    SY.wait_ge(ve.h, out_done.get(i - 2, 0))
                SY.dma_start(xbe[i % 2][:, :],
                             xp[PRE + i * 128: PRE + (i + 1) * 128, :]
                             ).then_inc(qxe[i % 2].h, 16)
                x_vals[i] = qxe[i % 2].inc(16)

            ld_vals = [pre_e[0], pre_e[1]]
            emit_xe_load(0)
            emit_xe_load(1)
            for c in range(NC_OWN):
                TE.wait_ge(ld_vals[c][0].h, ld_vals[c][1])
                vals = emit_chunk_groups(c, xnbuf[c % 2], te, free_vals, ve,
                                         e_evac)
                grp_last[c] = vals[-1]
                if c + 2 < NC_OWN:
                    ld_vals.append((qzl[c % 2],
                                    load_fm_chunk(xnbuf[c % 2], z_s, c + 2,
                                                  qzl[c % 2],
                                                  dep=(te, grp_last[c]))))
                for s in range(4):
                    i = c * 4 + s
                    ob = oste[i % 2]
                    if i >= 2:
                        VE.wait_ge(qpar[i % 2].h, qpar[i % 2].n)
                    for mg in range(4):
                        bank = pTR[mg % 4]
                        fkey = (i - 1, mg)
                        if fkey in tr_free:
                            TE.wait_ge(ve.h, tr_free[fkey])
                        for mi in range(4):
                            m = mg * 4 + mi
                            TE.wait_ge(ve.h, evv[(c, m)])
                            _inst = TE.transpose(
                                bank[:, mi * 128:(mi + 1) * 128],
                                attn_sb[:, m * T + s * 128:
                                        m * T + (s + 1) * 128],
                                sid32[:, :])
                        _inst.then_inc(te.h, 1)
                        trv = te.inc()
                        VE.wait_ge(te.h, trv)
                        VE.wait_ge(qxe[i % 2].h, x_vals[i])
                        VE.tensor_add(ob[:, mg * 512:(mg + 1) * 512],
                                      bank[:, :],
                                      xbe[i % 2][:, mg * 512:(mg + 1) * 512]).then_inc(ve.h, 1)
                        tr_free[(i, mg)] = ve.inc()
                    out_done[i] = tr_free[(i, 3)]
                    SY.wait_ge(ve.h, out_done[i])
                    SY.dma_start(par_s[i * 128:(i + 1) * 128, :], ob[:, :]
                                 ).then_inc(qpar[i % 2].h, 16)
                    qpar[i % 2].inc(16)
                    if i + 2 < OWN // 128:
                        emit_xe_load(i + 2)
            barrier(queues=[(SY, qzl[0]), (SY, qzl[1]), (SY, qxe[0]),
                            (SY, qxe[1]), (SY, qpar[0]), (SY, qpar[1]),
                            (SY, qg)])
    _emit_bcde()
    ws.close()

    if _STOP in ("B", "C", "D", "E"):
        emit_dbg_dumps()
        es.close()
        return nc

    DR = mybir.MatmulPerfMode.DoubleRow
    NJC = NCH // 2
    with ExitStack() as ph:
        fsb = lambda nm, sh, dt_: ph.enter_context(nc.sbuf_tensor(nm, sh, dt_))
        pFF = [ph.enter_context(nc.psum_tensor(f"pff{i}", [128, 1024], F32))
               for i in range(4)]
        xn8 = fsb("xn8", [128, NCH * OWN], F8)
        wg8a = [fsb(f"wg8a{i}", [128, NCH * 128], F8) for i in range(2)]
        wg8b = [fsb(f"wg8b{i}", [128, NCH * 128], F8) for i in range(2)]
        W2 = 1024
        ffg = [fsb(f"ffg{i}", [128, W2], BF16) for i in range(2)]
        fgel = [fsb(f"fgel{i}", [128, W2], BF16) for i in range(2)]
        gt1 = [fsb(f"gt1_{i}", [128, W2], BF16) for i in range(2)]
        gt2 = [fsb(f"gt2_{i}", [128, W2], BF16) for i in range(2)]
        ff1 = [fsb(f"ff1{i}", [128, W2], BF16) for i in range(2)]
        factb = [fsb(f"factb{i}", [128, W2], BF16) for i in range(2)]
        fact8 = [fsb(f"fact8{i}", [128, W2], F8) for i in range(2)]
        wlb = [fsb(f"wlb{i}", [128, D], F8) for i in range(2)]

        qx8 = sem("qx8_f1")
        qwg = dstream("qwg_f1")
        qa8 = dstream("qa8_f1")
        qwl = dstream("qwl_f1")
        qws = dstream("qws_f1")
        te = sem("te_f1")
        ve = sem("ve_f1")
        se = sem("se_f1")

        xn8_ps = xn8.ap().ap[0][0]
        for jc in range(NJC):
            src_ap = bass.AP(xn8_s.tensor, 256 * jc * OWN,
                             [[2 * OWN, 128], [OWN, 2], [1, OWN]])
            SY.dma_start(xn8[:, jc * 2 * OWN:(jc + 1) * 2 * OWN], src_ap
                         ).then_inc(qx8.h, 16)
            qx8.inc(16)

        def xn8_rhs(jc, tc):
            return bass.AP(xn8.ap().tensor,
                           xn8.ap().offset + jc * 2 * OWN + tc * T,
                           [[xn8_ps, 128], [OWN, 2], [1, T]])

        hc_grp_last = [0] * NHC

        def load_wg8(hc, dep_te):
            bi = hc % 2
            if dep_te:
                GE.wait_ge(te.h, dep_te)
            for j, wbuf in ((0, wg8a), (1, wg8b)):
                for i in range(2):
                    src_ap = bass.AP(wgt.tensor,
                                     j * D * HID + i * HID + hc * 128,
                                     [[2 * HID, 128], [256 * HID, NJC],
                                      [1, 128]])
                    dst_ap = bass.AP(wbuf[bi].ap().tensor,
                                     wbuf[bi].ap().offset + i * 128,
                                     [[wbuf[bi].ap().ap[0][0], 128],
                                      [256, NJC], [1, 128]])
                    GE.dma_start(dst_ap, src_ap).then_inc(qwg[bi].h, 16)
                    qwg[bi].inc(16)
            return qwg[bi].n

        def wg_lhsT(wbuf, jc):
            return bass.AP(wbuf.ap().tensor,
                           wbuf.ap().offset + jc * 256,
                           [[wbuf.ap().ap[0][0], 128], [128, 2], [1, 128]])

        wg_vals = [load_wg8(0, 0)]
        chain_used = [0, 0]
        pff_free = [(None, 0)] * 4
        TE.wait_ge(qx8.h, qx8.n)
        for hc in range(NHC):
            if hc + 1 < NHC:
                wg_vals.append(load_wg8(hc + 1,
                                        hc_grp_last[hc - 1] if hc >= 1 else 0))
            p = hc % 2
            GE.wait_ge(qws[p].h, qws[p].n)
            GE.dma_start(wlb[p][:, :], wlin[hc * 128:(hc + 1) * 128, :]
                         ).then_inc(qwl[p].h, 16)
            qwl[p].inc(16)
            GE.wait_ge(qwl[p].h, qwl[p].n)
            GE.dma_start(wl8_s[hc * 128:(hc + 1) * 128, :], wlb[p][:, :]
                         ).then_inc(qws[p].h, 16)
            qws[p].inc(16)

            TE.wait_ge(qwg[hc % 2].h, wg_vals[hc])
            for tcp in range(NC_OWN // 2):
                pp = tcp % 2
                pg = pFF[pp]
                pf = pFF[2 + pp]
                for bank, bank_i in ((pg, pp), (pf, 2 + pp)):
                    ctr, v = pff_free[bank_i]
                    if ctr is not None:
                        TE.wait_ge(ctr.h, v)
                for half in range(2):
                    tc = 2 * tcp + half
                    for bank, wbuf in ((pg, wg8a), (pf, wg8b)):
                        for jc in range(NJC):
                            _inst = TE.matmul(
                                bank[:, half * T:(half + 1) * T],
                                wg_lhsT(wbuf[hc % 2], jc),
                                xn8_rhs(jc, tc),
                                start=(jc == 0), stop=(jc == NJC - 1),
                                perf_mode=DR)
                _inst.then_inc(te.h, 1)
                grp_v = te.inc()
                hc_grp_last[hc] = grp_v
                SE.wait_ge(te.h, grp_v)
                SE.wait_ge(ve.h, chain_used[pp])
                SE.copy(ffg[pp][:, :], pg[:, :]).then_inc(se.h, 1)
                ffg_v = se.inc()
                pff_free[pp] = (se, ffg_v)
                SE.copy(ff1[pp][:, :], pf[:, :]).then_inc(se.h, 1)
                ff1_v = se.inc()
                pff_free[2 + pp] = (se, ff1_v)
                VE.wait_ge(se.h, ffg_v)
                VE.wait_ge(ve.h, chain_used[pp])
                VE.tensor_mul(gt1[pp][:, :], ffg[pp][:, :], ffg[pp][:, :])
                VE.tensor_mul(gt1[pp][:, :], gt1[pp][:, :], ffg[pp][:, :])
                VE.tensor_scalar_mul(gt1[pp][:, :], gt1[pp][:, :], C2BF)
                VE.tensor_add(gt1[pp][:, :], ffg[pp][:, :], gt1[pp][:, :])
                VE.tensor_scalar_mul(gt1[pp][:, :], gt1[pp][:, :], C1BF).then_inc(ve.h, 1)
                g5_v = ve.inc()
                SE.wait_ge(ve.h, g5_v)
                SE.activation(gt2[pp][:, :], gt1[pp][:, :], AF.Tanh).then_inc(se.h, 1)
                tanh_v = se.inc()
                VE.wait_ge(se.h, tanh_v)
                VE.tensor_scalar(gt2[pp][:, :], gt2[pp][:, :], 0.5, 0.5,
                                 op0=OP.mult, op1=OP.add)
                VE.tensor_mul(fgel[pp][:, :], ffg[pp][:, :], gt2[pp][:, :])
                VE.wait_ge(se.h, ff1_v)
                VE.tensor_mul(factb[pp][:, :], fgel[pp][:, :], ff1[pp][:, :])
                VE.wait_ge(qa8[pp].h, qa8[pp].n)
                VE.tensor_copy(fact8[pp][:, :], factb[pp][:, :]).then_inc(ve.h, 1)
                a8v = ve.inc()
                chain_used[pp] = a8v
                SY.wait_ge(ve.h, a8v)
                SY.dma_start(
                    act8_s[hc * 128:(hc + 1) * 128,
                           2 * tcp * T:(2 * tcp + 2) * T],
                    fact8[pp][:, :]).then_inc(qa8[pp].h, 16)
                qa8[pp].inc(16)
        barrier(queues=[(SY, qx8), (SY, qa8[0]), (SY, qa8[1]),
                        (GE, qwg[0]), (GE, qwg[1]), (GE, qwl[0]),
                        (GE, qwl[1]), (GE, qws[0]), (GE, qws[1])])

    if _STOP == "F1":
        emit_dbg_dumps()
        es.close()
        return nc

    with ExitStack() as ph:
        fsb = lambda nm, sh, dt_: ph.enter_context(nc.sbuf_tensor(nm, sh, dt_))
        banks8 = [ph.enter_context(nc.psum_tensor(f"pex{i}", [128, T], F32))
                  for i in range(8)]
        a8 = [[fsb(f"a8_{p}_{i}", [128, (NHC // 2) * T], F8) for i in range(2)]
              for p in range(2)]
        wl2 = [fsb(f"wl2_{i}", [128, 2048], F8) for i in range(4)]
        ffsl = fsb("ffsl", [128, NCH * T], BF16)
        outtm = [fsb(f"outtm{i}", [128, D], BF16) for i in range(2)]
        parb = [fsb(f"parb{i}", [128, D], F32) for i in range(2)]
        outb = [fsb(f"outb{i}", [128, D], F32) for i in range(2)]

        qa8 = dstream("qa8_f2")
        qwl2 = dstream("qwl2_f2", 4)
        qpar2 = dstream("qpar_f2")
        qout = dstream("qout_f2")
        qfT = dstream("qfT_f2")
        qtt = dstream("qtt_f2")
        te = sem("te_f2")
        ve = sem("ve_f2")

        NJC2 = NHC // 2
        def load_a8(tc, dep_te):
            p = tc % 2
            if dep_te:
                GE.wait_ge(te.h, dep_te)
            for jc in range(NJC2):
                half = jc // (NJC2 // 2)
                jj = jc % (NJC2 // 2)
                buf = a8[p][half]
                ps = buf.ap().ap[0][0]
                for i in range(2):
                    src_ap = bass.AP(act8_s.tensor,
                                     (256 * jc + i) * OWN + tc * T,
                                     [[2 * OWN, 128], [1, T]])
                    dst_ap = bass.AP(buf.ap().tensor,
                                     buf.ap().offset + jj * 2 * T + i * T,
                                     [[ps, 128], [1, T]])
                    GE.dma_start(dst_ap, src_ap).then_inc(qa8[p].h, 16)
                    qa8[p].inc(16)
            return qa8[p].n

        def a8_rhs(tc, jc):
            half = jc // (NJC2 // 2)
            jj = jc % (NJC2 // 2)
            buf = a8[tc % 2][half]
            return bass.AP(buf.ap().tensor,
                           buf.ap().offset + jj * 2 * T,
                           [[buf.ap().ap[0][0], 128], [T, 2], [1, T]])

        a8_vals = [load_a8(0, 0)]
        par_vals = {}
        tr_free2 = {}
        bank_busy = [0] * 8
        mm_last = 0
        wl_buf_user = [0, 0, 0, 0]
        wl_vals = {}

        SEr = nc.scalar

        def emit_par_load(i):
            SEr.wait_ge(ve.h, tr_free2.get((i - 2, 0), 0))
            SEr.dma_start(parb[i % 2][:, :], par_s[i * 128:(i + 1) * 128, :]
                          ).then_inc(qpar2[i % 2].h, 16)
            par_vals[i] = qpar2[i % 2].inc(16)

        emit_par_load(0)
        emit_par_load(1)

        def load_wl2(key, buf):
            tck, mh, jc = key
            SY.wait_ge(te.h, wl_buf_user[buf])
            ps = wl2[buf].ap().ap[0][0]
            for i in range(2):
                src_ap = bass.AP(wl8_s.tensor,
                                 (256 * jc + i) * D + mh * 1024,
                                 [[2 * D, 128], [1, 1024]])
                dst_ap = bass.AP(wl2[buf].ap().tensor,
                                 wl2[buf].ap().offset + i * 1024,
                                 [[ps, 128], [1, 1024]])
                SY.dma_start(dst_ap, src_ap).then_inc(qwl2[buf].h, 16)
                qwl2[buf].inc(16)
            wl_vals[key] = qwl2[buf].n

        def wl2_lhsT(buf, mi):
            return bass.AP(wl2[buf].ap().tensor,
                           wl2[buf].ap().offset + mi * 128,
                           [[wl2[buf].ap().ap[0][0], 128], [1024, 2],
                            [1, 128]])

        all_keys = [(tc_, mh_, jc_) for tc_ in range(NC_OWN)
                    for mh_ in range(2) for jc_ in range(NJC2)]
        for idx0 in range(3):
            load_wl2(all_keys[idx0], idx0 % 4)
        gstep = 0
        for tc in range(NC_OWN):
            p = tc % 2
            TE.wait_ge(qa8[p].h, a8_vals[tc])
            if tc + 1 < NC_OWN:
                a8_vals.append(load_a8(tc + 1, mm_last))
            evac_v = {}
            for mh in range(2):
                for jc in range(NJC2):
                    buf = gstep % 4
                    if gstep + 3 < len(all_keys):
                        load_wl2(all_keys[gstep + 3], (gstep + 3) % 4)
                    gstep += 1
                    TE.wait_ge(qwl2[buf].h, wl_vals[(tc, mh, jc)])
                    if jc == 0:
                        for mi in range(8):
                            if bank_busy[mi]:
                                TE.wait_ge(ve.h, bank_busy[mi])
                    rhs = a8_rhs(tc, jc)
                    for mi in range(8):
                        _inst = TE.matmul(
                            banks8[mi][:, :],
                            wl2_lhsT(buf, mi),
                            rhs,
                            start=(jc == 0), stop=(jc == NJC2 - 1),
                            perf_mode=mybir.MatmulPerfMode.DoubleRow)
                    _inst.then_inc(te.h, 1)
                    mm_last = te.inc()
                    wl_buf_user[buf] = mm_last
                for mi in range(8):
                    m = mh * 8 + mi
                    VE.wait_ge(te.h, mm_last)
                    if tc > 0 and mi == 0:
                        VE.wait_ge(qfT[0].h, qfT[0].n)
                        VE.wait_ge(qfT[1].h, qfT[1].n)
                    VE.tensor_copy(ffsl[:, m * T:(m + 1) * T],
                                   banks8[mi][:, :]).then_inc(ve.h, 1)
                    evac_v[m] = ve.inc()
                    bank_busy[mi] = evac_v[m]
                    SY.wait_ge(ve.h, evac_v[m])
                    SY.dma_start(
                        ffnT_s[m * 128:(m + 1) * 128, tc * T:(tc + 1) * T],
                        ffsl[:, m * T:(m + 1) * T]).then_inc(qfT[m % 2].h, 16)
                    qfT[m % 2].inc(16)
            fT_done = (qfT[0].n, qfT[1].n)
            for s in range(4):
                i = tc * 4 + s
                ob = outb[i % 2]
                SEr.wait_ge(ve.h, tr_free2.get((i - 2, 0), 0))
                SEr.wait_ge(qfT[0].h, fT_done[0])
                SEr.wait_ge(qfT[1].h, fT_done[1])
                SEr.dma_start_transpose(
                    outtm[i % 2][:, :],
                    ffnT_s[:, i * 128:(i + 1) * 128]).then_inc(qtt[i % 2].h, 16)
                qtt[i % 2].inc(16)
                if i >= 2:
                    VE.wait_ge(qout[i % 2].h, qout[i % 2].n)
                VE.wait_ge(qtt[i % 2].h, qtt[i % 2].n)
                VE.wait_ge(qpar2[i % 2].h, par_vals[i])
                VE.tensor_add(ob[:, :], outtm[i % 2][:, :],
                              parb[i % 2][:, :]).then_inc(ve.h, 1)
                tr_free2[(i, 0)] = ve.inc()
                SEr.wait_ge(ve.h, tr_free2[(i, 0)])
                SEr.dma_start(out[i * 128:(i + 1) * 128, :], ob[:, :]
                              ).then_inc(qout[i % 2].h, 16)
                qout[i % 2].inc(16)
                if i + 2 < OWN // 128:
                    emit_par_load(i + 2)
        SEr.wait_ge(qout[0].h, qout[0].n)
        SEr.wait_ge(qout[1].h, qout[1].n)
        SY.wait_ge(qfT[0].h, qfT[0].n)
        SY.wait_ge(qfT[1].h, qfT[1].n)
        GE.wait_ge(qa8[0].h, qa8[0].n)
        GE.wait_ge(qa8[1].h, qa8[1].n)
        emit_dbg_dumps()

    es.close()
    return nc


_NC_CACHE = None


def _get_nc():
    global _NC_CACHE
    if _NC_CACHE is None:
        _NC_CACHE = build()
    return _NC_CACHE


def _make_in_maps(x, Wq, Wv, Wg, Wo, beta, ln_g, ln_b, wgt_, wlin_):
    import ml_dtypes
    beta = np.ascontiguousarray(
        np.broadcast_to(beta.reshape(1, H), (128, H)))
    ln_g = np.ascontiguousarray(ln_g.reshape(NCH, 128).T)
    ln_b = np.ascontiguousarray(ln_b.reshape(NCH, 128).T)
    cid32 = np.eye(128, dtype=np.float32)
    cid16 = np.eye(128, dtype=ml_dtypes.bfloat16)
    conc = np.ones((128, 1), np.float32)
    conr = np.ones((1, 128), np.float32)
    in_maps = []
    for c in range(8):
        b, h = c % 4, c // 4
        xpv = np.zeros((NT, D), np.float32)
        if h == 0:
            xpv[PRE:] = x[b, 0:OWN]
        else:
            xpv[:] = x[b]
        in_maps.append({
            "xp": xpv, "wq": Wq, "wv": Wv, "wg": Wg, "wo": Wo,
            "beta": beta, "lng": ln_g, "lnb": ln_b,
            "wgt": wgt_, "wlin": wlin_,
            "cid32": cid32, "cid16": np.asarray(cid16),
            "conc": conc, "conr": conr,
        })
    return in_maps


def kernel(L0_attn_x, L0_attn_Wq, L0_attn_Wv, L0_attn_Wg, L0_attn_Wo,
           L0_attn_beta, L0_attn_ln_g, L0_attn_ln_b,
           L0_ffn_w_gating, L0_ffn_w_linear):
    x = np.ascontiguousarray(np.asarray(L0_attn_x, dtype=np.float32))
    Wq = np.ascontiguousarray(np.asarray(L0_attn_Wq, dtype=np.float32))
    Wv = np.ascontiguousarray(np.asarray(L0_attn_Wv, dtype=np.float32))
    Wg = np.ascontiguousarray(np.asarray(L0_attn_Wg, dtype=np.float32))
    Wo = np.ascontiguousarray(np.asarray(L0_attn_Wo, dtype=np.float32))
    beta = np.asarray(L0_attn_beta, dtype=np.float32)
    ln_g = np.asarray(L0_attn_ln_g, dtype=np.float32)
    ln_b = np.asarray(L0_attn_ln_b, dtype=np.float32)
    wgt_ = np.ascontiguousarray(np.asarray(L0_ffn_w_gating, dtype=np.float32))
    wlin_ = np.ascontiguousarray(np.asarray(L0_ffn_w_linear, dtype=np.float32))

    nc = _get_nc()
    in_maps = _make_in_maps(x, Wq, Wv, Wg, Wo, beta, ln_g, ln_b, wgt_, wlin_)
    res = run_bass_kernel_spmd(nc, in_maps, core_ids=list(range(8)))
    outp = np.empty((B_, S_, D), np.float32)
    for c in range(8):
        b, h = c % 4, c // 4
        outp[b, h * OWN:(h + 1) * OWN] = res.results[c]["out"]
    return outp


# version 15
# speedup vs baseline: 1.0464x; 1.0438x over previous
import numpy as np
from contextlib import ExitStack

import concourse.bass as bass
import concourse.mybir as mybir
from concourse.bass_utils import run_bass_kernel_spmd

F32 = mybir.dt.float32
BF16 = mybir.dt.bfloat16
F8 = mybir.dt.float8e4
AF = mybir.ActivationFunctionType
OP = mybir.AluOpType
AXX = mybir.AxisListType.X

B_, S_, D = 4, 8192, 2048
H, DH = 16, 128
NCH = D // 128
OWN = 4096
PRE = 4096
NT = PRE + OWN
T = 512
NC_ALL = NT // T
NC_OWN = OWN // T
HID = 8192
NHC = HID // 128
EPS = 1e-5
INV_D = 1.0 / D

C1BF = 0.796875
C2BF = 0.044677734375

_DBG = False
_STOP = None
_A_GROUPS = None
_A_MODE = "full"


class Ctr:
    __slots__ = ("h", "n")

    def __init__(self, h):
        self.h = h
        self.n = 0

    def inc(self, k=1):
        self.n += k
        return self.n


def build():
    nc = bass.Bass("TRN2", target_bir_lowering=False)

    xp = nc.dram_tensor("xp", [NT, D], F32, kind="ExternalInput").ap()
    wq = nc.dram_tensor("wq", [D, D], F32, kind="ExternalInput").ap()
    wv = nc.dram_tensor("wv", [D, D], F32, kind="ExternalInput").ap()
    wg = nc.dram_tensor("wg", [D, D], F32, kind="ExternalInput").ap()
    wo = nc.dram_tensor("wo", [D, D], F32, kind="ExternalInput").ap()
    beta = nc.dram_tensor("beta", [128, H], F32, kind="ExternalInput").ap()
    lng = nc.dram_tensor("lng", [128, NCH], F32, kind="ExternalInput").ap()
    lnb = nc.dram_tensor("lnb", [128, NCH], F32, kind="ExternalInput").ap()
    wgt = nc.dram_tensor("wgt", [2, D, HID], F32, kind="ExternalInput").ap()
    wlin = nc.dram_tensor("wlin", [HID, D], F32, kind="ExternalInput").ap()
    cid32 = nc.dram_tensor("cid32", [128, 128], F32, kind="ExternalInput").ap()
    cid16 = nc.dram_tensor("cid16", [128, 128], BF16, kind="ExternalInput").ap()
    conc = nc.dram_tensor("conc", [128, 1], F32, kind="ExternalInput").ap()
    conr = nc.dram_tensor("conr", [1, 128], F32, kind="ExternalInput").ap()
    out = nc.dram_tensor("out", [OWN, D], F32, kind="ExternalOutput").ap()

    xnbf_s = nc.dram_tensor("xnbf_s", [D, NT], BF16, kind="Internal").ap()
    xn8_s = nc.dram_tensor("xn8_s", [D, OWN], F8, kind="Internal").ap()
    st_s = nc.dram_tensor("st_s", [D, OWN], BF16, kind="Internal").ap()
    z1_s = nc.dram_tensor("z1_s", [D, OWN], BF16, kind="Internal").ap()
    z_s = nc.dram_tensor("z_s", [D, OWN], BF16, kind="Internal").ap()
    act8_s = nc.dram_tensor("act8_s", [HID, OWN], F8, kind="Internal").ap()
    par_s = nc.dram_tensor("par_s", [OWN, D], F32, kind="Internal").ap()
    wl8_s = nc.dram_tensor("wl8_s", [HID, D], F8, kind="Internal").ap()
    ffnT_s = nc.dram_tensor("ffnT_s", [D, OWN], BF16, kind="Internal").ap()
    wbf_s = nc.dram_tensor("wbf_s", [3 * D, D], BF16, kind="Internal").ap()

    dbg = {}
    if _DBG:
        for nm, sh, dt_ in [
            ("d_xnbf", [D, NT], BF16), ("d_xn8", [D, OWN], F8),
            ("d_st", [D, OWN], BF16), ("d_z1", [D, OWN], BF16),
            ("d_z", [D, OWN], BF16), ("d_par", [OWN, D], F32),
            ("d_act8", [HID, OWN], F8), ("d_lam", [128, H], F32),
        ]:
            dbg[nm] = nc.dram_tensor(nm, sh, dt_, kind="ExternalOutput").ap()

    class _EngWrap:

        def __init__(self, eng):
            self._e = eng
            self._w = {}
            self._r = {}

        @staticmethod
        def _iv(ap):
            lo = ap.offset
            hi = lo + 1
            for step, count in list(ap.ap)[1:]:
                hi += step * (count - 1)
            return (ap.tensor.name, lo, hi)

        @staticmethod
        def _hit(iv, table):
            spans = table.get(iv[0])
            if not spans:
                return False
            for lo, hi in spans:
                if iv[1] < hi and lo < iv[2]:
                    return True
            return False

        @staticmethod
        def _add(iv, table):
            table.setdefault(iv[0], []).append((iv[1], iv[2]))

        def drain(self, *a, **k):
            self._w.clear()
            self._r.clear()
            return self._e.drain(*a, **k)

        def __getattr__(self, name):
            attr = getattr(self._e, name)
            if name in ("wait_ge", "sem_inc", "dma_start"):
                return attr
            if not callable(attr):
                return attr

            def call(*args, **kwargs):
                aps = [a for a in args if isinstance(a, bass.AP)]
                outs = aps[:1]
                ins = aps[1:]
                for kwn, kv in kwargs.items():
                    if isinstance(kv, bass.AP):
                        (outs if kwn == "accum_out" else ins).append(kv)
                ivs_o = [self._iv(a) for a in outs]
                ivs_i = [self._iv(a) for a in ins]
                if (any(self._hit(iv, self._w) for iv in ivs_i)
                        or any(self._hit(iv, self._w) or self._hit(iv, self._r)
                               for iv in ivs_o)):
                    self._e.drain()
                    self._w.clear()
                    self._r.clear()
                for iv in ivs_o:
                    self._add(iv, self._w)
                for iv in ivs_i:
                    self._add(iv, self._r)
                return attr(*args, **kwargs)

            return call

    TE, GE, SY = nc.tensor, nc.gpsimd, nc.sync
    VE = _EngWrap(nc.vector)
    SE = _EngWrap(nc.scalar)
    GP = _EngWrap(nc.gpsimd)
    SEr = nc.scalar

    es = ExitStack()

    def sbuf(nm, sh, dt_):
        return es.enter_context(nc.sbuf_tensor(nm, sh, dt_))

    def psum(nm, sh, dt_):
        return es.enter_context(nc.psum_tensor(nm, sh, dt_))

    def sem(nm):
        return Ctr(es.enter_context(nc.semaphore(nm)))

    def dstream(nm, n=2):
        return [sem(f"{nm}{i}") for i in range(n)]

    sid32 = sbuf("sid32", [128, 128], F32)
    sid16 = sbuf("sid16", [128, 128], BF16)
    sonc = sbuf("sonc", [128, 1], F32)
    sonr = sbuf("sonr", [1, 128], F32)
    sbeta = sbuf("sbeta", [128, H], F32)
    slam = sbuf("slam", [128, H], F32)
    slng = sbuf("slng", [128, NCH], F32)
    slnb = sbuf("slnb", [128, NCH], F32)
    scarry = sbuf("scarry", [128, H], F32)


    bar = sem("bar")
    ALL_ENG = (TE, VE, SE, GE, SY)

    def barrier(queues=()):
        for eng, q in queues:
            eng.wait_ge(q.h, q.n)
        for eng in (TE, VE, SE):
            eng.drain()
        for eng in ALL_ENG:
            eng.sem_inc(bar.h, 1)
        bar.inc(5)
        for eng in ALL_ENG:
            eng.wait_ge(bar.h, bar.n)

    def emit_dbg_dumps():
        if not _DBG:
            return
        barrier()
        qd = sem("qs_dbg")
        for nm, src_ in [("d_xnbf", xnbf_s), ("d_xn8", xn8_s),
                         ("d_st", st_s), ("d_z1", z1_s), ("d_z", z_s),
                         ("d_par", par_s), ("d_act8", act8_s)]:
            SY.dma_start(dbg[nm], src_).then_inc(qd.h, 16)
            qd.inc(16)
        SY.wait_ge(qd.h, qd.n)

    def lam_bcast(hd, n):
        col = slam[:, hd:hd + 1]
        return bass.AP(col.tensor, col.offset, [[col.ap[0][0], 128], [0, n]])

    qgp = sem("qg_prep")
    for dst, src in [(sid32, cid32), (sid16, cid16), (sonc, conc),
                     (sonr, conr), (sbeta, beta), (slng, lng), (slnb, lnb)]:
        GE.dma_start(dst[:, :], src).then_inc(qgp.h, 16)
        qgp.inc(16)
    SE.wait_ge(qgp.h, qgp.n)
    SE.activation(slam[:, :], sbeta[:, :], AF.Sigmoid)
    if _DBG:
        sep = sem("se_prep")
        SE.drain()
        nc.scalar.sem_inc(sep.h, 1)
        sep.inc()
        qsp = sem("qs_prep")
        SY.wait_ge(sep.h, sep.n)
        SY.dma_start(dbg["d_lam"], slam[:, :]).then_inc(qsp.h, 16)
        qsp.inc(16)
        SY.wait_ge(qsp.h, qsp.n)
    barrier(queues=[(GE, qgp)])

    if _STOP == "P":
        es.close()
        return nc

    ws = ExitStack()
    w_sb = ws.enter_context(nc.sbuf_tensor("w_sb", [128, NCH * D], BF16))
    pMM = [ws.enter_context(nc.psum_tensor(f"pmm{i}", [128, T], F32))
           for i in range(4)]
    qg_w = [sem(f"qg_w{i}") for i in range(4)]

    def load_weight(wap, qgc):
        for k in range(NCH):
            GE.dma_start(w_sb[:, k * D:(k + 1) * D],
                         wap[k * 128:(k + 1) * 128, :]).then_inc(qgc.h, 16)
            qgc.inc(16)

    def load_weight_bf(wi, qgc):
        for k in range(NCH):
            SEr.dma_start(w_sb[:, k * D:(k + 1) * D],
                          wbf_s[(wi * NCH + k) * 128:(wi * NCH + k + 1) * 128, :]
                          ).then_inc(qgc.h, 16)
            qgc.inc(16)

    load_weight(wv, qg_w[0])

    with ExitStack() as ph:
        asb = lambda nm, sh, dt_: ph.enter_context(nc.sbuf_tensor(nm, sh, dt_))
        xbuf = [asb(f"a_x{i}", [128, D], F32) for i in range(8)]
        xntm = [asb(f"a_xn{i}", [128, D], F32) for i in range(8)]
        stt_t = asb("a_stt", [128, 48], F32)
        stgbf = [asb(f"a_gbf{i}", [128, T], BF16) for i in range(2)]
        stg8 = [asb(f"a_g8{i}", [128, T], F8) for i in range(2)]
        wcb = [asb(f"a_wcb{i}", [128, D], BF16) for i in range(2)]
        pTR = [ph.enter_context(nc.psum_tensor(f"ptra{i}", [128, T], F32))
               for i in range(2)] + pMM[0:2]

        qwc = dstream("qwc_a")
        qwps = dstream("qwps_a")
        for wi, wap in enumerate((wq, wg, wo)):
            for k in range(NCH):
                p = k % 2
                if wi or k >= 2:
                    GE.wait_ge(qwps[p].h, qwps[p].n)
                GE.dma_start(wcb[p][:, :], wap[k * 128:(k + 1) * 128, :]
                             ).then_inc(qwc[p].h, 16)
                qwc[p].inc(16)
                GE.wait_ge(qwc[p].h, qwc[p].n)
                GE.dma_start(
                    wbf_s[(wi * NCH + k) * 128:(wi * NCH + k + 1) * 128, :],
                    wcb[p][:, :]).then_inc(qwps[p].h, 16)
                qwps[p].inc(16)

        qx = dstream("qx_a", 8)
        qbf = dstream("qbf_a")
        q8 = dstream("q8_a")
        ve = sem("ve_a")
        se = sem("se_a")
        te = sem("te_a")

        NTILE = NT // 128
        ld_done = [0] * NTILE
        xn_done = [0] * NTILE
        grp_tr_done = [0] * NC_ALL
        evac_done = {}

        def wait_xn(eng, i):
            v = xn_done[i]
            if isinstance(v, tuple):
                eng.wait_ge(se.h, v[1])
            else:
                eng.wait_ge(ve.h, v)

        def emit_x_load(i):
            if i >= 8:
                wait_xn(SY, i - 8)
            SY.dma_start(xbuf[i % 8][:, :], xp[i * 128:(i + 1) * 128, :]
                         ).then_inc(qx[i % 8].h, 16)
            ld_done[i] = qx[i % 8].inc(16)

        for i in range(8):
            emit_x_load(i)

        for g in range(NC_ALL):
            own_g = g >= NC_ALL - NC_OWN
            stt = stt_t[:, (g % 2) * 24:(g % 2) * 24 + 24]
            if g >= 2:
                VE.wait_ge(te.h, grp_tr_done[g - 2])
                SE.wait_ge(te.h, grp_tr_done[g - 2])
            for s in range(4):
                i = g * 4 + s
                xb = xbuf[i % 8]
                dmp = xntm[(g % 2) * 4 + s]
                VE.wait_ge(qx[i % 8].h, ld_done[i])
                SE.wait_ge(qx[i % 8].h, ld_done[i])
                if s < 2:
                    VE.reduce_sum(stt[:, s:s + 1], xb[:, :], axis=AXX)
                else:
                    SE.activation(dmp[:, :], xb[:, :], AF.Identity,
                                  accum_out=stt[:, s:s + 1])
                SE.activation(dmp[:, :], xb[:, :], AF.Square,
                              accum_out=stt[:, 4 + s:5 + s])
            SE.drain()
            nc.scalar.sem_inc(se.h, 1)
            sq_v = se.inc()
            VE.wait_ge(se.h, sq_v)
            VE.tensor_scalar_mul(stt[:, 8:12], stt[:, 0:4], -INV_D)
            VE.tensor_scalar_mul(stt[:, 12:16], stt[:, 4:8], INV_D)
            VE.tensor_mul(stt[:, 16:20], stt[:, 8:12], stt[:, 8:12])
            VE.tensor_sub(stt[:, 12:16], stt[:, 12:16], stt[:, 16:20])
            VE.tensor_scalar_add(stt[:, 12:16], stt[:, 12:16], EPS).then_inc(ve.h, 1)
            d_v = ve.inc()
            SE.wait_ge(ve.h, d_v)
            SE.activation(stt[:, 16:20], stt[:, 12:16], AF.Sqrt).then_inc(se.h, 1)
            s0_v = se.inc()
            VE.wait_ge(se.h, s0_v)
            VE.reciprocal(stt[:, 20:24], stt[:, 16:20])
            VE.tensor_mul(stt[:, 16:20], stt[:, 20:24], stt[:, 20:24])
            VE.tensor_mul(stt[:, 16:20], stt[:, 16:20], stt[:, 12:16])
            VE.tensor_scalar(stt[:, 16:20], stt[:, 16:20], -0.5, 1.5,
                             op0=OP.mult, op1=OP.add)
            VE.tensor_mul(stt[:, 20:24], stt[:, 20:24], stt[:, 16:20])
            VE.tensor_mul(stt[:, 12:16], stt[:, 8:12], stt[:, 20:24]).then_inc(ve.h, 1)
            r_v = ve.inc()
            SE.wait_ge(ve.h, r_v)
            for s in range(4):
                i = g * 4 + s
                xb = xbuf[i % 8]
                xt = xntm[(g % 2) * 4 + s]
                if s < 2:
                    VE.tensor_scalar(xt[:, :], xb[:, :], stt[:, 8 + s:9 + s],
                                     stt[:, 20 + s:21 + s],
                                     op0=OP.add, op1=OP.mult).then_inc(ve.h, 1)
                    xn_done[i] = ve.inc()
                else:
                    SE.activation(xt[:, :], xb[:, :], AF.Identity,
                                  bias=stt[:, 12 + s:13 + s],
                                  scale=stt[:, 20 + s:21 + s]).then_inc(se.h, 1)
                    xn_done[i] = ("se", se.inc())
                if i + 8 < NTILE:
                    emit_x_load(i + 8)

            for j in range(NCH):
                bank = pTR[j % 4]
                key = (g, j - 4) if j >= 4 else (g - 1, NCH - 4 + j)
                if key in evac_done:
                    vv, sv = evac_done[key]
                    TE.wait_ge(ve.h, vv)
                    TE.wait_ge(se.h, sv)
                if j == 0:
                    for s4 in range(4):
                        wait_xn(TE, g * 4 + s4)
                for s in range(4):
                    xt = xntm[(g % 2) * 4 + s]
                    _inst = TE.transpose(bank[:, s * 128:(s + 1) * 128],
                                         xt[:, j * 128:(j + 1) * 128],
                                         sid32[:, :])
                _inst.then_inc(te.h, 1)
                tr_v = te.inc()
                if j == NCH - 1:
                    grp_tr_done[g] = tr_v
                stb = stgbf[j % 2]
                st8 = stg8[j % 2]
                VE.wait_ge(te.h, tr_v)
                VE.wait_ge(qbf[j % 2].h, qbf[j % 2].n)
                VE.tensor_copy(stb[:, :], bank[:, :]).then_inc(ve.h, 1)
                bf_v = ve.inc()
                sv_8 = 0
                if own_g:
                    SE.wait_ge(ve.h, bf_v)
                    SE.wait_ge(q8[j % 2].h, q8[j % 2].n)
                    SE.copy(st8[:, :], bank[:, :]).then_inc(se.h, 1)
                    sv_8 = se.inc()
                evac_done[(g, j)] = (bf_v, sv_8)
                SY.wait_ge(ve.h, bf_v)
                SY.dma_start(xnbf_s[j * 128:(j + 1) * 128, g * T:(g + 1) * T],
                             stb[:, :]).then_inc(qbf[j % 2].h, 16)
                qbf[j % 2].inc(16)
                if own_g:
                    go = g - (NC_ALL - NC_OWN)
                    SY.wait_ge(se.h, sv_8)
                    SY.dma_start(
                        xn8_s[j * 128:(j + 1) * 128, go * T:(go + 1) * T],
                        st8[:, :]).then_inc(q8[j % 2].h, 16)
                    q8[j % 2].inc(16)

        barrier(queues=[(SY, q) for q in qx] +
                       [(SY, qbf[0]), (SY, qbf[1]),
                        (SY, q8[0]), (SY, q8[1]),
                        (GE, qwc[0]), (GE, qwc[1]),
                        (GE, qwps[0]), (GE, qwps[1])])

    if _STOP == "A":
        emit_dbg_dumps()
        ws.close()
        es.close()
        return nc

    def _emit_bcde():
      with ExitStack() as ph:
        bsb = lambda nm, sh, dt_: ph.enter_context(nc.sbuf_tensor(nm, sh, dt_))
        xnbuf = [bsb(f"xnb{i}", [128, NCH * T], BF16) for i in range(2)]
        v_sb = bsb("v_sb", [128, NCH * T], F32)
        s2_sb = bsb("s2_sb", [128, NCH * T], F32)

        qpre = dstream("qpre")
        pre_c = {}
        pre_d = {}
        pre_e = {}

        def load_fm_chunk(dst, src, c, qsc, dep=None, eng=SY):
            if dep is not None:
                eng.wait_ge(dep[0].h, dep[1])
            for k in range(NCH):
                eng.dma_start(dst[:, k * T:(k + 1) * T],
                              src[k * 128:(k + 1) * 128, c * T:(c + 1) * T]
                              ).then_inc(qsc.h, 16)
                qsc.inc(16)
            return qsc.n

        def emit_chunk_groups(c, xnb, te_c, free_list, free_sem, emit_evac,
                              hook=None):
            vals = []
            for m in range(NCH):
                if hook is not None:
                    hook(m)
                gi = c * NCH + m
                if gi >= 4:
                    TE.wait_ge(free_sem.h, free_list[gi - 4])
                bank = pMM[gi % 4]
                for k in range(NCH):
                    _inst = TE.matmul(
                        bank[:, :],
                        w_sb[:, k * D + m * 128: k * D + (m + 1) * 128],
                        xnb[:, k * T:(k + 1) * T],
                        start=(k == 0), stop=(k == NCH - 1))
                _inst.then_inc(te_c.h, 1)
                v = te_c.inc()
                vals.append(v)
                emit_evac(c, m, bank, v)
            return vals

        qg = qg_w[0]
        qxn = dstream("qxn_b")
        qst = dstream("qst_b")
        te = sem("te_b")
        ve = sem("ve_b")
        se = sem("se_b")
        b_se_last = [0]
        with ExitStack() as pb:
            stg = [pb.enter_context(nc.sbuf_tensor(f"b_stg{i}", [128, T], BF16))
                   for i in range(2)]
            v_sb2 = pb.enter_context(
                nc.sbuf_tensor("v_sb2", [128, NCH * T], F32))
            vbufs = [v_sb, v_sb2]

            TE.wait_ge(qg.h, qg.n)

            free_vals = []
            grp_last = [0] * NC_ALL

            def b_evac(c, m, bank, v):
                VE.wait_ge(te.h, v)
                VE.tensor_copy(vbufs[c % 2][:, m * T:(m + 1) * T],
                               bank[:, :]).then_inc(ve.h, 1)
                free_vals.append(ve.inc())

            def b_scan(c):
                vb = vbufs[c % 2]
                own_c = c >= NC_ALL - NC_OWN
                scan_v = {}
                if b_se_last[0]:
                    VE.wait_ge(se.h, b_se_last[0])
                for hd in range(H):
                    init = scarry[:, hd:hd + 1] if c > 0 else 0.0
                    VE.tensor_tensor_scan(
                        s2_sb[:, hd * T:(hd + 1) * T], lam_bcast(hd, T),
                        vb[:, hd * T:(hd + 1) * T], init,
                        op0=OP.mult, op1=OP.add).then_inc(ve.h, 1)
                    scan_v[hd] = ve.inc()
                lastcols = bass.AP(s2_sb.ap().tensor,
                                   s2_sb.ap().offset + T - 1,
                                   [[s2_sb.ap().ap[0][0], 128], [T, H]])
                VE.tensor_copy(scarry[:, :], lastcols)
                if own_c:
                    for hd in range(H):
                        SE.wait_ge(ve.h, scan_v[hd])
                        SE.wait_ge(qst[hd % 2].h, qst[hd % 2].n)
                        SE.copy(stg[hd % 2][:, :],
                                s2_sb[:, hd * T:(hd + 1) * T]).then_inc(se.h, 1)
                        sv = se.inc()
                        SY.wait_ge(se.h, sv)
                        co = c - (NC_ALL - NC_OWN)
                        SY.dma_start(
                            st_s[hd * 128:(hd + 1) * 128,
                                 co * T:(co + 1) * T],
                            stg[hd % 2][:, :]).then_inc(qst[hd % 2].h, 16)
                        qst[hd % 2].inc(16)
                    b_se_last[0] = se.n

            ld_vals = [(qxn[0], load_fm_chunk(xnbuf[0], xnbf_s, 0, qxn[0]))]
            for c in range(NC_ALL):
                if c + 1 < NC_ALL:
                    dep = (te, grp_last[c - 1]) if c >= 1 else None
                    ld_vals.append((qxn[(c + 1) % 2],
                                    load_fm_chunk(xnbuf[(c + 1) % 2], xnbf_s,
                                                  c + 1, qxn[(c + 1) % 2],
                                                  dep=dep)))
                TE.wait_ge(ld_vals[c][0].h, ld_vals[c][1])
                vals = emit_chunk_groups(c, xnbuf[c % 2], te, free_vals, ve,
                                         b_evac)
                grp_last[c] = vals[-1]
                if c >= 1:
                    b_scan(c - 1)
            b_scan(NC_ALL - 1)
            SEr.wait_ge(te.h, grp_last[NC_ALL - 1])
            load_weight_bf(0, qg_w[1])
            pre_c[0] = (qpre[0], load_fm_chunk(
                xnbuf[0], xnbf_s, NC_ALL - NC_OWN, qpre[0],
                dep=(te, grp_last[NC_ALL - 2])))
            pre_c[1] = (qpre[1], load_fm_chunk(
                xnbuf[1], xnbf_s, NC_ALL - NC_OWN + 1, qpre[1],
                dep=(te, grp_last[NC_ALL - 1])))
            barrier(queues=[(SY, qxn[0]), (SY, qxn[1]), (SY, qst[0]),
                            (SY, qst[1]), (GE, qg)])

        if _STOP == "B":
            return
        qg = qg_w[1]
        qxn = dstream("qxn_c")
        qstl = dstream("qstl_c")
        qz1 = dstream("qz1_c")
        bc = sem("bc_c")
        te = sem("te_c")
        ve = sem("ve_c")
        se = sem("se_c")
        ge = sem("ge_c")
        with ExitStack() as pc:
            csb = lambda nm, sh, dt_: pc.enter_context(nc.sbuf_tensor(nm, sh, dt_))
            pst1 = pc.enter_context(nc.psum_tensor("pst1", [1, T], F32))
            pst2 = pc.enter_context(nc.psum_tensor("pst2", [1, T], F32))
            pTR = [pc.enter_context(nc.psum_tensor(f"ptrc{i}", [128, T], F32))
                   for i in range(2)]
            stbuf = [csb(f"stb{i}", [128, NCH * T], BF16) for i in range(2)]
            gztmp = csb("gztmp", [128, 1024], F32)
            musigrep = csb("musigrep", [128, T], F32)
            rsigrep = csb("rsigrep", [128, T], F32)
            ztmp = csb("ztmp", [128, T], F32)
            y2t = [csb("y2t0", [128, T], F32), ztmp]
            z1stg = [csb(f"z1s{i}", [128, T], BF16) for i in range(2)]

            ybufs = [v_sb, s2_sb]

            TE.wait_ge(qg.h, qg.n)

            free_vals = []
            grp_last = [0] * NC_OWN
            y_last = [0] * NC_OWN
            st_vals = [0] * NC_OWN
            st2_v = {}
            row_done = [0] * NC_OWN
            z1_done = [0] * NC_OWN
            z1_pair = {}

            def c_evac(c, m, bank, v):
                VE.wait_ge(te.h, v)
                if m == 0:
                    VE.wait_ge(qstl[c % 2].h, st_vals[c])
                if c >= 2:
                    VE.wait_ge(ge.h, z1_pair[(c - 2, m // 2)])
                VE.tensor_tensor(ybufs[c % 2][:, m * T:(m + 1) * T], bank[:, :],
                                 stbuf[c % 2][:, m * T:(m + 1) * T],
                                 op=OP.mult).then_inc(ve.h, 1)
                free_vals.append(ve.inc())
                if m == NCH - 1:
                    y_last[c] = free_vals[-1]

            def c_stats_z1(c):
                y_sb = ybufs[c % 2]
                TE.wait_ge(ve.h, y_last[c])
                if c >= 1:
                    TE.wait_ge(ve.h, row_done[c - 1])
                for m in range(NCH):
                    _inst = TE.matmul(pst1[:, :], sonc[:, :],
                                      y_sb[:, m * T:(m + 1) * T],
                                      start=(m == 0), stop=(m == NCH - 1))
                _inst.then_inc(te.h, 1)
                te.inc()
                for m in range(NCH):
                    if m >= 2:
                        VE.wait_ge(te.h, st2_v[(c, m - 2)])
                    VE.tensor_mul(y2t[m % 2][:, :],
                                  y_sb[:, m * T:(m + 1) * T],
                                  y_sb[:, m * T:(m + 1) * T]).then_inc(ve.h, 1)
                    y2v = ve.inc()
                    TE.wait_ge(ve.h, y2v)
                    _inst = TE.matmul(pst2[:, :], sonc[:, :], y2t[m % 2][:, :],
                                      start=(m == 0), stop=(m == NCH - 1))
                    _inst.then_inc(te.h, 1)
                    st2_v[(c, m)] = te.inc()
                st_mm = st2_v[(c, NCH - 1)]
                VE.wait_ge(te.h, st_mm)
                if c >= 1:
                    VE.wait_ge(ge.h, z1_done[c - 1])
                rowb = pst2[0:1, :]
                rowN = gztmp[0:1, 0:T]
                VE.tensor_scalar_mul(rowN, pst1[0:1, :], -INV_D)
                VE.tensor_scalar_mul(rowb, rowb, INV_D)
                VE.tensor_mul(rsigrep[0:1, 0:T], rowN, rowN)
                VE.tensor_sub(rowb, rowb, rsigrep[0:1, 0:T])
                VE.tensor_scalar_add(rowb, rowb, EPS).then_inc(ve.h, 1)
                d_v = ve.inc()
                SE.wait_ge(ve.h, d_v)
                SE.activation(rsigrep[0:1, 0:T], rowb, AF.Sqrt).then_inc(se.h, 1)
                s0_v = se.inc()
                VE.wait_ge(se.h, s0_v)
                VE.reciprocal(musigrep[0:1, 0:T], rsigrep[0:1, 0:T])
                VE.tensor_mul(rsigrep[0:1, 0:T], musigrep[0:1, 0:T],
                              musigrep[0:1, 0:T])
                VE.tensor_mul(rsigrep[0:1, 0:T], rsigrep[0:1, 0:T],
                              rowb)
                VE.tensor_scalar(rsigrep[0:1, 0:T], rsigrep[0:1, 0:T],
                                 -0.5, 1.5, op0=OP.mult, op1=OP.add)
                VE.tensor_mul(rsigrep[0:1, 0:T], musigrep[0:1, 0:T],
                              rsigrep[0:1, 0:T])
                VE.tensor_mul(musigrep[0:1, 0:T], rowN,
                              rsigrep[0:1, 0:T]).then_inc(ve.h, 1)
                rows_v = ve.inc()
                row_done[c] = rows_v
                VE.wait_ge(bc.h, 2 * (c + 1))
                VE.tensor_copy(musigrep[:, :], pTR[0][:, :])
                VE.tensor_copy(rsigrep[:, :], pTR[1][:, :]).then_inc(ve.h, 1)
                rep_v = ve.inc()
                GP.wait_ge(ve.h, rep_v)
                rs_ps = rsigrep.ap().ap[0][0]
                rs2 = bass.AP(rsigrep.ap().tensor, rsigrep.ap().offset,
                              [[rs_ps, 128], [0, 2], [1, T]])
                mu2b = bass.AP(musigrep.ap().tensor, musigrep.ap().offset,
                               [[musigrep.ap().ap[0][0], 128], [0, 2], [1, T]])
                for mp in range(NCH // 2):
                    GP.tensor_mul(gztmp[:, :],
                                  y_sb[:, 2 * mp * T:(2 * mp + 2) * T], rs2)
                    GP.tensor_add(gztmp[:, :], gztmp[:, :], mu2b).then_inc(ge.h, 1)
                    gv = ge.inc()
                    z1_pair[(c, mp)] = gv
                    for h2 in range(2):
                        m = 2 * mp + h2
                        GP.wait_ge(qz1[m % 2].h, qz1[m % 2].n)
                        GP.tensor_scalar(z1stg[m % 2][:, :],
                                         gztmp[:, h2 * T:(h2 + 1) * T],
                                         slng[:, m:m + 1], slnb[:, m:m + 1],
                                         op0=OP.mult, op1=OP.add).then_inc(ge.h, 1)
                        zv = ge.inc()
                        SY.wait_ge(ge.h, zv)
                        SY.dma_start(
                            z1_s[m * 128:(m + 1) * 128, c * T:(c + 1) * T],
                            z1stg[m % 2][:, :]).then_inc(qz1[m % 2].h, 16)
                        qz1[m % 2].inc(16)
                z1_done[c] = ge.n

            def c_bcast(c):
                TE.wait_ge(ve.h, row_done[c])
                TE.matmul(pTR[0][:, :], sonr[:, :], musigrep[0:1, 0:T],
                          start=True, stop=True).then_inc(bc.h, 1)
                bc.inc()
                TE.matmul(pTR[1][:, :], sonr[:, :], rsigrep[0:1, 0:T],
                          start=True, stop=True).then_inc(bc.h, 1)
                bc.inc()

            ld_vals = [pre_c[0], pre_c[1]]
            st_vals[0] = load_fm_chunk(stbuf[0], st_s, 0, qstl[0], eng=SEr)
            for c in range(NC_OWN):
                if c + 1 < NC_OWN:
                    dep2 = (ve, y_last[c - 1]) if c >= 1 else None
                    st_vals[c + 1] = load_fm_chunk(stbuf[(c + 1) % 2], st_s,
                                                   c + 1, qstl[(c + 1) % 2],
                                                   dep=dep2, eng=SEr)
                TE.wait_ge(ld_vals[c][0].h, ld_vals[c][1])
                hook = None
                if c >= 2:
                    def hook(m, _cc=c - 2):
                        if m == 4:
                            c_bcast(_cc)
                vals = emit_chunk_groups(c, xnbuf[c % 2], te, free_vals, ve,
                                         c_evac, hook=hook)
                grp_last[c] = vals[-1]
                if c + 2 < NC_OWN:
                    ld_vals.append((qxn[c % 2],
                                    load_fm_chunk(xnbuf[c % 2], xnbf_s,
                                                  NC_ALL - NC_OWN + c + 2,
                                                  qxn[c % 2],
                                                  dep=(te, grp_last[c]))))
                if c >= 1:
                    c_stats_z1(c - 1)
            c_bcast(NC_OWN - 2)
            c_stats_z1(NC_OWN - 1)
            c_bcast(NC_OWN - 1)
            SEr.wait_ge(te.h, grp_last[NC_OWN - 1])
            load_weight_bf(1, qg_w[2])
            pre_d[0] = (qpre[0], load_fm_chunk(
                xnbuf[0], xnbf_s, NC_ALL - NC_OWN, qpre[0],
                dep=(te, grp_last[NC_OWN - 2])))
            pre_d[1] = (qpre[1], load_fm_chunk(
                xnbuf[1], xnbf_s, NC_ALL - NC_OWN + 1, qpre[1],
                dep=(te, grp_last[NC_OWN - 1])))
            barrier(queues=[(SY, qxn[0]), (SY, qxn[1]), (SEr, qstl[0]),
                            (SEr, qstl[1]), (SY, qz1[0]), (SY, qz1[1]),
                            (SEr, qg)])

        if _STOP == "C":
            return
        qg = qg_w[2]
        qxn = dstream("qxn_d")
        qz1l = dstream("qz1l_d")
        qzs = dstream("qzs_d")
        te = sem("te_d")
        se = sem("se_d")
        ve = sem("ve_d")
        with ExitStack() as pd:
            dsb = lambda nm, sh, dt_: pd.enter_context(nc.sbuf_tensor(nm, sh, dt_))
            z1buf = [dsb(f"z1b{i}", [128, NCH * T], BF16) for i in range(2)]
            gstg = [dsb(f"ph_g{i}", [128, T], BF16) for i in range(2)]
            zstg = [dsb(f"ph_z{i}", [128, T], BF16) for i in range(2)]

            TE.wait_ge(qg.h, qg.n)
            free_vals = []
            grp_last = [0] * NC_OWN
            z_last = [0] * NC_OWN
            z1_vals = [0] * NC_OWN
            z_of = {}
            zs_after = {}

            def d_evac(c, m, bank, v):
                SE.wait_ge(te.h, v)
                if (c * NCH + m) >= 2:
                    SE.wait_ge(ve.h, z_of.get(c * NCH + m - 2, 0))
                SE.activation(gstg[m % 2][:, :], bank[:, :], AF.Silu).then_inc(se.h, 1)
                free_vals.append(se.inc())
                VE.wait_ge(se.h, free_vals[-1])
                if m == 0:
                    VE.wait_ge(qz1l[c % 2].h, z1_vals[c])
                VE.wait_ge(qzs[m % 2].h, qzs[m % 2].n)
                VE.tensor_mul(zstg[m % 2][:, :],
                              z1buf[c % 2][:, m * T:(m + 1) * T],
                              gstg[m % 2][:, :]).then_inc(ve.h, 1)
                zv = ve.inc()
                z_of[c * NCH + m] = zv
                if m == NCH - 1:
                    z_last[c] = zv
                SY.wait_ge(ve.h, zv)
                SY.dma_start(z_s[m * 128:(m + 1) * 128, c * T:(c + 1) * T],
                             zstg[m % 2][:, :]).then_inc(qzs[m % 2].h, 16)
                qzs[m % 2].inc(16)

            ld_vals = [pre_d[0], pre_d[1]]
            z1_vals[0] = load_fm_chunk(z1buf[0], z1_s, 0, qz1l[0], eng=SEr)
            for c in range(NC_OWN):
                if c + 1 < NC_OWN:
                    dep2 = (ve, z_last[c - 1]) if c >= 1 else None
                    z1_vals[c + 1] = load_fm_chunk(z1buf[(c + 1) % 2], z1_s,
                                                   c + 1, qz1l[(c + 1) % 2],
                                                   dep=dep2, eng=SEr)
                TE.wait_ge(ld_vals[c][0].h, ld_vals[c][1])
                vals = emit_chunk_groups(c, xnbuf[c % 2], te, free_vals, se,
                                         d_evac)
                grp_last[c] = vals[-1]
                zs_after[c] = (qzs[0].n, qzs[1].n)
                if c + 2 < NC_OWN:
                    ld_vals.append((qxn[c % 2],
                                    load_fm_chunk(xnbuf[c % 2], xnbf_s,
                                                  NC_ALL - NC_OWN + c + 2,
                                                  qxn[c % 2],
                                                  dep=(te, grp_last[c]))))
            SEr.wait_ge(te.h, grp_last[NC_OWN - 1])
            load_weight_bf(2, qg_w[3])
            SY.wait_ge(qzs[0].h, zs_after[0][0])
            SY.wait_ge(qzs[1].h, zs_after[0][1])
            pre_e[0] = (qpre[0], load_fm_chunk(
                xnbuf[0], z_s, 0, qpre[0],
                dep=(te, grp_last[NC_OWN - 2])))
            SY.wait_ge(qzs[0].h, zs_after[1][0])
            SY.wait_ge(qzs[1].h, zs_after[1][1])
            pre_e[1] = (qpre[1], load_fm_chunk(
                xnbuf[1], z_s, 1, qpre[1],
                dep=(te, grp_last[NC_OWN - 1])))
            barrier(queues=[(SY, qxn[0]), (SY, qxn[1]), (SEr, qz1l[0]),
                            (SEr, qz1l[1]), (SY, qzs[0]), (SY, qzs[1]),
                            (SEr, qg)])

        if _STOP == "D":
            return
        qg = qg_w[3]
        qzl = dstream("qzl_e")
        qxe = dstream("qxe_e")
        qpar = dstream("qpar_e")
        te = sem("te_e")
        ve = sem("ve_e")
        with ExitStack() as pe:
            esb = lambda nm, sh, dt_: pe.enter_context(nc.sbuf_tensor(nm, sh, dt_))
            pTR = [pe.enter_context(nc.psum_tensor(f"ptre{i}", [128, T], F32))
                   for i in range(4)]
            xbe = [esb(f"e_x{i}", [128, D], F32) for i in range(2)]
            oste = [esb(f"e_o{i}", [128, D], F32) for i in range(2)]
            attn_sb = v_sb

            TE.wait_ge(qg.h, qg.n)
            free_vals = []
            grp_last = [0] * NC_OWN
            evv = {}
            x_vals = {}
            out_done = {}
            tr_free = {}

            def e_evac(c, m, bank, v):
                VE.wait_ge(te.h, v)
                VE.tensor_copy(attn_sb[:, m * T:(m + 1) * T], bank[:, :]).then_inc(ve.h, 1)
                free_vals.append(ve.inc())
                evv[(c, m)] = free_vals[-1]

            def emit_xe_load(i):
                if i >= 2:
                    SY.wait_ge(ve.h, out_done.get(i - 2, 0))
                SY.dma_start(xbe[i % 2][:, :],
                             xp[PRE + i * 128: PRE + (i + 1) * 128, :]
                             ).then_inc(qxe[i % 2].h, 16)
                x_vals[i] = qxe[i % 2].inc(16)

            ld_vals = [pre_e[0], pre_e[1]]
            emit_xe_load(0)
            emit_xe_load(1)
            for c in range(NC_OWN):
                TE.wait_ge(ld_vals[c][0].h, ld_vals[c][1])
                vals = emit_chunk_groups(c, xnbuf[c % 2], te, free_vals, ve,
                                         e_evac)
                grp_last[c] = vals[-1]
                if c + 2 < NC_OWN:
                    ld_vals.append((qzl[c % 2],
                                    load_fm_chunk(xnbuf[c % 2], z_s, c + 2,
                                                  qzl[c % 2],
                                                  dep=(te, grp_last[c]),
                                                  eng=SEr)))
                for s in range(4):
                    i = c * 4 + s
                    ob = oste[i % 2]
                    if i >= 2:
                        VE.wait_ge(qpar[i % 2].h, qpar[i % 2].n)
                    for mg in range(4):
                        bank = pTR[mg % 4]
                        fkey = (i - 1, mg)
                        if fkey in tr_free:
                            TE.wait_ge(ve.h, tr_free[fkey])
                        for mi in range(4):
                            m = mg * 4 + mi
                            TE.wait_ge(ve.h, evv[(c, m)])
                            _inst = TE.transpose(
                                bank[:, mi * 128:(mi + 1) * 128],
                                attn_sb[:, m * T + s * 128:
                                        m * T + (s + 1) * 128],
                                sid32[:, :])
                        _inst.then_inc(te.h, 1)
                        trv = te.inc()
                        VE.wait_ge(te.h, trv)
                        VE.wait_ge(qxe[i % 2].h, x_vals[i])
                        VE.tensor_add(ob[:, mg * 512:(mg + 1) * 512],
                                      bank[:, :],
                                      xbe[i % 2][:, mg * 512:(mg + 1) * 512]).then_inc(ve.h, 1)
                        tr_free[(i, mg)] = ve.inc()
                    out_done[i] = tr_free[(i, 3)]
                    SY.wait_ge(ve.h, out_done[i])
                    SY.dma_start(par_s[i * 128:(i + 1) * 128, :], ob[:, :]
                                 ).then_inc(qpar[i % 2].h, 16)
                    qpar[i % 2].inc(16)
                    if i + 2 < OWN // 128:
                        emit_xe_load(i + 2)
            barrier(queues=[(SEr, qzl[0]), (SEr, qzl[1]), (SY, qxe[0]),
                            (SY, qxe[1]), (SY, qpar[0]), (SY, qpar[1]),
                            (SEr, qg)])
    _emit_bcde()
    ws.close()

    if _STOP in ("B", "C", "D", "E"):
        emit_dbg_dumps()
        es.close()
        return nc

    DR = mybir.MatmulPerfMode.DoubleRow
    NJC = NCH // 2
    with ExitStack() as ph:
        fsb = lambda nm, sh, dt_: ph.enter_context(nc.sbuf_tensor(nm, sh, dt_))
        pFF = [ph.enter_context(nc.psum_tensor(f"pff{i}", [128, 1024], F32))
               for i in range(4)]
        xn8 = fsb("xn8", [128, NCH * OWN], F8)
        wg8a = [fsb(f"wg8a{i}", [128, NCH * 128], F8) for i in range(2)]
        wg8b = [fsb(f"wg8b{i}", [128, NCH * 128], F8) for i in range(2)]
        W2 = 1024
        ffg = [fsb(f"ffg{i}", [128, W2], BF16) for i in range(2)]
        fgel = [fsb(f"fgel{i}", [128, W2], BF16) for i in range(2)]
        gt1 = [fsb(f"gt1_{i}", [128, W2], BF16) for i in range(2)]
        gt2 = [fsb(f"gt2_{i}", [128, W2], BF16) for i in range(2)]
        ff1 = [fsb(f"ff1{i}", [128, W2], BF16) for i in range(2)]
        factb = [fsb(f"factb{i}", [128, W2], BF16) for i in range(2)]
        fact8 = [fsb(f"fact8{i}", [128, W2], F8) for i in range(2)]
        wlb = [fsb(f"wlb{i}", [128, D], F8) for i in range(2)]

        qx8 = sem("qx8_f1")
        qwg = dstream("qwg_f1")
        qa8 = dstream("qa8_f1")
        qwl = dstream("qwl_f1")
        qws = dstream("qws_f1")
        te = sem("te_f1")
        ve = sem("ve_f1")
        se = sem("se_f1")

        xn8_ps = xn8.ap().ap[0][0]
        for jc in range(NJC):
            src_ap = bass.AP(xn8_s.tensor, 256 * jc * OWN,
                             [[2 * OWN, 128], [OWN, 2], [1, OWN]])
            SEr.dma_start(xn8[:, jc * 2 * OWN:(jc + 1) * 2 * OWN], src_ap
                          ).then_inc(qx8.h, 16)
            qx8.inc(16)

        def xn8_rhs(jc, tc):
            return bass.AP(xn8.ap().tensor,
                           xn8.ap().offset + jc * 2 * OWN + tc * T,
                           [[xn8_ps, 128], [OWN, 2], [1, T]])

        hc_grp_last = [0] * NHC

        def load_wg8(hc, dep_te):
            bi = hc % 2
            if dep_te:
                GE.wait_ge(te.h, dep_te)
            for j, wbuf in ((0, wg8a), (1, wg8b)):
                for i in range(2):
                    src_ap = bass.AP(wgt.tensor,
                                     j * D * HID + i * HID + hc * 128,
                                     [[2 * HID, 128], [256 * HID, NJC],
                                      [1, 128]])
                    dst_ap = bass.AP(wbuf[bi].ap().tensor,
                                     wbuf[bi].ap().offset + i * 128,
                                     [[wbuf[bi].ap().ap[0][0], 128],
                                      [256, NJC], [1, 128]])
                    GE.dma_start(dst_ap, src_ap).then_inc(qwg[bi].h, 16)
                    qwg[bi].inc(16)
            return qwg[bi].n

        def wg_lhsT(wbuf, jc):
            return bass.AP(wbuf.ap().tensor,
                           wbuf.ap().offset + jc * 256,
                           [[wbuf.ap().ap[0][0], 128], [128, 2], [1, 128]])

        wg_vals = [load_wg8(0, 0)]
        chain_used = [0, 0]
        pff_free = [(None, 0)] * 4
        TE.wait_ge(qx8.h, qx8.n)
        for hc in range(NHC):
            if hc + 1 < NHC:
                wg_vals.append(load_wg8(hc + 1,
                                        hc_grp_last[hc - 1] if hc >= 1 else 0))
            p = hc % 2
            GE.wait_ge(qws[p].h, qws[p].n)
            GE.dma_start(wlb[p][:, :], wlin[hc * 128:(hc + 1) * 128, :]
                         ).then_inc(qwl[p].h, 16)
            qwl[p].inc(16)
            GE.wait_ge(qwl[p].h, qwl[p].n)
            GE.dma_start(wl8_s[hc * 128:(hc + 1) * 128, :], wlb[p][:, :]
                         ).then_inc(qws[p].h, 16)
            qws[p].inc(16)

            TE.wait_ge(qwg[hc % 2].h, wg_vals[hc])
            for tcp in range(NC_OWN // 2):
                pp = tcp % 2
                pg = pFF[pp]
                pf = pFF[2 + pp]
                for bank, bank_i in ((pg, pp), (pf, 2 + pp)):
                    ctr, v = pff_free[bank_i]
                    if ctr is not None:
                        TE.wait_ge(ctr.h, v)
                for half in range(2):
                    tc = 2 * tcp + half
                    for bank, wbuf in ((pg, wg8a), (pf, wg8b)):
                        for jc in range(NJC):
                            _inst = TE.matmul(
                                bank[:, half * T:(half + 1) * T],
                                wg_lhsT(wbuf[hc % 2], jc),
                                xn8_rhs(jc, tc),
                                start=(jc == 0), stop=(jc == NJC - 1),
                                perf_mode=DR)
                _inst.then_inc(te.h, 1)
                grp_v = te.inc()
                hc_grp_last[hc] = grp_v
                SE.wait_ge(te.h, grp_v)
                SE.wait_ge(ve.h, chain_used[pp])
                SE.copy(ffg[pp][:, :], pg[:, :]).then_inc(se.h, 1)
                ffg_v = se.inc()
                pff_free[pp] = (se, ffg_v)
                SE.copy(ff1[pp][:, :], pf[:, :]).then_inc(se.h, 1)
                ff1_v = se.inc()
                pff_free[2 + pp] = (se, ff1_v)
                VE.wait_ge(se.h, ffg_v)
                VE.wait_ge(ve.h, chain_used[pp])
                VE.tensor_mul(gt1[pp][:, :], ffg[pp][:, :], ffg[pp][:, :])
                VE.tensor_mul(gt1[pp][:, :], gt1[pp][:, :], ffg[pp][:, :])
                VE.tensor_scalar_mul(gt1[pp][:, :], gt1[pp][:, :], C2BF)
                VE.tensor_add(gt1[pp][:, :], ffg[pp][:, :], gt1[pp][:, :])
                VE.tensor_scalar_mul(gt1[pp][:, :], gt1[pp][:, :], C1BF).then_inc(ve.h, 1)
                g5_v = ve.inc()
                SE.wait_ge(ve.h, g5_v)
                SE.activation(gt2[pp][:, :], gt1[pp][:, :], AF.Tanh).then_inc(se.h, 1)
                tanh_v = se.inc()
                VE.wait_ge(se.h, tanh_v)
                VE.tensor_scalar(gt2[pp][:, :], gt2[pp][:, :], 0.5, 0.5,
                                 op0=OP.mult, op1=OP.add)
                VE.tensor_mul(fgel[pp][:, :], ffg[pp][:, :], gt2[pp][:, :])
                VE.wait_ge(se.h, ff1_v)
                VE.tensor_mul(factb[pp][:, :], fgel[pp][:, :], ff1[pp][:, :])
                VE.wait_ge(qa8[pp].h, qa8[pp].n)
                VE.tensor_copy(fact8[pp][:, :], factb[pp][:, :]).then_inc(ve.h, 1)
                a8v = ve.inc()
                chain_used[pp] = a8v
                SY.wait_ge(ve.h, a8v)
                SY.dma_start(
                    act8_s[hc * 128:(hc + 1) * 128,
                           2 * tcp * T:(2 * tcp + 2) * T],
                    fact8[pp][:, :]).then_inc(qa8[pp].h, 16)
                qa8[pp].inc(16)
        barrier(queues=[(SY, qx8), (SY, qa8[0]), (SY, qa8[1]),
                        (GE, qwg[0]), (GE, qwg[1]), (GE, qwl[0]),
                        (GE, qwl[1]), (GE, qws[0]), (GE, qws[1])])

    if _STOP == "F1":
        emit_dbg_dumps()
        es.close()
        return nc

    with ExitStack() as ph:
        fsb = lambda nm, sh, dt_: ph.enter_context(nc.sbuf_tensor(nm, sh, dt_))
        banks8 = [ph.enter_context(nc.psum_tensor(f"pex{i}", [128, T], F32))
                  for i in range(8)]
        a8 = [[fsb(f"a8_{p}_{i}", [128, (NHC // 2) * T], F8) for i in range(2)]
              for p in range(2)]
        wl2 = [fsb(f"wl2_{i}", [128, 2048], F8) for i in range(4)]
        ffsl = fsb("ffsl", [128, NCH * T], BF16)
        outtm = [fsb(f"outtm{i}", [128, D], BF16) for i in range(2)]
        parb = [fsb(f"parb{i}", [128, D], F32) for i in range(2)]
        outb = [fsb(f"outb{i}", [128, D], F32) for i in range(2)]

        qa8 = dstream("qa8_f2")
        qwl2 = dstream("qwl2_f2", 4)
        qpar2 = dstream("qpar_f2")
        qout = dstream("qout_f2")
        qfT = dstream("qfT_f2")
        qtt = dstream("qtt_f2")
        te = sem("te_f2")
        ve = sem("ve_f2")

        NJC2 = NHC // 2
        def load_a8(tc, dep_te):
            p = tc % 2
            if dep_te:
                GE.wait_ge(te.h, dep_te)
            for jc in range(NJC2):
                half = jc // (NJC2 // 2)
                jj = jc % (NJC2 // 2)
                buf = a8[p][half]
                ps = buf.ap().ap[0][0]
                for i in range(2):
                    src_ap = bass.AP(act8_s.tensor,
                                     (256 * jc + i) * OWN + tc * T,
                                     [[2 * OWN, 128], [1, T]])
                    dst_ap = bass.AP(buf.ap().tensor,
                                     buf.ap().offset + jj * 2 * T + i * T,
                                     [[ps, 128], [1, T]])
                    GE.dma_start(dst_ap, src_ap).then_inc(qa8[p].h, 16)
                    qa8[p].inc(16)
            return qa8[p].n

        def a8_rhs(tc, jc):
            half = jc // (NJC2 // 2)
            jj = jc % (NJC2 // 2)
            buf = a8[tc % 2][half]
            return bass.AP(buf.ap().tensor,
                           buf.ap().offset + jj * 2 * T,
                           [[buf.ap().ap[0][0], 128], [T, 2], [1, T]])

        a8_vals = [load_a8(0, 0)]
        par_vals = {}
        tr_free2 = {}
        bank_busy = [0] * 8
        mm_last = 0
        wl_buf_user = [0, 0, 0, 0]
        wl_vals = {}

        SEr = nc.scalar

        def emit_par_load(i):
            SEr.wait_ge(ve.h, tr_free2.get((i - 2, 0), 0))
            SEr.dma_start(parb[i % 2][:, :], par_s[i * 128:(i + 1) * 128, :]
                          ).then_inc(qpar2[i % 2].h, 16)
            par_vals[i] = qpar2[i % 2].inc(16)

        emit_par_load(0)
        emit_par_load(1)

        def load_wl2(key, buf):
            tck, mh, jc = key
            SY.wait_ge(te.h, wl_buf_user[buf])
            ps = wl2[buf].ap().ap[0][0]
            for i in range(2):
                src_ap = bass.AP(wl8_s.tensor,
                                 (256 * jc + i) * D + mh * 1024,
                                 [[2 * D, 128], [1, 1024]])
                dst_ap = bass.AP(wl2[buf].ap().tensor,
                                 wl2[buf].ap().offset + i * 1024,
                                 [[ps, 128], [1, 1024]])
                SY.dma_start(dst_ap, src_ap).then_inc(qwl2[buf].h, 16)
                qwl2[buf].inc(16)
            wl_vals[key] = qwl2[buf].n

        def wl2_lhsT(buf, mi):
            return bass.AP(wl2[buf].ap().tensor,
                           wl2[buf].ap().offset + mi * 128,
                           [[wl2[buf].ap().ap[0][0], 128], [1024, 2],
                            [1, 128]])

        all_keys = [(tc_, mh_, jc_) for tc_ in range(NC_OWN)
                    for mh_ in range(2) for jc_ in range(NJC2)]
        for idx0 in range(3):
            load_wl2(all_keys[idx0], idx0 % 4)
        gstep = 0
        for tc in range(NC_OWN):
            p = tc % 2
            TE.wait_ge(qa8[p].h, a8_vals[tc])
            if tc + 1 < NC_OWN:
                a8_vals.append(load_a8(tc + 1, mm_last))
            evac_v = {}
            for mh in range(2):
                for jc in range(NJC2):
                    buf = gstep % 4
                    if gstep + 3 < len(all_keys):
                        load_wl2(all_keys[gstep + 3], (gstep + 3) % 4)
                    gstep += 1
                    TE.wait_ge(qwl2[buf].h, wl_vals[(tc, mh, jc)])
                    if jc == 0:
                        for mi in range(8):
                            if bank_busy[mi]:
                                TE.wait_ge(ve.h, bank_busy[mi])
                    rhs = a8_rhs(tc, jc)
                    for mi in range(8):
                        _inst = TE.matmul(
                            banks8[mi][:, :],
                            wl2_lhsT(buf, mi),
                            rhs,
                            start=(jc == 0), stop=(jc == NJC2 - 1),
                            perf_mode=mybir.MatmulPerfMode.DoubleRow)
                    _inst.then_inc(te.h, 1)
                    mm_last = te.inc()
                    wl_buf_user[buf] = mm_last
                for mi in range(8):
                    m = mh * 8 + mi
                    VE.wait_ge(te.h, mm_last)
                    if tc > 0 and mi == 0:
                        VE.wait_ge(qfT[0].h, qfT[0].n)
                        VE.wait_ge(qfT[1].h, qfT[1].n)
                    VE.tensor_copy(ffsl[:, m * T:(m + 1) * T],
                                   banks8[mi][:, :]).then_inc(ve.h, 1)
                    evac_v[m] = ve.inc()
                    bank_busy[mi] = evac_v[m]
                    SY.wait_ge(ve.h, evac_v[m])
                    SY.dma_start(
                        ffnT_s[m * 128:(m + 1) * 128, tc * T:(tc + 1) * T],
                        ffsl[:, m * T:(m + 1) * T]).then_inc(qfT[m % 2].h, 16)
                    qfT[m % 2].inc(16)
            fT_done = (qfT[0].n, qfT[1].n)
            for s in range(4):
                i = tc * 4 + s
                ob = outb[i % 2]
                SEr.wait_ge(ve.h, tr_free2.get((i - 2, 0), 0))
                SEr.wait_ge(qfT[0].h, fT_done[0])
                SEr.wait_ge(qfT[1].h, fT_done[1])
                SEr.dma_start_transpose(
                    outtm[i % 2][:, :],
                    ffnT_s[:, i * 128:(i + 1) * 128]).then_inc(qtt[i % 2].h, 16)
                qtt[i % 2].inc(16)
                if i >= 2:
                    VE.wait_ge(qout[i % 2].h, qout[i % 2].n)
                VE.wait_ge(qtt[i % 2].h, qtt[i % 2].n)
                VE.wait_ge(qpar2[i % 2].h, par_vals[i])
                VE.tensor_add(ob[:, :], outtm[i % 2][:, :],
                              parb[i % 2][:, :]).then_inc(ve.h, 1)
                tr_free2[(i, 0)] = ve.inc()
                SEr.wait_ge(ve.h, tr_free2[(i, 0)])
                SEr.dma_start(out[i * 128:(i + 1) * 128, :], ob[:, :]
                              ).then_inc(qout[i % 2].h, 16)
                qout[i % 2].inc(16)
                if i + 2 < OWN // 128:
                    emit_par_load(i + 2)
        SEr.wait_ge(qout[0].h, qout[0].n)
        SEr.wait_ge(qout[1].h, qout[1].n)
        SY.wait_ge(qfT[0].h, qfT[0].n)
        SY.wait_ge(qfT[1].h, qfT[1].n)
        GE.wait_ge(qa8[0].h, qa8[0].n)
        GE.wait_ge(qa8[1].h, qa8[1].n)
        emit_dbg_dumps()

    es.close()
    return nc


_NC_CACHE = None


def _get_nc():
    global _NC_CACHE
    if _NC_CACHE is None:
        _NC_CACHE = build()
    return _NC_CACHE


def _make_in_maps(x, Wq, Wv, Wg, Wo, beta, ln_g, ln_b, wgt_, wlin_):
    import ml_dtypes
    beta = np.ascontiguousarray(
        np.broadcast_to(beta.reshape(1, H), (128, H)))
    ln_g = np.ascontiguousarray(ln_g.reshape(NCH, 128).T)
    ln_b = np.ascontiguousarray(ln_b.reshape(NCH, 128).T)
    cid32 = np.eye(128, dtype=np.float32)
    cid16 = np.eye(128, dtype=ml_dtypes.bfloat16)
    conc = np.ones((128, 1), np.float32)
    conr = np.ones((1, 128), np.float32)
    in_maps = []
    for c in range(8):
        b, h = c % 4, c // 4
        xpv = np.zeros((NT, D), np.float32)
        if h == 0:
            xpv[PRE:] = x[b, 0:OWN]
        else:
            xpv[:] = x[b]
        in_maps.append({
            "xp": xpv, "wq": Wq, "wv": Wv, "wg": Wg, "wo": Wo,
            "beta": beta, "lng": ln_g, "lnb": ln_b,
            "wgt": wgt_, "wlin": wlin_,
            "cid32": cid32, "cid16": np.asarray(cid16),
            "conc": conc, "conr": conr,
        })
    return in_maps


def kernel(L0_attn_x, L0_attn_Wq, L0_attn_Wv, L0_attn_Wg, L0_attn_Wo,
           L0_attn_beta, L0_attn_ln_g, L0_attn_ln_b,
           L0_ffn_w_gating, L0_ffn_w_linear):
    x = np.ascontiguousarray(np.asarray(L0_attn_x, dtype=np.float32))
    Wq = np.ascontiguousarray(np.asarray(L0_attn_Wq, dtype=np.float32))
    Wv = np.ascontiguousarray(np.asarray(L0_attn_Wv, dtype=np.float32))
    Wg = np.ascontiguousarray(np.asarray(L0_attn_Wg, dtype=np.float32))
    Wo = np.ascontiguousarray(np.asarray(L0_attn_Wo, dtype=np.float32))
    beta = np.asarray(L0_attn_beta, dtype=np.float32)
    ln_g = np.asarray(L0_attn_ln_g, dtype=np.float32)
    ln_b = np.asarray(L0_attn_ln_b, dtype=np.float32)
    wgt_ = np.ascontiguousarray(np.asarray(L0_ffn_w_gating, dtype=np.float32))
    wlin_ = np.ascontiguousarray(np.asarray(L0_ffn_w_linear, dtype=np.float32))

    nc = _get_nc()
    in_maps = _make_in_maps(x, Wq, Wv, Wg, Wo, beta, ln_g, ln_b, wgt_, wlin_)
    res = run_bass_kernel_spmd(nc, in_maps, core_ids=list(range(8)))
    outp = np.empty((B_, S_, D), np.float32)
    for c in range(8):
        b, h = c % 4, c // 4
        outp[b, h * OWN:(h + 1) * OWN] = res.results[c]["out"]
    return outp


# version 16
# speedup vs baseline: 1.0516x; 1.0050x over previous
import numpy as np
from contextlib import ExitStack

import concourse.bass as bass
import concourse.mybir as mybir
from concourse.bass_utils import run_bass_kernel_spmd

F32 = mybir.dt.float32
BF16 = mybir.dt.bfloat16
F8 = mybir.dt.float8e4
AF = mybir.ActivationFunctionType
OP = mybir.AluOpType
AXX = mybir.AxisListType.X

B_, S_, D = 4, 8192, 2048
H, DH = 16, 128
NCH = D // 128
OWN = 4096
PRE = 4096
NT = PRE + OWN
T = 512
NC_ALL = NT // T
NC_OWN = OWN // T
HID = 8192
NHC = HID // 128
EPS = 1e-5
INV_D = 1.0 / D

C1BF = 0.796875
C2BF = 0.044677734375

_DBG = False
_STOP = None
_A_GROUPS = None
_A_MODE = "full"


class Ctr:
    __slots__ = ("h", "n")

    def __init__(self, h):
        self.h = h
        self.n = 0

    def inc(self, k=1):
        self.n += k
        return self.n


def build():
    nc = bass.Bass("TRN2", target_bir_lowering=False)

    xp = nc.dram_tensor("xp", [NT, D], F32, kind="ExternalInput").ap()
    wq = nc.dram_tensor("wq", [D, D], F32, kind="ExternalInput").ap()
    wv = nc.dram_tensor("wv", [D, D], F32, kind="ExternalInput").ap()
    wg = nc.dram_tensor("wg", [D, D], F32, kind="ExternalInput").ap()
    wo = nc.dram_tensor("wo", [D, D], F32, kind="ExternalInput").ap()
    beta = nc.dram_tensor("beta", [128, H], F32, kind="ExternalInput").ap()
    lng = nc.dram_tensor("lng", [128, NCH], F32, kind="ExternalInput").ap()
    lnb = nc.dram_tensor("lnb", [128, NCH], F32, kind="ExternalInput").ap()
    wgt = nc.dram_tensor("wgt", [2, D, HID], F32, kind="ExternalInput").ap()
    wlin = nc.dram_tensor("wlin", [HID, D], F32, kind="ExternalInput").ap()
    cid32 = nc.dram_tensor("cid32", [128, 128], F32, kind="ExternalInput").ap()
    cid16 = nc.dram_tensor("cid16", [128, 128], BF16, kind="ExternalInput").ap()
    conc = nc.dram_tensor("conc", [128, 1], F32, kind="ExternalInput").ap()
    conr = nc.dram_tensor("conr", [1, 128], F32, kind="ExternalInput").ap()
    out = nc.dram_tensor("out", [OWN, D], F32, kind="ExternalOutput").ap()

    xnbf_s = nc.dram_tensor("xnbf_s", [D, NT], BF16, kind="Internal").ap()
    xn8_s = nc.dram_tensor("xn8_s", [D, OWN], F8, kind="Internal").ap()
    st_s = nc.dram_tensor("st_s", [D, OWN], BF16, kind="Internal").ap()
    z1_s = nc.dram_tensor("z1_s", [D, OWN], BF16, kind="Internal").ap()
    z_s = nc.dram_tensor("z_s", [D, OWN], BF16, kind="Internal").ap()
    act8_s = nc.dram_tensor("act8_s", [HID, OWN], F8, kind="Internal").ap()
    par_s = nc.dram_tensor("par_s", [OWN, D], F32, kind="Internal").ap()
    wl8_s = nc.dram_tensor("wl8_s", [HID, D], F8, kind="Internal").ap()
    ffnT_s = nc.dram_tensor("ffnT_s", [D, OWN], BF16, kind="Internal").ap()
    wbf_s = nc.dram_tensor("wbf_s", [3 * D, D], BF16, kind="Internal").ap()

    dbg = {}
    if _DBG:
        for nm, sh, dt_ in [
            ("d_xnbf", [D, NT], BF16), ("d_xn8", [D, OWN], F8),
            ("d_st", [D, OWN], BF16), ("d_z1", [D, OWN], BF16),
            ("d_z", [D, OWN], BF16), ("d_par", [OWN, D], F32),
            ("d_act8", [HID, OWN], F8), ("d_lam", [128, H], F32),
        ]:
            dbg[nm] = nc.dram_tensor(nm, sh, dt_, kind="ExternalOutput").ap()

    class _EngWrap:

        def __init__(self, eng):
            self._e = eng
            self._w = {}
            self._r = {}

        @staticmethod
        def _iv(ap):
            lo = ap.offset
            hi = lo + 1
            for step, count in list(ap.ap)[1:]:
                hi += step * (count - 1)
            return (ap.tensor.name, lo, hi)

        @staticmethod
        def _hit(iv, table):
            spans = table.get(iv[0])
            if not spans:
                return False
            for lo, hi in spans:
                if iv[1] < hi and lo < iv[2]:
                    return True
            return False

        @staticmethod
        def _add(iv, table):
            table.setdefault(iv[0], []).append((iv[1], iv[2]))

        def drain(self, *a, **k):
            self._w.clear()
            self._r.clear()
            return self._e.drain(*a, **k)

        def __getattr__(self, name):
            attr = getattr(self._e, name)
            if name in ("wait_ge", "sem_inc", "dma_start"):
                return attr
            if not callable(attr):
                return attr

            def call(*args, **kwargs):
                aps = [a for a in args if isinstance(a, bass.AP)]
                outs = aps[:1]
                ins = aps[1:]
                for kwn, kv in kwargs.items():
                    if isinstance(kv, bass.AP):
                        (outs if kwn == "accum_out" else ins).append(kv)
                ivs_o = [self._iv(a) for a in outs]
                ivs_i = [self._iv(a) for a in ins]
                if (any(self._hit(iv, self._w) for iv in ivs_i)
                        or any(self._hit(iv, self._w) or self._hit(iv, self._r)
                               for iv in ivs_o)):
                    self._e.drain()
                    self._w.clear()
                    self._r.clear()
                for iv in ivs_o:
                    self._add(iv, self._w)
                for iv in ivs_i:
                    self._add(iv, self._r)
                return attr(*args, **kwargs)

            return call

    TE, GE, SY = nc.tensor, nc.gpsimd, nc.sync
    VE = _EngWrap(nc.vector)
    SE = _EngWrap(nc.scalar)
    GP = _EngWrap(nc.gpsimd)
    SEr = nc.scalar

    es = ExitStack()

    def sbuf(nm, sh, dt_):
        return es.enter_context(nc.sbuf_tensor(nm, sh, dt_))

    def psum(nm, sh, dt_):
        return es.enter_context(nc.psum_tensor(nm, sh, dt_))

    def sem(nm):
        return Ctr(es.enter_context(nc.semaphore(nm)))

    def dstream(nm, n=2):
        return [sem(f"{nm}{i}") for i in range(n)]

    sid32 = sbuf("sid32", [128, 128], F32)
    sid16 = sbuf("sid16", [128, 128], BF16)
    sonc = sbuf("sonc", [128, 1], F32)
    sonr = sbuf("sonr", [1, 128], F32)
    sbeta = sbuf("sbeta", [128, H], F32)
    slam = sbuf("slam", [128, H], F32)
    slng = sbuf("slng", [128, NCH], F32)
    slnb = sbuf("slnb", [128, NCH], F32)
    scarry = sbuf("scarry", [128, H], F32)
    barv = sbuf("barv", [1, 16], F32)
    bars = sbuf("bars", [1, 16], F32)


    bar = sem("bar")
    ALL_ENG = (TE, VE, SE, GE, SY)
    bar_k = [0]

    def barrier(queues=()):
        for eng, q in queues:
            eng.wait_ge(q.h, q.n)
        k = bar_k[0]
        bar_k[0] += 1
        TE.drain()
        nc.vector.tensor_copy(barv[0:1, k:k + 1],
                              sonc[0:1, 0:1]).then_inc(bar.h, 1)
        nc.scalar.copy(bars[0:1, k:k + 1],
                       sonc[0:1, 0:1]).then_inc(bar.h, 1)
        for eng in (TE, GE, SY):
            eng.sem_inc(bar.h, 1)
        bar.inc(5)
        for eng in ALL_ENG:
            eng.wait_ge(bar.h, bar.n)

    def emit_dbg_dumps():
        if not _DBG:
            return
        barrier()
        qd = sem("qs_dbg")
        for nm, src_ in [("d_xnbf", xnbf_s), ("d_xn8", xn8_s),
                         ("d_st", st_s), ("d_z1", z1_s), ("d_z", z_s),
                         ("d_par", par_s), ("d_act8", act8_s)]:
            SY.dma_start(dbg[nm], src_).then_inc(qd.h, 16)
            qd.inc(16)
        SY.wait_ge(qd.h, qd.n)

    def lam_bcast(hd, n):
        col = slam[:, hd:hd + 1]
        return bass.AP(col.tensor, col.offset, [[col.ap[0][0], 128], [0, n]])

    qgp = sem("qg_prep")
    for dst, src in [(sid32, cid32), (sid16, cid16), (sonc, conc),
                     (sonr, conr), (sbeta, beta), (slng, lng), (slnb, lnb)]:
        GE.dma_start(dst[:, :], src).then_inc(qgp.h, 16)
        qgp.inc(16)
    SE.wait_ge(qgp.h, qgp.n)
    nc.vector.wait_ge(qgp.h, qgp.n)
    SE.activation(slam[:, :], sbeta[:, :], AF.Sigmoid)
    if _DBG:
        sep = sem("se_prep")
        SE.drain()
        nc.scalar.sem_inc(sep.h, 1)
        sep.inc()
        qsp = sem("qs_prep")
        SY.wait_ge(sep.h, sep.n)
        SY.dma_start(dbg["d_lam"], slam[:, :]).then_inc(qsp.h, 16)
        qsp.inc(16)
        SY.wait_ge(qsp.h, qsp.n)
    barrier(queues=[(GE, qgp)])

    if _STOP == "P":
        es.close()
        return nc

    ws = ExitStack()
    w_sb = ws.enter_context(nc.sbuf_tensor("w_sb", [128, NCH * D], BF16))
    pMM = [ws.enter_context(nc.psum_tensor(f"pmm{i}", [128, T], F32))
           for i in range(4)]
    qg_w = [sem(f"qg_w{i}") for i in range(4)]
    qg_w2 = [sem(f"qgb_w{i}") for i in range(4)]

    def load_weight(wap, qgc):
        for k in range(NCH):
            GE.dma_start(w_sb[:, k * D:(k + 1) * D],
                         wap[k * 128:(k + 1) * 128, :]).then_inc(qgc.h, 16)
            qgc.inc(16)

    def load_weight_bf(wi, qgc, use_ge=True):
        qgc2 = qg_w2[wi + 1]
        for k in range(NCH):
            if k % 2 == 0 or not use_ge:
                eng, qq = SEr, qgc
            else:
                eng, qq = GE, qgc2
            eng.dma_start(w_sb[:, k * D:(k + 1) * D],
                          wbf_s[(wi * NCH + k) * 128:(wi * NCH + k + 1) * 128, :]
                          ).then_inc(qq.h, 16)
            qq.inc(16)

    load_weight(wv, qg_w[0])

    with ExitStack() as ph:
        asb = lambda nm, sh, dt_: ph.enter_context(nc.sbuf_tensor(nm, sh, dt_))
        xbuf = [asb(f"a_x{i}", [128, D], F32) for i in range(8)]
        xntm = [asb(f"a_xn{i}", [128, D], F32) for i in range(8)]
        stt_t = asb("a_stt", [128, 48], F32)
        stgbf = [asb(f"a_gbf{i}", [128, T], BF16) for i in range(2)]
        stg8 = [asb(f"a_g8{i}", [128, T], F8) for i in range(2)]
        wcb = [asb(f"a_wcb{i}", [128, D], BF16) for i in range(2)]
        pTR = [ph.enter_context(nc.psum_tensor(f"ptra{i}", [128, T], F32))
               for i in range(2)] + pMM[0:2]

        qwc = dstream("qwc_a")
        qwps = dstream("qwps_a")
        for wi, wap in enumerate((wq, wg, wo)):
            for k in range(NCH):
                p = k % 2
                if wi or k >= 2:
                    GE.wait_ge(qwps[p].h, qwps[p].n)
                GE.dma_start(wcb[p][:, :], wap[k * 128:(k + 1) * 128, :]
                             ).then_inc(qwc[p].h, 16)
                qwc[p].inc(16)
                GE.wait_ge(qwc[p].h, qwc[p].n)
                GE.dma_start(
                    wbf_s[(wi * NCH + k) * 128:(wi * NCH + k + 1) * 128, :],
                    wcb[p][:, :]).then_inc(qwps[p].h, 16)
                qwps[p].inc(16)

        qx = dstream("qx_a", 8)
        qbf = dstream("qbf_a")
        q8 = dstream("q8_a")
        ve = sem("ve_a")
        se = sem("se_a")
        te = sem("te_a")

        NTILE = NT // 128
        ld_done = [0] * NTILE
        xn_done = [0] * NTILE
        grp_tr_done = [0] * NC_ALL
        evac_done = {}

        def wait_xn(eng, i):
            v = xn_done[i]
            if isinstance(v, tuple):
                eng.wait_ge(se.h, v[1])
            else:
                eng.wait_ge(ve.h, v)

        def emit_x_load(i):
            if i >= 8:
                wait_xn(SY, i - 8)
            SY.dma_start(xbuf[i % 8][:, :], xp[i * 128:(i + 1) * 128, :]
                         ).then_inc(qx[i % 8].h, 16)
            ld_done[i] = qx[i % 8].inc(16)

        for i in range(8):
            emit_x_load(i)

        for g in range(NC_ALL):
            own_g = g >= NC_ALL - NC_OWN
            stt = stt_t[:, (g % 2) * 24:(g % 2) * 24 + 24]
            if g >= 2:
                VE.wait_ge(te.h, grp_tr_done[g - 2])
                SE.wait_ge(te.h, grp_tr_done[g - 2])
            for s in range(4):
                i = g * 4 + s
                xb = xbuf[i % 8]
                dmp = xntm[(g % 2) * 4 + s]
                VE.wait_ge(qx[i % 8].h, ld_done[i])
                SE.wait_ge(qx[i % 8].h, ld_done[i])
                if s < 2:
                    VE.reduce_sum(stt[:, s:s + 1], xb[:, :], axis=AXX)
                else:
                    SE.activation(dmp[:, :], xb[:, :], AF.Identity,
                                  accum_out=stt[:, s:s + 1])
                SE.activation(dmp[:, :], xb[:, :], AF.Square,
                              accum_out=stt[:, 4 + s:5 + s])
            SE.drain()
            nc.scalar.sem_inc(se.h, 1)
            sq_v = se.inc()
            VE.wait_ge(se.h, sq_v)
            VE.tensor_scalar_mul(stt[:, 8:12], stt[:, 0:4], -INV_D)
            VE.tensor_scalar_mul(stt[:, 12:16], stt[:, 4:8], INV_D)
            VE.tensor_mul(stt[:, 16:20], stt[:, 8:12], stt[:, 8:12])
            VE.tensor_sub(stt[:, 12:16], stt[:, 12:16], stt[:, 16:20])
            VE.tensor_scalar_add(stt[:, 12:16], stt[:, 12:16], EPS).then_inc(ve.h, 1)
            d_v = ve.inc()
            SE.wait_ge(ve.h, d_v)
            SE.activation(stt[:, 16:20], stt[:, 12:16], AF.Sqrt).then_inc(se.h, 1)
            s0_v = se.inc()
            VE.wait_ge(se.h, s0_v)
            VE.reciprocal(stt[:, 20:24], stt[:, 16:20])
            VE.tensor_mul(stt[:, 16:20], stt[:, 20:24], stt[:, 20:24])
            VE.tensor_mul(stt[:, 16:20], stt[:, 16:20], stt[:, 12:16])
            VE.tensor_scalar(stt[:, 16:20], stt[:, 16:20], -0.5, 1.5,
                             op0=OP.mult, op1=OP.add)
            VE.tensor_mul(stt[:, 20:24], stt[:, 20:24], stt[:, 16:20])
            VE.tensor_mul(stt[:, 12:16], stt[:, 8:12], stt[:, 20:24]).then_inc(ve.h, 1)
            r_v = ve.inc()
            SE.wait_ge(ve.h, r_v)
            for s in range(4):
                i = g * 4 + s
                xb = xbuf[i % 8]
                xt = xntm[(g % 2) * 4 + s]
                if s < 2:
                    VE.tensor_scalar(xt[:, :], xb[:, :], stt[:, 8 + s:9 + s],
                                     stt[:, 20 + s:21 + s],
                                     op0=OP.add, op1=OP.mult).then_inc(ve.h, 1)
                    xn_done[i] = ve.inc()
                else:
                    SE.activation(xt[:, :], xb[:, :], AF.Identity,
                                  bias=stt[:, 12 + s:13 + s],
                                  scale=stt[:, 20 + s:21 + s]).then_inc(se.h, 1)
                    xn_done[i] = ("se", se.inc())
                if i + 8 < NTILE:
                    emit_x_load(i + 8)

            for j in range(NCH):
                bank = pTR[j % 4]
                key = (g, j - 4) if j >= 4 else (g - 1, NCH - 4 + j)
                if key in evac_done:
                    vv, sv = evac_done[key]
                    TE.wait_ge(ve.h, vv)
                    TE.wait_ge(se.h, sv)
                if j == 0:
                    for s4 in range(4):
                        wait_xn(TE, g * 4 + s4)
                for s in range(4):
                    xt = xntm[(g % 2) * 4 + s]
                    _inst = TE.transpose(bank[:, s * 128:(s + 1) * 128],
                                         xt[:, j * 128:(j + 1) * 128],
                                         sid32[:, :])
                _inst.then_inc(te.h, 1)
                tr_v = te.inc()
                if j == NCH - 1:
                    grp_tr_done[g] = tr_v
                stb = stgbf[j % 2]
                st8 = stg8[j % 2]
                VE.wait_ge(te.h, tr_v)
                VE.wait_ge(qbf[j % 2].h, qbf[j % 2].n)
                VE.tensor_copy(stb[:, :], bank[:, :]).then_inc(ve.h, 1)
                bf_v = ve.inc()
                sv_8 = 0
                if own_g:
                    SE.wait_ge(ve.h, bf_v)
                    SE.wait_ge(q8[j % 2].h, q8[j % 2].n)
                    SE.copy(st8[:, :], bank[:, :]).then_inc(se.h, 1)
                    sv_8 = se.inc()
                evac_done[(g, j)] = (bf_v, sv_8)
                SY.wait_ge(ve.h, bf_v)
                SY.dma_start(xnbf_s[j * 128:(j + 1) * 128, g * T:(g + 1) * T],
                             stb[:, :]).then_inc(qbf[j % 2].h, 16)
                qbf[j % 2].inc(16)
                if own_g:
                    go = g - (NC_ALL - NC_OWN)
                    SY.wait_ge(se.h, sv_8)
                    SY.dma_start(
                        xn8_s[j * 128:(j + 1) * 128, go * T:(go + 1) * T],
                        st8[:, :]).then_inc(q8[j % 2].h, 16)
                    q8[j % 2].inc(16)

        barrier(queues=[(SY, q) for q in qx] +
                       [(SY, qbf[0]), (SY, qbf[1]),
                        (SY, q8[0]), (SY, q8[1]),
                        (GE, qwc[0]), (GE, qwc[1]),
                        (GE, qwps[0]), (GE, qwps[1])])

    if _STOP == "A":
        emit_dbg_dumps()
        ws.close()
        es.close()
        return nc

    def _emit_bcde():
      with ExitStack() as ph:
        bsb = lambda nm, sh, dt_: ph.enter_context(nc.sbuf_tensor(nm, sh, dt_))
        xnbuf = [bsb(f"xnb{i}", [128, NCH * T], BF16) for i in range(2)]
        v_sb = bsb("v_sb", [128, NCH * T], F32)
        s2_sb = bsb("s2_sb", [128, NCH * T], F32)

        qpre = dstream("qpre")
        pre_c = {}
        pre_d = {}
        pre_e = {}

        def load_fm_chunk(dst, src, c, qsc, dep=None, eng=SY):
            if dep is not None:
                eng.wait_ge(dep[0].h, dep[1])
            for k in range(NCH):
                eng.dma_start(dst[:, k * T:(k + 1) * T],
                              src[k * 128:(k + 1) * 128, c * T:(c + 1) * T]
                              ).then_inc(qsc.h, 16)
                qsc.inc(16)
            return qsc.n

        def emit_chunk_groups(c, xnb, te_c, free_list, free_sem, emit_evac,
                              hook=None):
            vals = []
            for m in range(NCH):
                if hook is not None:
                    hook(m)
                gi = c * NCH + m
                if gi >= 4:
                    TE.wait_ge(free_sem.h, free_list[gi - 4])
                bank = pMM[gi % 4]
                for k in range(NCH):
                    _inst = TE.matmul(
                        bank[:, :],
                        w_sb[:, k * D + m * 128: k * D + (m + 1) * 128],
                        xnb[:, k * T:(k + 1) * T],
                        start=(k == 0), stop=(k == NCH - 1))
                _inst.then_inc(te_c.h, 1)
                v = te_c.inc()
                vals.append(v)
                emit_evac(c, m, bank, v)
            return vals

        qg = qg_w[0]
        qg2 = qg_w2[0]
        qxn = dstream("qxn_b")
        qst = dstream("qst_b")
        te = sem("te_b")
        ve = sem("ve_b")
        se = sem("se_b")
        b_se_last = [0]
        with ExitStack() as pb:
            stg = [pb.enter_context(nc.sbuf_tensor(f"b_stg{i}", [128, T], BF16))
                   for i in range(2)]
            v_sb2 = pb.enter_context(
                nc.sbuf_tensor("v_sb2", [128, NCH * T], F32))
            vbufs = [v_sb, v_sb2]

            TE.wait_ge(qg.h, qg.n)
            TE.wait_ge(qg2.h, qg2.n)

            free_vals = []
            grp_last = [0] * NC_ALL

            def b_evac(c, m, bank, v):
                VE.wait_ge(te.h, v)
                VE.tensor_copy(vbufs[c % 2][:, m * T:(m + 1) * T],
                               bank[:, :]).then_inc(ve.h, 1)
                free_vals.append(ve.inc())

            def b_scan(c):
                vb = vbufs[c % 2]
                own_c = c >= NC_ALL - NC_OWN
                scan_v = {}
                if b_se_last[0]:
                    VE.wait_ge(se.h, b_se_last[0])
                for hd in range(H):
                    init = scarry[:, hd:hd + 1] if c > 0 else 0.0
                    VE.tensor_tensor_scan(
                        s2_sb[:, hd * T:(hd + 1) * T], lam_bcast(hd, T),
                        vb[:, hd * T:(hd + 1) * T], init,
                        op0=OP.mult, op1=OP.add).then_inc(ve.h, 1)
                    scan_v[hd] = ve.inc()
                lastcols = bass.AP(s2_sb.ap().tensor,
                                   s2_sb.ap().offset + T - 1,
                                   [[s2_sb.ap().ap[0][0], 128], [T, H]])
                VE.tensor_copy(scarry[:, :], lastcols)
                if own_c:
                    for hd in range(H):
                        SE.wait_ge(ve.h, scan_v[hd])
                        SE.wait_ge(qst[hd % 2].h, qst[hd % 2].n)
                        SE.copy(stg[hd % 2][:, :],
                                s2_sb[:, hd * T:(hd + 1) * T]).then_inc(se.h, 1)
                        sv = se.inc()
                        SY.wait_ge(se.h, sv)
                        co = c - (NC_ALL - NC_OWN)
                        SY.dma_start(
                            st_s[hd * 128:(hd + 1) * 128,
                                 co * T:(co + 1) * T],
                            stg[hd % 2][:, :]).then_inc(qst[hd % 2].h, 16)
                        qst[hd % 2].inc(16)
                    b_se_last[0] = se.n

            ld_vals = [(qxn[0], load_fm_chunk(xnbuf[0], xnbf_s, 0, qxn[0]))]
            for c in range(NC_ALL):
                if c + 1 < NC_ALL:
                    dep = (te, grp_last[c - 1]) if c >= 1 else None
                    ld_vals.append((qxn[(c + 1) % 2],
                                    load_fm_chunk(xnbuf[(c + 1) % 2], xnbf_s,
                                                  c + 1, qxn[(c + 1) % 2],
                                                  dep=dep)))
                TE.wait_ge(ld_vals[c][0].h, ld_vals[c][1])
                vals = emit_chunk_groups(c, xnbuf[c % 2], te, free_vals, ve,
                                         b_evac)
                grp_last[c] = vals[-1]
                if c == NC_ALL - 1:
                    pre_c[0] = (qpre[0], load_fm_chunk(
                        xnbuf[0], xnbf_s, NC_ALL - NC_OWN, qpre[0],
                        dep=(te, grp_last[NC_ALL - 2])))
                if c >= 1:
                    b_scan(c - 1)
            pre_c[1] = (qpre[1], load_fm_chunk(
                xnbuf[1], xnbf_s, NC_ALL - NC_OWN + 1, qpre[1],
                dep=(te, grp_last[NC_ALL - 1])))
            b_scan(NC_ALL - 1)
            SEr.wait_ge(te.h, grp_last[NC_ALL - 1])
            GE.wait_ge(te.h, grp_last[NC_ALL - 1])
            load_weight_bf(0, qg_w[1])
            barrier(queues=[(SY, qxn[0]), (SY, qxn[1]), (SY, qst[0]),
                            (SY, qst[1]), (GE, qg)])

        if _STOP == "B":
            return
        qg = qg_w[1]
        qg2 = qg_w2[1]
        qxn = dstream("qxn_c")
        qstl = dstream("qstl_c")
        qz1 = dstream("qz1_c")
        bc = sem("bc_c")
        te = sem("te_c")
        ve = sem("ve_c")
        se = sem("se_c")
        ge = sem("ge_c")
        with ExitStack() as pc:
            csb = lambda nm, sh, dt_: pc.enter_context(nc.sbuf_tensor(nm, sh, dt_))
            pst1 = pc.enter_context(nc.psum_tensor("pst1", [1, T], F32))
            pst2 = pc.enter_context(nc.psum_tensor("pst2", [1, T], F32))
            pTR = [pc.enter_context(nc.psum_tensor(f"ptrc{i}", [128, T], F32))
                   for i in range(2)]
            stbuf = [csb(f"stb{i}", [128, NCH * T], BF16) for i in range(2)]
            gztmp = csb("gztmp", [128, 1024], F32)
            musigrep = csb("musigrep", [128, T], F32)
            rsigrep = csb("rsigrep", [128, T], F32)
            ztmp = csb("ztmp", [128, T], F32)
            y2t = [csb("y2t0", [128, T], F32), ztmp]
            z1stg = [csb(f"z1s{i}", [128, T], BF16) for i in range(2)]

            ybufs = [v_sb, s2_sb]

            TE.wait_ge(qg.h, qg.n)
            TE.wait_ge(qg2.h, qg2.n)

            free_vals = []
            grp_last = [0] * NC_OWN
            y_last = [0] * NC_OWN
            st_vals = [0] * NC_OWN
            st2_v = {}
            row_done = [0] * NC_OWN
            z1_done = [0] * NC_OWN
            z1_pair = {}

            def c_evac(c, m, bank, v):
                VE.wait_ge(te.h, v)
                if m == 0:
                    VE.wait_ge(qstl[c % 2].h, st_vals[c])
                if c >= 2:
                    VE.wait_ge(ge.h, z1_pair[(c - 2, m // 2)])
                VE.tensor_tensor(ybufs[c % 2][:, m * T:(m + 1) * T], bank[:, :],
                                 stbuf[c % 2][:, m * T:(m + 1) * T],
                                 op=OP.mult).then_inc(ve.h, 1)
                free_vals.append(ve.inc())
                if m == NCH - 1:
                    y_last[c] = free_vals[-1]

            def c_stats_z1(c):
                y_sb = ybufs[c % 2]
                TE.wait_ge(ve.h, y_last[c])
                if c >= 1:
                    TE.wait_ge(ve.h, row_done[c - 1])
                for m in range(NCH):
                    _inst = TE.matmul(pst1[:, :], sonc[:, :],
                                      y_sb[:, m * T:(m + 1) * T],
                                      start=(m == 0), stop=(m == NCH - 1))
                _inst.then_inc(te.h, 1)
                te.inc()
                for m in range(NCH):
                    if m >= 2:
                        VE.wait_ge(te.h, st2_v[(c, m - 2)])
                    VE.tensor_mul(y2t[m % 2][:, :],
                                  y_sb[:, m * T:(m + 1) * T],
                                  y_sb[:, m * T:(m + 1) * T]).then_inc(ve.h, 1)
                    y2v = ve.inc()
                    TE.wait_ge(ve.h, y2v)
                    _inst = TE.matmul(pst2[:, :], sonc[:, :], y2t[m % 2][:, :],
                                      start=(m == 0), stop=(m == NCH - 1))
                    _inst.then_inc(te.h, 1)
                    st2_v[(c, m)] = te.inc()
                st_mm = st2_v[(c, NCH - 1)]
                VE.wait_ge(te.h, st_mm)
                if c >= 1:
                    VE.wait_ge(ge.h, z1_done[c - 1])
                rowb = pst2[0:1, :]
                rowN = gztmp[0:1, 0:T]
                VE.tensor_scalar_mul(rowN, pst1[0:1, :], -INV_D)
                VE.tensor_scalar_mul(rowb, rowb, INV_D)
                VE.tensor_mul(rsigrep[0:1, 0:T], rowN, rowN)
                VE.tensor_sub(rowb, rowb, rsigrep[0:1, 0:T])
                VE.tensor_scalar_add(rowb, rowb, EPS).then_inc(ve.h, 1)
                d_v = ve.inc()
                SE.wait_ge(ve.h, d_v)
                SE.activation(rsigrep[0:1, 0:T], rowb, AF.Sqrt).then_inc(se.h, 1)
                s0_v = se.inc()
                VE.wait_ge(se.h, s0_v)
                VE.reciprocal(musigrep[0:1, 0:T], rsigrep[0:1, 0:T])
                VE.tensor_mul(rsigrep[0:1, 0:T], musigrep[0:1, 0:T],
                              musigrep[0:1, 0:T])
                VE.tensor_mul(rsigrep[0:1, 0:T], rsigrep[0:1, 0:T],
                              rowb)
                VE.tensor_scalar(rsigrep[0:1, 0:T], rsigrep[0:1, 0:T],
                                 -0.5, 1.5, op0=OP.mult, op1=OP.add)
                VE.tensor_mul(rsigrep[0:1, 0:T], musigrep[0:1, 0:T],
                              rsigrep[0:1, 0:T])
                VE.tensor_mul(musigrep[0:1, 0:T], rowN,
                              rsigrep[0:1, 0:T]).then_inc(ve.h, 1)
                rows_v = ve.inc()
                row_done[c] = rows_v
                VE.wait_ge(bc.h, 2 * (c + 1))
                VE.tensor_copy(musigrep[:, :], pTR[0][:, :])
                VE.tensor_copy(rsigrep[:, :], pTR[1][:, :]).then_inc(ve.h, 1)
                rep_v = ve.inc()
                GP.wait_ge(ve.h, rep_v)
                rs_ps = rsigrep.ap().ap[0][0]
                rs2 = bass.AP(rsigrep.ap().tensor, rsigrep.ap().offset,
                              [[rs_ps, 128], [0, 2], [1, T]])
                mu2b = bass.AP(musigrep.ap().tensor, musigrep.ap().offset,
                               [[musigrep.ap().ap[0][0], 128], [0, 2], [1, T]])
                for mp in range(NCH // 2):
                    GP.tensor_mul(gztmp[:, :],
                                  y_sb[:, 2 * mp * T:(2 * mp + 2) * T], rs2)
                    GP.tensor_add(gztmp[:, :], gztmp[:, :], mu2b).then_inc(ge.h, 1)
                    gv = ge.inc()
                    z1_pair[(c, mp)] = gv
                    for h2 in range(2):
                        m = 2 * mp + h2
                        GP.wait_ge(qz1[m % 2].h, qz1[m % 2].n)
                        GP.tensor_scalar(z1stg[m % 2][:, :],
                                         gztmp[:, h2 * T:(h2 + 1) * T],
                                         slng[:, m:m + 1], slnb[:, m:m + 1],
                                         op0=OP.mult, op1=OP.add).then_inc(ge.h, 1)
                        zv = ge.inc()
                        SY.wait_ge(ge.h, zv)
                        SY.dma_start(
                            z1_s[m * 128:(m + 1) * 128, c * T:(c + 1) * T],
                            z1stg[m % 2][:, :]).then_inc(qz1[m % 2].h, 16)
                        qz1[m % 2].inc(16)
                z1_done[c] = ge.n

            def c_bcast(c):
                TE.wait_ge(ve.h, row_done[c])
                TE.matmul(pTR[0][:, :], sonr[:, :], musigrep[0:1, 0:T],
                          start=True, stop=True).then_inc(bc.h, 1)
                bc.inc()
                TE.matmul(pTR[1][:, :], sonr[:, :], rsigrep[0:1, 0:T],
                          start=True, stop=True).then_inc(bc.h, 1)
                bc.inc()

            ld_vals = [pre_c[0], pre_c[1]]
            st_vals[0] = load_fm_chunk(stbuf[0], st_s, 0, qstl[0], eng=SEr)
            for c in range(NC_OWN):
                if c + 1 < NC_OWN:
                    dep2 = (ve, y_last[c - 1]) if c >= 1 else None
                    st_vals[c + 1] = load_fm_chunk(stbuf[(c + 1) % 2], st_s,
                                                   c + 1, qstl[(c + 1) % 2],
                                                   dep=dep2, eng=SEr)
                TE.wait_ge(ld_vals[c][0].h, ld_vals[c][1])
                hook = None
                if c >= 2:
                    def hook(m, _cc=c - 2):
                        if m == 4:
                            c_bcast(_cc)
                vals = emit_chunk_groups(c, xnbuf[c % 2], te, free_vals, ve,
                                         c_evac, hook=hook)
                grp_last[c] = vals[-1]
                if c + 2 < NC_OWN:
                    ld_vals.append((qxn[c % 2],
                                    load_fm_chunk(xnbuf[c % 2], xnbf_s,
                                                  NC_ALL - NC_OWN + c + 2,
                                                  qxn[c % 2],
                                                  dep=(te, grp_last[c]))))
                if c == NC_OWN - 1:
                    pre_d[0] = (qpre[0], load_fm_chunk(
                        xnbuf[0], xnbf_s, NC_ALL - NC_OWN, qpre[0],
                        dep=(te, grp_last[NC_OWN - 2])))
                if c >= 1:
                    c_stats_z1(c - 1)
            pre_d[1] = (qpre[1], load_fm_chunk(
                xnbuf[1], xnbf_s, NC_ALL - NC_OWN + 1, qpre[1],
                dep=(te, grp_last[NC_OWN - 1])))
            c_bcast(NC_OWN - 2)
            c_stats_z1(NC_OWN - 1)
            c_bcast(NC_OWN - 1)
            SEr.wait_ge(te.h, grp_last[NC_OWN - 1])
            load_weight_bf(1, qg_w[2], use_ge=False)
            barrier(queues=[(SY, qxn[0]), (SY, qxn[1]), (SEr, qstl[0]),
                            (SEr, qstl[1]), (SY, qz1[0]), (SY, qz1[1]),
                            (SEr, qg)])

        if _STOP == "C":
            return
        qg = qg_w[2]
        qg2 = qg_w2[2]
        qxn = dstream("qxn_d")
        qz1l = dstream("qz1l_d")
        qzs = dstream("qzs_d")
        te = sem("te_d")
        se = sem("se_d")
        ve = sem("ve_d")
        with ExitStack() as pd:
            dsb = lambda nm, sh, dt_: pd.enter_context(nc.sbuf_tensor(nm, sh, dt_))
            z1buf = [dsb(f"z1b{i}", [128, NCH * T], BF16) for i in range(2)]
            gstg = [dsb(f"ph_g{i}", [128, T], BF16) for i in range(2)]
            zstg = [dsb(f"ph_z{i}", [128, T], BF16) for i in range(2)]

            TE.wait_ge(qg.h, qg.n)
            TE.wait_ge(qg2.h, qg2.n)
            free_vals = []
            grp_last = [0] * NC_OWN
            z_last = [0] * NC_OWN
            z1_vals = [0] * NC_OWN
            z_of = {}
            zs_after = {}

            def d_evac(c, m, bank, v):
                SE.wait_ge(te.h, v)
                if (c * NCH + m) >= 2:
                    SE.wait_ge(ve.h, z_of.get(c * NCH + m - 2, 0))
                SE.activation(gstg[m % 2][:, :], bank[:, :], AF.Silu).then_inc(se.h, 1)
                free_vals.append(se.inc())
                VE.wait_ge(se.h, free_vals[-1])
                if m == 0:
                    VE.wait_ge(qz1l[c % 2].h, z1_vals[c])
                VE.wait_ge(qzs[m % 2].h, qzs[m % 2].n)
                VE.tensor_mul(zstg[m % 2][:, :],
                              z1buf[c % 2][:, m * T:(m + 1) * T],
                              gstg[m % 2][:, :]).then_inc(ve.h, 1)
                zv = ve.inc()
                z_of[c * NCH + m] = zv
                if m == NCH - 1:
                    z_last[c] = zv
                SY.wait_ge(ve.h, zv)
                SY.dma_start(z_s[m * 128:(m + 1) * 128, c * T:(c + 1) * T],
                             zstg[m % 2][:, :]).then_inc(qzs[m % 2].h, 16)
                qzs[m % 2].inc(16)

            ld_vals = [pre_d[0], pre_d[1]]
            z1_vals[0] = load_fm_chunk(z1buf[0], z1_s, 0, qz1l[0], eng=SEr)
            for c in range(NC_OWN):
                if c + 1 < NC_OWN:
                    dep2 = (ve, z_last[c - 1]) if c >= 1 else None
                    z1_vals[c + 1] = load_fm_chunk(z1buf[(c + 1) % 2], z1_s,
                                                   c + 1, qz1l[(c + 1) % 2],
                                                   dep=dep2, eng=SEr)
                TE.wait_ge(ld_vals[c][0].h, ld_vals[c][1])
                vals = emit_chunk_groups(c, xnbuf[c % 2], te, free_vals, se,
                                         d_evac)
                grp_last[c] = vals[-1]
                zs_after[c] = (qzs[0].n, qzs[1].n)
                if c + 2 < NC_OWN:
                    ld_vals.append((qxn[c % 2],
                                    load_fm_chunk(xnbuf[c % 2], xnbf_s,
                                                  NC_ALL - NC_OWN + c + 2,
                                                  qxn[c % 2],
                                                  dep=(te, grp_last[c]))))
            SEr.wait_ge(te.h, grp_last[NC_OWN - 1])
            GE.wait_ge(te.h, grp_last[NC_OWN - 1])
            load_weight_bf(2, qg_w[3])
            SY.wait_ge(qzs[0].h, zs_after[0][0])
            SY.wait_ge(qzs[1].h, zs_after[0][1])
            pre_e[0] = (qpre[0], load_fm_chunk(
                xnbuf[0], z_s, 0, qpre[0],
                dep=(te, grp_last[NC_OWN - 2])))
            SY.wait_ge(qzs[0].h, zs_after[1][0])
            SY.wait_ge(qzs[1].h, zs_after[1][1])
            pre_e[1] = (qpre[1], load_fm_chunk(
                xnbuf[1], z_s, 1, qpre[1],
                dep=(te, grp_last[NC_OWN - 1])))
            barrier(queues=[(SY, qxn[0]), (SY, qxn[1]), (SEr, qz1l[0]),
                            (SEr, qz1l[1]), (SY, qzs[0]), (SY, qzs[1]),
                            (SEr, qg)])

        if _STOP == "D":
            return
        qg = qg_w[3]
        qg2 = qg_w2[3]
        qzl = dstream("qzl_e")
        qxe = dstream("qxe_e")
        qpar = dstream("qpar_e")
        te = sem("te_e")
        ve = sem("ve_e")
        with ExitStack() as pe:
            esb = lambda nm, sh, dt_: pe.enter_context(nc.sbuf_tensor(nm, sh, dt_))
            pTR = [pe.enter_context(nc.psum_tensor(f"ptre{i}", [128, T], F32))
                   for i in range(4)]
            xbe = [esb(f"e_x{i}", [128, D], F32) for i in range(2)]
            oste = [esb(f"e_o{i}", [128, D], F32) for i in range(2)]
            attn_sb = v_sb

            TE.wait_ge(qg.h, qg.n)
            TE.wait_ge(qg2.h, qg2.n)
            free_vals = []
            grp_last = [0] * NC_OWN
            evv = {}
            x_vals = {}
            out_done = {}
            tr_free = {}

            def e_evac(c, m, bank, v):
                VE.wait_ge(te.h, v)
                VE.tensor_copy(attn_sb[:, m * T:(m + 1) * T], bank[:, :]).then_inc(ve.h, 1)
                free_vals.append(ve.inc())
                evv[(c, m)] = free_vals[-1]

            def emit_xe_load(i):
                if i >= 2:
                    SY.wait_ge(ve.h, out_done.get(i - 2, 0))
                SY.dma_start(xbe[i % 2][:, :],
                             xp[PRE + i * 128: PRE + (i + 1) * 128, :]
                             ).then_inc(qxe[i % 2].h, 16)
                x_vals[i] = qxe[i % 2].inc(16)

            ld_vals = [pre_e[0], pre_e[1]]
            emit_xe_load(0)
            emit_xe_load(1)
            for c in range(NC_OWN):
                TE.wait_ge(ld_vals[c][0].h, ld_vals[c][1])
                vals = emit_chunk_groups(c, xnbuf[c % 2], te, free_vals, ve,
                                         e_evac)
                grp_last[c] = vals[-1]
                if c + 2 < NC_OWN:
                    ld_vals.append((qzl[c % 2],
                                    load_fm_chunk(xnbuf[c % 2], z_s, c + 2,
                                                  qzl[c % 2],
                                                  dep=(te, grp_last[c]),
                                                  eng=SEr)))
                for s in range(4):
                    i = c * 4 + s
                    ob = oste[i % 2]
                    if i >= 2:
                        VE.wait_ge(qpar[i % 2].h, qpar[i % 2].n)
                    for mg in range(4):
                        bank = pTR[mg % 4]
                        fkey = (i - 1, mg)
                        if fkey in tr_free:
                            TE.wait_ge(ve.h, tr_free[fkey])
                        for mi in range(4):
                            m = mg * 4 + mi
                            TE.wait_ge(ve.h, evv[(c, m)])
                            _inst = TE.transpose(
                                bank[:, mi * 128:(mi + 1) * 128],
                                attn_sb[:, m * T + s * 128:
                                        m * T + (s + 1) * 128],
                                sid32[:, :])
                        _inst.then_inc(te.h, 1)
                        trv = te.inc()
                        VE.wait_ge(te.h, trv)
                        VE.wait_ge(qxe[i % 2].h, x_vals[i])
                        VE.tensor_add(ob[:, mg * 512:(mg + 1) * 512],
                                      bank[:, :],
                                      xbe[i % 2][:, mg * 512:(mg + 1) * 512]).then_inc(ve.h, 1)
                        tr_free[(i, mg)] = ve.inc()
                    out_done[i] = tr_free[(i, 3)]
                    SY.wait_ge(ve.h, out_done[i])
                    SY.dma_start(par_s[i * 128:(i + 1) * 128, :], ob[:, :]
                                 ).then_inc(qpar[i % 2].h, 16)
                    qpar[i % 2].inc(16)
                    if i + 2 < OWN // 128:
                        emit_xe_load(i + 2)
            barrier(queues=[(SEr, qzl[0]), (SEr, qzl[1]), (SY, qxe[0]),
                            (SY, qxe[1]), (SY, qpar[0]), (SY, qpar[1]),
                            (SEr, qg)])
    _emit_bcde()
    ws.close()

    if _STOP in ("B", "C", "D", "E"):
        emit_dbg_dumps()
        es.close()
        return nc

    DR = mybir.MatmulPerfMode.DoubleRow
    NJC = NCH // 2
    with ExitStack() as ph:
        fsb = lambda nm, sh, dt_: ph.enter_context(nc.sbuf_tensor(nm, sh, dt_))
        pFF = [ph.enter_context(nc.psum_tensor(f"pff{i}", [128, 1024], F32))
               for i in range(4)]
        xn8 = fsb("xn8", [128, NCH * OWN], F8)
        wg8a = [fsb(f"wg8a{i}", [128, NCH * 128], F8) for i in range(2)]
        wg8b = [fsb(f"wg8b{i}", [128, NCH * 128], F8) for i in range(2)]
        W2 = 1024
        ffg = [fsb(f"ffg{i}", [128, W2], BF16) for i in range(2)]
        fgel = [fsb(f"fgel{i}", [128, W2], BF16) for i in range(2)]
        gt1 = [fsb(f"gt1_{i}", [128, W2], BF16) for i in range(2)]
        gt2 = [fsb(f"gt2_{i}", [128, W2], BF16) for i in range(2)]
        ff1 = [fsb(f"ff1{i}", [128, W2], BF16) for i in range(2)]
        factb = [fsb(f"factb{i}", [128, W2], BF16) for i in range(2)]
        fact8 = [fsb(f"fact8{i}", [128, W2], F8) for i in range(2)]
        wlb = [fsb(f"wlb{i}", [128, D], F8) for i in range(2)]

        qx8 = sem("qx8_f1")
        qwg = dstream("qwg_f1")
        qa8 = dstream("qa8_f1")
        qwl = dstream("qwl_f1")
        qws = dstream("qws_f1")
        te = sem("te_f1")
        ve = sem("ve_f1")
        se = sem("se_f1")

        xn8_ps = xn8.ap().ap[0][0]
        for jc in range(NJC):
            src_ap = bass.AP(xn8_s.tensor, 256 * jc * OWN,
                             [[2 * OWN, 128], [OWN, 2], [1, OWN]])
            eng = SEr if jc % 2 == 0 else SY
            eng.dma_start(xn8[:, jc * 2 * OWN:(jc + 1) * 2 * OWN], src_ap
                          ).then_inc(qx8.h, 16)
            qx8.inc(16)

        def xn8_rhs(jc, tc):
            return bass.AP(xn8.ap().tensor,
                           xn8.ap().offset + jc * 2 * OWN + tc * T,
                           [[xn8_ps, 128], [OWN, 2], [1, T]])

        hc_grp_last = [0] * NHC

        def load_wg8(hc, dep_te):
            bi = hc % 2
            if dep_te:
                GE.wait_ge(te.h, dep_te)
            for j, wbuf in ((0, wg8a), (1, wg8b)):
                for i in range(2):
                    src_ap = bass.AP(wgt.tensor,
                                     j * D * HID + i * HID + hc * 128,
                                     [[2 * HID, 128], [256 * HID, NJC],
                                      [1, 128]])
                    dst_ap = bass.AP(wbuf[bi].ap().tensor,
                                     wbuf[bi].ap().offset + i * 128,
                                     [[wbuf[bi].ap().ap[0][0], 128],
                                      [256, NJC], [1, 128]])
                    GE.dma_start(dst_ap, src_ap).then_inc(qwg[bi].h, 16)
                    qwg[bi].inc(16)
            return qwg[bi].n

        def wg_lhsT(wbuf, jc):
            return bass.AP(wbuf.ap().tensor,
                           wbuf.ap().offset + jc * 256,
                           [[wbuf.ap().ap[0][0], 128], [128, 2], [1, 128]])

        wg_vals = [load_wg8(0, 0)]
        chain_used = [0, 0]
        pff_free = [(None, 0)] * 4
        TE.wait_ge(qx8.h, qx8.n)
        for hc in range(NHC):
            if hc + 1 < NHC:
                wg_vals.append(load_wg8(hc + 1,
                                        hc_grp_last[hc - 1] if hc >= 1 else 0))
            p = hc % 2
            GE.wait_ge(qws[p].h, qws[p].n)
            GE.dma_start(wlb[p][:, :], wlin[hc * 128:(hc + 1) * 128, :]
                         ).then_inc(qwl[p].h, 16)
            qwl[p].inc(16)
            GE.wait_ge(qwl[p].h, qwl[p].n)
            GE.dma_start(wl8_s[hc * 128:(hc + 1) * 128, :], wlb[p][:, :]
                         ).then_inc(qws[p].h, 16)
            qws[p].inc(16)

            TE.wait_ge(qwg[hc % 2].h, wg_vals[hc])
            for tcp in range(NC_OWN // 2):
                pp = tcp % 2
                pg = pFF[pp]
                pf = pFF[2 + pp]
                for bank, bank_i in ((pg, pp), (pf, 2 + pp)):
                    ctr, v = pff_free[bank_i]
                    if ctr is not None:
                        TE.wait_ge(ctr.h, v)
                for half in range(2):
                    tc = 2 * tcp + half
                    for bank, wbuf in ((pg, wg8a), (pf, wg8b)):
                        for jc in range(NJC):
                            _inst = TE.matmul(
                                bank[:, half * T:(half + 1) * T],
                                wg_lhsT(wbuf[hc % 2], jc),
                                xn8_rhs(jc, tc),
                                start=(jc == 0), stop=(jc == NJC - 1),
                                perf_mode=DR)
                _inst.then_inc(te.h, 1)
                grp_v = te.inc()
                hc_grp_last[hc] = grp_v
                SE.wait_ge(te.h, grp_v)
                SE.wait_ge(ve.h, chain_used[pp])
                SE.copy(ffg[pp][:, :], pg[:, :]).then_inc(se.h, 1)
                ffg_v = se.inc()
                pff_free[pp] = (se, ffg_v)
                SE.copy(ff1[pp][:, :], pf[:, :]).then_inc(se.h, 1)
                ff1_v = se.inc()
                pff_free[2 + pp] = (se, ff1_v)
                VE.wait_ge(se.h, ffg_v)
                VE.wait_ge(ve.h, chain_used[pp])
                VE.tensor_mul(gt1[pp][:, :], ffg[pp][:, :], ffg[pp][:, :])
                VE.tensor_mul(gt1[pp][:, :], gt1[pp][:, :], ffg[pp][:, :])
                VE.tensor_scalar_mul(gt1[pp][:, :], gt1[pp][:, :], C2BF)
                VE.tensor_add(gt1[pp][:, :], ffg[pp][:, :], gt1[pp][:, :])
                VE.tensor_scalar_mul(gt1[pp][:, :], gt1[pp][:, :], C1BF).then_inc(ve.h, 1)
                g5_v = ve.inc()
                SE.wait_ge(ve.h, g5_v)
                SE.activation(gt2[pp][:, :], gt1[pp][:, :], AF.Tanh).then_inc(se.h, 1)
                tanh_v = se.inc()
                VE.wait_ge(se.h, tanh_v)
                VE.tensor_scalar(gt2[pp][:, :], gt2[pp][:, :], 0.5, 0.5,
                                 op0=OP.mult, op1=OP.add)
                VE.tensor_mul(fgel[pp][:, :], ffg[pp][:, :], gt2[pp][:, :])
                VE.wait_ge(se.h, ff1_v)
                VE.tensor_mul(factb[pp][:, :], fgel[pp][:, :], ff1[pp][:, :])
                VE.wait_ge(qa8[pp].h, qa8[pp].n)
                VE.tensor_copy(fact8[pp][:, :], factb[pp][:, :]).then_inc(ve.h, 1)
                a8v = ve.inc()
                chain_used[pp] = a8v
                SY.wait_ge(ve.h, a8v)
                SY.dma_start(
                    act8_s[hc * 128:(hc + 1) * 128,
                           2 * tcp * T:(2 * tcp + 2) * T],
                    fact8[pp][:, :]).then_inc(qa8[pp].h, 16)
                qa8[pp].inc(16)
        barrier(queues=[(SY, qx8), (SY, qa8[0]), (SY, qa8[1]),
                        (GE, qwg[0]), (GE, qwg[1]), (GE, qwl[0]),
                        (GE, qwl[1]), (GE, qws[0]), (GE, qws[1])])

    if _STOP == "F1":
        emit_dbg_dumps()
        es.close()
        return nc

    with ExitStack() as ph:
        fsb = lambda nm, sh, dt_: ph.enter_context(nc.sbuf_tensor(nm, sh, dt_))
        banks8 = [ph.enter_context(nc.psum_tensor(f"pex{i}", [128, T], F32))
                  for i in range(8)]
        a8 = [[fsb(f"a8_{p}_{i}", [128, (NHC // 2) * T], F8) for i in range(2)]
              for p in range(2)]
        wl2 = [fsb(f"wl2_{i}", [128, 2048], F8) for i in range(4)]
        ffsl = fsb("ffsl", [128, NCH * T], BF16)
        outtm = [fsb(f"outtm{i}", [128, D], BF16) for i in range(2)]
        parb = [fsb(f"parb{i}", [128, D], F32) for i in range(2)]
        outb = [fsb(f"outb{i}", [128, D], F32) for i in range(2)]

        qa8 = dstream("qa8_f2")
        qwl2 = dstream("qwl2_f2", 4)
        qpar2 = dstream("qpar_f2")
        qout = dstream("qout_f2")
        qfT = dstream("qfT_f2")
        qtt = dstream("qtt_f2")
        te = sem("te_f2")
        ve = sem("ve_f2")

        NJC2 = NHC // 2
        def load_a8(tc, dep_te):
            p = tc % 2
            if dep_te:
                GE.wait_ge(te.h, dep_te)
            for jc in range(NJC2):
                half = jc // (NJC2 // 2)
                jj = jc % (NJC2 // 2)
                buf = a8[p][half]
                ps = buf.ap().ap[0][0]
                for i in range(2):
                    src_ap = bass.AP(act8_s.tensor,
                                     (256 * jc + i) * OWN + tc * T,
                                     [[2 * OWN, 128], [1, T]])
                    dst_ap = bass.AP(buf.ap().tensor,
                                     buf.ap().offset + jj * 2 * T + i * T,
                                     [[ps, 128], [1, T]])
                    GE.dma_start(dst_ap, src_ap).then_inc(qa8[p].h, 16)
                    qa8[p].inc(16)
            return qa8[p].n

        def a8_rhs(tc, jc):
            half = jc // (NJC2 // 2)
            jj = jc % (NJC2 // 2)
            buf = a8[tc % 2][half]
            return bass.AP(buf.ap().tensor,
                           buf.ap().offset + jj * 2 * T,
                           [[buf.ap().ap[0][0], 128], [T, 2], [1, T]])

        a8_vals = [load_a8(0, 0)]
        par_vals = {}
        tr_free2 = {}
        bank_busy = [0] * 8
        mm_last = 0
        wl_buf_user = [0, 0, 0, 0]
        wl_vals = {}

        SEr = nc.scalar

        def emit_par_load(i):
            SEr.wait_ge(ve.h, tr_free2.get((i - 2, 0), 0))
            SEr.dma_start(parb[i % 2][:, :], par_s[i * 128:(i + 1) * 128, :]
                          ).then_inc(qpar2[i % 2].h, 16)
            par_vals[i] = qpar2[i % 2].inc(16)

        emit_par_load(0)
        emit_par_load(1)

        def load_wl2(key, buf):
            tck, mh, jc = key
            SY.wait_ge(te.h, wl_buf_user[buf])
            ps = wl2[buf].ap().ap[0][0]
            for i in range(2):
                src_ap = bass.AP(wl8_s.tensor,
                                 (256 * jc + i) * D + mh * 1024,
                                 [[2 * D, 128], [1, 1024]])
                dst_ap = bass.AP(wl2[buf].ap().tensor,
                                 wl2[buf].ap().offset + i * 1024,
                                 [[ps, 128], [1, 1024]])
                SY.dma_start(dst_ap, src_ap).then_inc(qwl2[buf].h, 16)
                qwl2[buf].inc(16)
            wl_vals[key] = qwl2[buf].n

        def wl2_lhsT(buf, mi):
            return bass.AP(wl2[buf].ap().tensor,
                           wl2[buf].ap().offset + mi * 128,
                           [[wl2[buf].ap().ap[0][0], 128], [1024, 2],
                            [1, 128]])

        all_keys = [(tc_, mh_, jc_) for tc_ in range(NC_OWN)
                    for mh_ in range(2) for jc_ in range(NJC2)]
        for idx0 in range(3):
            load_wl2(all_keys[idx0], idx0 % 4)
        gstep = 0
        for tc in range(NC_OWN):
            p = tc % 2
            TE.wait_ge(qa8[p].h, a8_vals[tc])
            if tc + 1 < NC_OWN:
                a8_vals.append(load_a8(tc + 1, mm_last))
            evac_v = {}
            for mh in range(2):
                for jc in range(NJC2):
                    buf = gstep % 4
                    if gstep + 3 < len(all_keys):
                        load_wl2(all_keys[gstep + 3], (gstep + 3) % 4)
                    gstep += 1
                    TE.wait_ge(qwl2[buf].h, wl_vals[(tc, mh, jc)])
                    if jc == 0:
                        for mi in range(8):
                            if bank_busy[mi]:
                                TE.wait_ge(ve.h, bank_busy[mi])
                    rhs = a8_rhs(tc, jc)
                    for mi in range(8):
                        _inst = TE.matmul(
                            banks8[mi][:, :],
                            wl2_lhsT(buf, mi),
                            rhs,
                            start=(jc == 0), stop=(jc == NJC2 - 1),
                            perf_mode=mybir.MatmulPerfMode.DoubleRow)
                    _inst.then_inc(te.h, 1)
                    mm_last = te.inc()
                    wl_buf_user[buf] = mm_last
                for mi in range(8):
                    m = mh * 8 + mi
                    VE.wait_ge(te.h, mm_last)
                    if tc > 0 and mi == 0:
                        VE.wait_ge(qfT[0].h, qfT[0].n)
                        VE.wait_ge(qfT[1].h, qfT[1].n)
                    VE.tensor_copy(ffsl[:, m * T:(m + 1) * T],
                                   banks8[mi][:, :]).then_inc(ve.h, 1)
                    evac_v[m] = ve.inc()
                    bank_busy[mi] = evac_v[m]
                    SY.wait_ge(ve.h, evac_v[m])
                    SY.dma_start(
                        ffnT_s[m * 128:(m + 1) * 128, tc * T:(tc + 1) * T],
                        ffsl[:, m * T:(m + 1) * T]).then_inc(qfT[m % 2].h, 16)
                    qfT[m % 2].inc(16)
            fT_done = (qfT[0].n, qfT[1].n)
            for s in range(4):
                i = tc * 4 + s
                ob = outb[i % 2]
                SEr.wait_ge(ve.h, tr_free2.get((i - 2, 0), 0))
                SEr.wait_ge(qfT[0].h, fT_done[0])
                SEr.wait_ge(qfT[1].h, fT_done[1])
                SEr.dma_start_transpose(
                    outtm[i % 2][:, :],
                    ffnT_s[:, i * 128:(i + 1) * 128]).then_inc(qtt[i % 2].h, 16)
                qtt[i % 2].inc(16)
                if i >= 2:
                    VE.wait_ge(qout[i % 2].h, qout[i % 2].n)
                VE.wait_ge(qtt[i % 2].h, qtt[i % 2].n)
                VE.wait_ge(qpar2[i % 2].h, par_vals[i])
                VE.tensor_add(ob[:, :], outtm[i % 2][:, :],
                              parb[i % 2][:, :]).then_inc(ve.h, 1)
                tr_free2[(i, 0)] = ve.inc()
                SEr.wait_ge(ve.h, tr_free2[(i, 0)])
                SEr.dma_start(out[i * 128:(i + 1) * 128, :], ob[:, :]
                              ).then_inc(qout[i % 2].h, 16)
                qout[i % 2].inc(16)
                if i + 2 < OWN // 128:
                    emit_par_load(i + 2)
        SEr.wait_ge(qout[0].h, qout[0].n)
        SEr.wait_ge(qout[1].h, qout[1].n)
        SY.wait_ge(qfT[0].h, qfT[0].n)
        SY.wait_ge(qfT[1].h, qfT[1].n)
        GE.wait_ge(qa8[0].h, qa8[0].n)
        GE.wait_ge(qa8[1].h, qa8[1].n)
        emit_dbg_dumps()

    es.close()
    return nc


_NC_CACHE = None


def _get_nc():
    global _NC_CACHE
    if _NC_CACHE is None:
        _NC_CACHE = build()
    return _NC_CACHE


def _make_in_maps(x, Wq, Wv, Wg, Wo, beta, ln_g, ln_b, wgt_, wlin_):
    import ml_dtypes
    beta = np.ascontiguousarray(
        np.broadcast_to(beta.reshape(1, H), (128, H)))
    ln_g = np.ascontiguousarray(ln_g.reshape(NCH, 128).T)
    ln_b = np.ascontiguousarray(ln_b.reshape(NCH, 128).T)
    cid32 = np.eye(128, dtype=np.float32)
    cid16 = np.eye(128, dtype=ml_dtypes.bfloat16)
    conc = np.ones((128, 1), np.float32)
    conr = np.ones((1, 128), np.float32)
    in_maps = []
    for c in range(8):
        b, h = c % 4, c // 4
        xpv = np.zeros((NT, D), np.float32)
        if h == 0:
            xpv[PRE:] = x[b, 0:OWN]
        else:
            xpv[:] = x[b]
        in_maps.append({
            "xp": xpv, "wq": Wq, "wv": Wv, "wg": Wg, "wo": Wo,
            "beta": beta, "lng": ln_g, "lnb": ln_b,
            "wgt": wgt_, "wlin": wlin_,
            "cid32": cid32, "cid16": np.asarray(cid16),
            "conc": conc, "conr": conr,
        })
    return in_maps


def kernel(L0_attn_x, L0_attn_Wq, L0_attn_Wv, L0_attn_Wg, L0_attn_Wo,
           L0_attn_beta, L0_attn_ln_g, L0_attn_ln_b,
           L0_ffn_w_gating, L0_ffn_w_linear):
    x = np.ascontiguousarray(np.asarray(L0_attn_x, dtype=np.float32))
    Wq = np.ascontiguousarray(np.asarray(L0_attn_Wq, dtype=np.float32))
    Wv = np.ascontiguousarray(np.asarray(L0_attn_Wv, dtype=np.float32))
    Wg = np.ascontiguousarray(np.asarray(L0_attn_Wg, dtype=np.float32))
    Wo = np.ascontiguousarray(np.asarray(L0_attn_Wo, dtype=np.float32))
    beta = np.asarray(L0_attn_beta, dtype=np.float32)
    ln_g = np.asarray(L0_attn_ln_g, dtype=np.float32)
    ln_b = np.asarray(L0_attn_ln_b, dtype=np.float32)
    wgt_ = np.ascontiguousarray(np.asarray(L0_ffn_w_gating, dtype=np.float32))
    wlin_ = np.ascontiguousarray(np.asarray(L0_ffn_w_linear, dtype=np.float32))

    nc = _get_nc()
    in_maps = _make_in_maps(x, Wq, Wv, Wg, Wo, beta, ln_g, ln_b, wgt_, wlin_)
    res = run_bass_kernel_spmd(nc, in_maps, core_ids=list(range(8)))
    outp = np.empty((B_, S_, D), np.float32)
    for c in range(8):
        b, h = c % 4, c // 4
        outp[b, h * OWN:(h + 1) * OWN] = res.results[c]["out"]
    return outp


# version 17 (final)
# speedup vs baseline: 1.0701x; 1.0175x over previous
import numpy as np
from contextlib import ExitStack

import concourse.bass as bass
import concourse.mybir as mybir
from concourse.bass_utils import run_bass_kernel_spmd

F32 = mybir.dt.float32
BF16 = mybir.dt.bfloat16
F8 = mybir.dt.float8e4
AF = mybir.ActivationFunctionType
OP = mybir.AluOpType
AXX = mybir.AxisListType.X

B_, S_, D = 4, 8192, 2048
H, DH = 16, 128
NCH = D // 128
OWN = 4096
PRE = 4096
NT = PRE + OWN
T = 512
NC_ALL = NT // T
NC_OWN = OWN // T
HID = 8192
NHC = HID // 128
EPS = 1e-5
INV_D = 1.0 / D

C1BF = 0.796875
C2BF = 0.044677734375

_DBG = False
_STOP = None
_A_GROUPS = None
_A_MODE = "full"


class Ctr:
    __slots__ = ("h", "n")

    def __init__(self, h):
        self.h = h
        self.n = 0

    def inc(self, k=1):
        self.n += k
        return self.n


def build():
    nc = bass.Bass("TRN2", target_bir_lowering=False)

    xp = nc.dram_tensor("xp", [NT, D], F32, kind="ExternalInput").ap()
    wq = nc.dram_tensor("wq", [D, D], F32, kind="ExternalInput").ap()
    wv = nc.dram_tensor("wv", [D, D], F32, kind="ExternalInput").ap()
    wg = nc.dram_tensor("wg", [D, D], F32, kind="ExternalInput").ap()
    wo = nc.dram_tensor("wo", [D, D], F32, kind="ExternalInput").ap()
    beta = nc.dram_tensor("beta", [128, H], F32, kind="ExternalInput").ap()
    lng = nc.dram_tensor("lng", [128, NCH], F32, kind="ExternalInput").ap()
    lnb = nc.dram_tensor("lnb", [128, NCH], F32, kind="ExternalInput").ap()
    wgt = nc.dram_tensor("wgt", [2, D, HID], F32, kind="ExternalInput").ap()
    wlin = nc.dram_tensor("wlin", [HID, D], F32, kind="ExternalInput").ap()
    cid32 = nc.dram_tensor("cid32", [128, 128], F32, kind="ExternalInput").ap()
    cid16 = nc.dram_tensor("cid16", [128, 128], BF16, kind="ExternalInput").ap()
    conc = nc.dram_tensor("conc", [128, 1], F32, kind="ExternalInput").ap()
    conr = nc.dram_tensor("conr", [1, 128], F32, kind="ExternalInput").ap()
    out = nc.dram_tensor("out", [OWN, D], F32, kind="ExternalOutput").ap()

    xnbf_s = nc.dram_tensor("xnbf_s", [D, NT], BF16, kind="Internal").ap()
    xn8_s = nc.dram_tensor("xn8_s", [D, OWN], F8, kind="Internal").ap()
    st_s = nc.dram_tensor("st_s", [D, OWN], BF16, kind="Internal").ap()
    z1_s = nc.dram_tensor("z1_s", [D, OWN], BF16, kind="Internal").ap()
    z_s = nc.dram_tensor("z_s", [D, OWN], BF16, kind="Internal").ap()
    act8_s = nc.dram_tensor("act8_s", [HID, OWN], F8, kind="Internal").ap()
    par_s = nc.dram_tensor("par_s", [OWN, D], F32, kind="Internal").ap()
    wl8_s = nc.dram_tensor("wl8_s", [HID, D], F8, kind="Internal").ap()
    ffnT_s = nc.dram_tensor("ffnT_s", [D, OWN], BF16, kind="Internal").ap()
    wbf_s = nc.dram_tensor("wbf_s", [3 * D, D], BF16, kind="Internal").ap()

    dbg = {}
    if _DBG:
        for nm, sh, dt_ in [
            ("d_xnbf", [D, NT], BF16), ("d_xn8", [D, OWN], F8),
            ("d_st", [D, OWN], BF16), ("d_z1", [D, OWN], BF16),
            ("d_z", [D, OWN], BF16), ("d_par", [OWN, D], F32),
            ("d_act8", [HID, OWN], F8), ("d_lam", [128, H], F32),
        ]:
            dbg[nm] = nc.dram_tensor(nm, sh, dt_, kind="ExternalOutput").ap()

    class _EngWrap:

        def __init__(self, eng):
            self._e = eng
            self._w = {}
            self._r = {}

        @staticmethod
        def _iv(ap):
            lo = ap.offset
            hi = lo + 1
            for step, count in list(ap.ap)[1:]:
                hi += step * (count - 1)
            return (ap.tensor.name, lo, hi)

        @staticmethod
        def _hit(iv, table):
            spans = table.get(iv[0])
            if not spans:
                return False
            for lo, hi in spans:
                if iv[1] < hi and lo < iv[2]:
                    return True
            return False

        @staticmethod
        def _add(iv, table):
            table.setdefault(iv[0], []).append((iv[1], iv[2]))

        def drain(self, *a, **k):
            self._w.clear()
            self._r.clear()
            return self._e.drain(*a, **k)

        def __getattr__(self, name):
            attr = getattr(self._e, name)
            if name in ("wait_ge", "sem_inc", "dma_start"):
                return attr
            if not callable(attr):
                return attr

            def call(*args, **kwargs):
                aps = [a for a in args if isinstance(a, bass.AP)]
                outs = aps[:1]
                ins = aps[1:]
                for kwn, kv in kwargs.items():
                    if isinstance(kv, bass.AP):
                        (outs if kwn == "accum_out" else ins).append(kv)
                ivs_o = [self._iv(a) for a in outs]
                ivs_i = [self._iv(a) for a in ins]
                if (any(self._hit(iv, self._w) for iv in ivs_i)
                        or any(self._hit(iv, self._w) or self._hit(iv, self._r)
                               for iv in ivs_o)):
                    self._e.drain()
                    self._w.clear()
                    self._r.clear()
                for iv in ivs_o:
                    self._add(iv, self._w)
                for iv in ivs_i:
                    self._add(iv, self._r)
                return attr(*args, **kwargs)

            return call

    TE, GE, SY = nc.tensor, nc.gpsimd, nc.sync
    VE = _EngWrap(nc.vector)
    SE = _EngWrap(nc.scalar)
    GP = _EngWrap(nc.gpsimd)
    SEr = nc.scalar

    es = ExitStack()

    def sbuf(nm, sh, dt_):
        return es.enter_context(nc.sbuf_tensor(nm, sh, dt_))

    def psum(nm, sh, dt_):
        return es.enter_context(nc.psum_tensor(nm, sh, dt_))

    def sem(nm):
        return Ctr(es.enter_context(nc.semaphore(nm)))

    def dstream(nm, n=2):
        return [sem(f"{nm}{i}") for i in range(n)]

    sid32 = sbuf("sid32", [128, 128], F32)
    sid16 = sbuf("sid16", [128, 128], BF16)
    sonc = sbuf("sonc", [128, 1], F32)
    sonr = sbuf("sonr", [1, 128], F32)
    sbeta = sbuf("sbeta", [128, H], F32)
    slam = sbuf("slam", [128, H], F32)
    slng = sbuf("slng", [128, NCH], F32)
    slnb = sbuf("slnb", [128, NCH], F32)
    scarry = sbuf("scarry", [128, H], F32)
    barv = sbuf("barv", [1, 16], F32)
    bars = sbuf("bars", [1, 16], F32)


    bar = sem("bar")
    ALL_ENG = (TE, VE, SE, GE, SY)
    bar_k = [0]

    def barrier(queues=()):
        for eng, q in queues:
            eng.wait_ge(q.h, q.n)
        k = bar_k[0]
        bar_k[0] += 1
        TE.drain()
        nc.vector.tensor_copy(barv[0:1, k:k + 1],
                              sonc[0:1, 0:1]).then_inc(bar.h, 1)
        nc.scalar.copy(bars[0:1, k:k + 1],
                       sonc[0:1, 0:1]).then_inc(bar.h, 1)
        for eng in (TE, GE, SY):
            eng.sem_inc(bar.h, 1)
        bar.inc(5)
        for eng in ALL_ENG:
            eng.wait_ge(bar.h, bar.n)

    def emit_dbg_dumps():
        if not _DBG:
            return
        barrier()
        qd = sem("qs_dbg")
        for nm, src_ in [("d_xnbf", xnbf_s), ("d_xn8", xn8_s),
                         ("d_st", st_s), ("d_z1", z1_s), ("d_z", z_s),
                         ("d_par", par_s), ("d_act8", act8_s)]:
            SY.dma_start(dbg[nm], src_).then_inc(qd.h, 16)
            qd.inc(16)
        SY.wait_ge(qd.h, qd.n)

    def lam_bcast(hd, n):
        col = slam[:, hd:hd + 1]
        return bass.AP(col.tensor, col.offset, [[col.ap[0][0], 128], [0, n]])

    qgp = sem("qg_prep")
    for dst, src in [(sid32, cid32), (sid16, cid16), (sonc, conc),
                     (sonr, conr), (sbeta, beta), (slng, lng), (slnb, lnb)]:
        GE.dma_start(dst[:, :], src).then_inc(qgp.h, 16)
        qgp.inc(16)
    SE.wait_ge(qgp.h, qgp.n)
    nc.vector.wait_ge(qgp.h, qgp.n)
    SE.activation(slam[:, :], sbeta[:, :], AF.Sigmoid)
    if _DBG:
        sep = sem("se_prep")
        SE.drain()
        nc.scalar.sem_inc(sep.h, 1)
        sep.inc()
        qsp = sem("qs_prep")
        SY.wait_ge(sep.h, sep.n)
        SY.dma_start(dbg["d_lam"], slam[:, :]).then_inc(qsp.h, 16)
        qsp.inc(16)
        SY.wait_ge(qsp.h, qsp.n)
    barrier(queues=[(GE, qgp)])

    if _STOP == "P":
        es.close()
        return nc

    ws = ExitStack()
    w_sb = ws.enter_context(nc.sbuf_tensor("w_sb", [128, NCH * D], BF16))
    pMM = [ws.enter_context(nc.psum_tensor(f"pmm{i}", [128, T], F32))
           for i in range(4)]
    qg_w = [sem(f"qg_w{i}") for i in range(4)]
    qg_w2 = [sem(f"qgb_w{i}") for i in range(4)]

    def load_weight(wap, qgc):
        for k in range(NCH):
            GE.dma_start(w_sb[:, k * D:(k + 1) * D],
                         wap[k * 128:(k + 1) * 128, :]).then_inc(qgc.h, 16)
            qgc.inc(16)

    def load_weight_bf(wi, qgc, use_ge=True):
        qgc2 = qg_w2[wi + 1]
        for k in range(NCH):
            if k % 2 == 0 or not use_ge:
                eng, qq = SEr, qgc
            else:
                eng, qq = GE, qgc2
            eng.dma_start(w_sb[:, k * D:(k + 1) * D],
                          wbf_s[(wi * NCH + k) * 128:(wi * NCH + k + 1) * 128, :]
                          ).then_inc(qq.h, 16)
            qq.inc(16)

    load_weight(wv, qg_w[0])

    with ExitStack() as ph:
        asb = lambda nm, sh, dt_: ph.enter_context(nc.sbuf_tensor(nm, sh, dt_))
        xbuf = [asb(f"a_x{i}", [128, D], F32) for i in range(8)]
        xntm = [asb(f"a_xn{i}", [128, D], F32) for i in range(8)]
        stt_t = asb("a_stt", [128, 48], F32)
        stgbf = [asb(f"a_gbf{i}", [128, T], BF16) for i in range(2)]
        stg8 = [asb(f"a_g8{i}", [128, T], F8) for i in range(2)]
        wcb = [asb(f"a_wcb{i}", [128, D], BF16) for i in range(2)]
        pTR = [ph.enter_context(nc.psum_tensor(f"ptra{i}", [128, T], F32))
               for i in range(2)] + pMM[0:2]

        qwc = dstream("qwc_a")
        qwps = dstream("qwps_a")
        for wi, wap in enumerate((wq, wg, wo)):
            for k in range(NCH):
                p = k % 2
                if wi or k >= 2:
                    GE.wait_ge(qwps[p].h, qwps[p].n)
                GE.dma_start(wcb[p][:, :], wap[k * 128:(k + 1) * 128, :]
                             ).then_inc(qwc[p].h, 16)
                qwc[p].inc(16)
                GE.wait_ge(qwc[p].h, qwc[p].n)
                GE.dma_start(
                    wbf_s[(wi * NCH + k) * 128:(wi * NCH + k + 1) * 128, :],
                    wcb[p][:, :]).then_inc(qwps[p].h, 16)
                qwps[p].inc(16)

        qx = dstream("qx_a", 8)
        qbf = dstream("qbf_a")
        q8 = dstream("q8_a")
        ve = sem("ve_a")
        se = sem("se_a")
        te = sem("te_a")

        NTILE = NT // 128
        ld_done = [0] * NTILE
        xn_done = [0] * NTILE
        grp_tr_done = [0] * NC_ALL
        evac_done = {}

        def wait_xn(eng, i):
            v = xn_done[i]
            if isinstance(v, tuple):
                eng.wait_ge(se.h, v[1])
            else:
                eng.wait_ge(ve.h, v)

        def emit_x_load(i):
            if i >= 8:
                wait_xn(SY, i - 8)
            SY.dma_start(xbuf[i % 8][:, :], xp[i * 128:(i + 1) * 128, :]
                         ).then_inc(qx[i % 8].h, 16)
            ld_done[i] = qx[i % 8].inc(16)

        for i in range(8):
            emit_x_load(i)

        for g in range(NC_ALL):
            own_g = g >= NC_ALL - NC_OWN
            stt = stt_t[:, (g % 2) * 24:(g % 2) * 24 + 24]
            if g >= 2:
                VE.wait_ge(te.h, grp_tr_done[g - 2])
                SE.wait_ge(te.h, grp_tr_done[g - 2])
            for s in range(4):
                i = g * 4 + s
                xb = xbuf[i % 8]
                dmp = xntm[(g % 2) * 4 + s]
                VE.wait_ge(qx[i % 8].h, ld_done[i])
                SE.wait_ge(qx[i % 8].h, ld_done[i])
                if s < 2:
                    VE.reduce_sum(stt[:, s:s + 1], xb[:, :], axis=AXX)
                else:
                    SE.activation(dmp[:, :], xb[:, :], AF.Identity,
                                  accum_out=stt[:, s:s + 1])
                SE.activation(dmp[:, :], xb[:, :], AF.Square,
                              accum_out=stt[:, 4 + s:5 + s])
            SE.drain()
            nc.scalar.sem_inc(se.h, 1)
            sq_v = se.inc()
            VE.wait_ge(se.h, sq_v)
            VE.tensor_scalar_mul(stt[:, 8:12], stt[:, 0:4], -INV_D)
            VE.tensor_scalar_mul(stt[:, 12:16], stt[:, 4:8], INV_D)
            VE.tensor_mul(stt[:, 16:20], stt[:, 8:12], stt[:, 8:12])
            VE.tensor_sub(stt[:, 12:16], stt[:, 12:16], stt[:, 16:20])
            VE.tensor_scalar_add(stt[:, 12:16], stt[:, 12:16], EPS).then_inc(ve.h, 1)
            d_v = ve.inc()
            SE.wait_ge(ve.h, d_v)
            SE.activation(stt[:, 16:20], stt[:, 12:16], AF.Sqrt).then_inc(se.h, 1)
            s0_v = se.inc()
            VE.wait_ge(se.h, s0_v)
            VE.reciprocal(stt[:, 20:24], stt[:, 16:20])
            VE.tensor_mul(stt[:, 16:20], stt[:, 20:24], stt[:, 20:24])
            VE.tensor_mul(stt[:, 16:20], stt[:, 16:20], stt[:, 12:16])
            VE.tensor_scalar(stt[:, 16:20], stt[:, 16:20], -0.5, 1.5,
                             op0=OP.mult, op1=OP.add)
            VE.tensor_mul(stt[:, 20:24], stt[:, 20:24], stt[:, 16:20])
            VE.tensor_mul(stt[:, 12:16], stt[:, 8:12], stt[:, 20:24]).then_inc(ve.h, 1)
            r_v = ve.inc()
            SE.wait_ge(ve.h, r_v)
            for s in range(4):
                i = g * 4 + s
                xb = xbuf[i % 8]
                xt = xntm[(g % 2) * 4 + s]
                if s < 2:
                    VE.tensor_scalar(xt[:, :], xb[:, :], stt[:, 8 + s:9 + s],
                                     stt[:, 20 + s:21 + s],
                                     op0=OP.add, op1=OP.mult).then_inc(ve.h, 1)
                    xn_done[i] = ve.inc()
                else:
                    SE.activation(xt[:, :], xb[:, :], AF.Identity,
                                  bias=stt[:, 12 + s:13 + s],
                                  scale=stt[:, 20 + s:21 + s]).then_inc(se.h, 1)
                    xn_done[i] = ("se", se.inc())
                if i + 8 < NTILE:
                    emit_x_load(i + 8)

            for j in range(NCH):
                bank = pTR[j % 4]
                key = (g, j - 4) if j >= 4 else (g - 1, NCH - 4 + j)
                if key in evac_done:
                    vv, sv = evac_done[key]
                    TE.wait_ge(ve.h, vv)
                    TE.wait_ge(se.h, sv)
                if j == 0:
                    for s4 in range(4):
                        wait_xn(TE, g * 4 + s4)
                for s in range(4):
                    xt = xntm[(g % 2) * 4 + s]
                    _inst = TE.transpose(bank[:, s * 128:(s + 1) * 128],
                                         xt[:, j * 128:(j + 1) * 128],
                                         sid32[:, :])
                _inst.then_inc(te.h, 1)
                tr_v = te.inc()
                if j == NCH - 1:
                    grp_tr_done[g] = tr_v
                stb = stgbf[j % 2]
                st8 = stg8[j % 2]
                VE.wait_ge(te.h, tr_v)
                VE.wait_ge(qbf[j % 2].h, qbf[j % 2].n)
                VE.tensor_copy(stb[:, :], bank[:, :]).then_inc(ve.h, 1)
                bf_v = ve.inc()
                sv_8 = 0
                if own_g:
                    SE.wait_ge(ve.h, bf_v)
                    SE.wait_ge(q8[j % 2].h, q8[j % 2].n)
                    SE.copy(st8[:, :], bank[:, :]).then_inc(se.h, 1)
                    sv_8 = se.inc()
                evac_done[(g, j)] = (bf_v, sv_8)
                SY.wait_ge(ve.h, bf_v)
                SY.dma_start(xnbf_s[j * 128:(j + 1) * 128, g * T:(g + 1) * T],
                             stb[:, :]).then_inc(qbf[j % 2].h, 16)
                qbf[j % 2].inc(16)
                if own_g:
                    go = g - (NC_ALL - NC_OWN)
                    SY.wait_ge(se.h, sv_8)
                    SY.dma_start(
                        xn8_s[j * 128:(j + 1) * 128, go * T:(go + 1) * T],
                        st8[:, :]).then_inc(q8[j % 2].h, 16)
                    q8[j % 2].inc(16)

        barrier(queues=[(SY, q) for q in qx] +
                       [(SY, qbf[0]), (SY, qbf[1]),
                        (SY, q8[0]), (SY, q8[1]),
                        (GE, qwc[0]), (GE, qwc[1]),
                        (GE, qwps[0]), (GE, qwps[1])])

    if _STOP == "A":
        emit_dbg_dumps()
        ws.close()
        es.close()
        return nc

    def _emit_bcde():
      with ExitStack() as ph:
        bsb = lambda nm, sh, dt_: ph.enter_context(nc.sbuf_tensor(nm, sh, dt_))
        xnbuf = [bsb(f"xnb{i}", [128, NCH * T], BF16) for i in range(2)]
        v_sb = bsb("v_sb", [128, NCH * T], F32)
        s2_sb = bsb("s2_sb", [128, NCH * T], F32)

        qpre = dstream("qpre")
        pre_c = {}
        pre_d = {}
        pre_e = {}

        def load_fm_chunk(dst, src, c, qsc, dep=None, eng=SY):
            if dep is not None:
                eng.wait_ge(dep[0].h, dep[1])
            for k in range(NCH):
                eng.dma_start(dst[:, k * T:(k + 1) * T],
                              src[k * 128:(k + 1) * 128, c * T:(c + 1) * T]
                              ).then_inc(qsc.h, 16)
                qsc.inc(16)
            return qsc.n

        def emit_chunk_groups(c, xnb, te_c, free_list, free_sem, emit_evac,
                              hook=None):
            vals = []
            for m in range(NCH):
                if hook is not None:
                    hook(m)
                gi = c * NCH + m
                if gi >= 4:
                    TE.wait_ge(free_sem.h, free_list[gi - 4])
                bank = pMM[gi % 4]
                for k in range(NCH):
                    _inst = TE.matmul(
                        bank[:, :],
                        w_sb[:, k * D + m * 128: k * D + (m + 1) * 128],
                        xnb[:, k * T:(k + 1) * T],
                        start=(k == 0), stop=(k == NCH - 1))
                _inst.then_inc(te_c.h, 1)
                v = te_c.inc()
                vals.append(v)
                emit_evac(c, m, bank, v)
            return vals

        qg = qg_w[0]
        qg2 = qg_w2[0]
        qxn = dstream("qxn_b")
        qst = dstream("qst_b")
        te = sem("te_b")
        ve = sem("ve_b")
        se = sem("se_b")
        b_se_last = [0]
        with ExitStack() as pb:
            stg = [pb.enter_context(nc.sbuf_tensor(f"b_stg{i}", [128, T], BF16))
                   for i in range(2)]
            v_sb2 = pb.enter_context(
                nc.sbuf_tensor("v_sb2", [128, NCH * T], F32))
            vbufs = [v_sb, v_sb2]

            TE.wait_ge(qg.h, qg.n)
            TE.wait_ge(qg2.h, qg2.n)

            free_vals = []
            grp_last = [0] * NC_ALL

            def b_evac(c, m, bank, v):
                VE.wait_ge(te.h, v)
                VE.tensor_copy(vbufs[c % 2][:, m * T:(m + 1) * T],
                               bank[:, :]).then_inc(ve.h, 1)
                free_vals.append(ve.inc())

            def b_scan(c):
                vb = vbufs[c % 2]
                own_c = c >= NC_ALL - NC_OWN
                scan_v = {}
                if b_se_last[0]:
                    VE.wait_ge(se.h, b_se_last[0])
                for hd in range(H):
                    init = scarry[:, hd:hd + 1] if c > 0 else 0.0
                    VE.tensor_tensor_scan(
                        s2_sb[:, hd * T:(hd + 1) * T], lam_bcast(hd, T),
                        vb[:, hd * T:(hd + 1) * T], init,
                        op0=OP.mult, op1=OP.add).then_inc(ve.h, 1)
                    scan_v[hd] = ve.inc()
                lastcols = bass.AP(s2_sb.ap().tensor,
                                   s2_sb.ap().offset + T - 1,
                                   [[s2_sb.ap().ap[0][0], 128], [T, H]])
                VE.tensor_copy(scarry[:, :], lastcols)
                if own_c:
                    for hd in range(H):
                        SE.wait_ge(ve.h, scan_v[hd])
                        SE.wait_ge(qst[hd % 2].h, qst[hd % 2].n)
                        SE.copy(stg[hd % 2][:, :],
                                s2_sb[:, hd * T:(hd + 1) * T]).then_inc(se.h, 1)
                        sv = se.inc()
                        SY.wait_ge(se.h, sv)
                        co = c - (NC_ALL - NC_OWN)
                        SY.dma_start(
                            st_s[hd * 128:(hd + 1) * 128,
                                 co * T:(co + 1) * T],
                            stg[hd % 2][:, :]).then_inc(qst[hd % 2].h, 16)
                        qst[hd % 2].inc(16)
                    b_se_last[0] = se.n

            ld_vals = [(qxn[0], load_fm_chunk(xnbuf[0], xnbf_s, 0, qxn[0]))]
            for c in range(NC_ALL):
                if c + 1 < NC_ALL:
                    dep = (te, grp_last[c - 1]) if c >= 1 else None
                    ld_vals.append((qxn[(c + 1) % 2],
                                    load_fm_chunk(xnbuf[(c + 1) % 2], xnbf_s,
                                                  c + 1, qxn[(c + 1) % 2],
                                                  dep=dep)))
                TE.wait_ge(ld_vals[c][0].h, ld_vals[c][1])
                vals = emit_chunk_groups(c, xnbuf[c % 2], te, free_vals, ve,
                                         b_evac)
                grp_last[c] = vals[-1]
                if c == NC_ALL - 1:
                    pre_c[0] = (qpre[0], load_fm_chunk(
                        xnbuf[0], xnbf_s, NC_ALL - NC_OWN, qpre[0],
                        dep=(te, grp_last[NC_ALL - 2])))
                if c >= 1:
                    b_scan(c - 1)
            pre_c[1] = (qpre[1], load_fm_chunk(
                xnbuf[1], xnbf_s, NC_ALL - NC_OWN + 1, qpre[1],
                dep=(te, grp_last[NC_ALL - 1])))
            b_scan(NC_ALL - 1)
            SEr.wait_ge(te.h, grp_last[NC_ALL - 1])
            GE.wait_ge(te.h, grp_last[NC_ALL - 1])
            load_weight_bf(0, qg_w[1])
            barrier(queues=[(SY, qxn[0]), (SY, qxn[1]), (SY, qst[0]),
                            (SY, qst[1]), (GE, qg)])

        if _STOP == "B":
            return
        qg = qg_w[1]
        qg2 = qg_w2[1]
        qxn = dstream("qxn_c")
        qstl = dstream("qstl_c")
        qz1 = dstream("qz1_c")
        bc = sem("bc_c")
        te = sem("te_c")
        ve = sem("ve_c")
        se = sem("se_c")
        ge = sem("ge_c")
        with ExitStack() as pc:
            csb = lambda nm, sh, dt_: pc.enter_context(nc.sbuf_tensor(nm, sh, dt_))
            pst1 = pc.enter_context(nc.psum_tensor("pst1", [1, T], F32))
            pst2 = pc.enter_context(nc.psum_tensor("pst2", [1, T], F32))
            pTR = [pc.enter_context(nc.psum_tensor(f"ptrc{i}", [128, T], F32))
                   for i in range(2)]
            stbuf = [csb(f"stb{i}", [128, NCH * T], BF16) for i in range(2)]
            gztmp = csb("gztmp", [128, 1024], F32)
            musigrep = csb("musigrep", [128, T], F32)
            rsigrep = csb("rsigrep", [128, T], F32)
            ztmp = csb("ztmp", [128, T], F32)
            y2t = [csb("y2t0", [128, T], F32), ztmp]
            z1stg = [csb(f"z1s{i}", [128, T], BF16) for i in range(2)]

            ybufs = [v_sb, s2_sb]

            TE.wait_ge(qg.h, qg.n)
            TE.wait_ge(qg2.h, qg2.n)

            free_vals = []
            grp_last = [0] * NC_OWN
            y_last = [0] * NC_OWN
            st_vals = [0] * NC_OWN
            st2_v = {}
            row_done = [0] * NC_OWN
            z1_done = [0] * NC_OWN
            z1_pair = {}

            def c_evac(c, m, bank, v):
                VE.wait_ge(te.h, v)
                if m == 0:
                    VE.wait_ge(qstl[c % 2].h, st_vals[c])
                if c >= 2:
                    VE.wait_ge(ge.h, z1_pair[(c - 2, m // 2)])
                VE.tensor_tensor(ybufs[c % 2][:, m * T:(m + 1) * T], bank[:, :],
                                 stbuf[c % 2][:, m * T:(m + 1) * T],
                                 op=OP.mult).then_inc(ve.h, 1)
                free_vals.append(ve.inc())
                if m == NCH - 1:
                    y_last[c] = free_vals[-1]

            def c_stats_z1(c):
                y_sb = ybufs[c % 2]
                TE.wait_ge(ve.h, y_last[c])
                if c >= 1:
                    TE.wait_ge(ve.h, row_done[c - 1])
                for m in range(NCH):
                    _inst = TE.matmul(pst1[:, :], sonc[:, :],
                                      y_sb[:, m * T:(m + 1) * T],
                                      start=(m == 0), stop=(m == NCH - 1))
                _inst.then_inc(te.h, 1)
                te.inc()
                for m in range(NCH):
                    if m >= 2:
                        VE.wait_ge(te.h, st2_v[(c, m - 2)])
                    VE.tensor_mul(y2t[m % 2][:, :],
                                  y_sb[:, m * T:(m + 1) * T],
                                  y_sb[:, m * T:(m + 1) * T]).then_inc(ve.h, 1)
                    y2v = ve.inc()
                    TE.wait_ge(ve.h, y2v)
                    _inst = TE.matmul(pst2[:, :], sonc[:, :], y2t[m % 2][:, :],
                                      start=(m == 0), stop=(m == NCH - 1))
                    _inst.then_inc(te.h, 1)
                    st2_v[(c, m)] = te.inc()
                st_mm = st2_v[(c, NCH - 1)]
                VE.wait_ge(te.h, st_mm)
                if c >= 1:
                    VE.wait_ge(ge.h, z1_done[c - 1])
                rowb = pst2[0:1, :]
                rowN = gztmp[0:1, 0:T]
                VE.tensor_scalar_mul(rowN, pst1[0:1, :], -INV_D)
                VE.tensor_scalar_mul(rowb, rowb, INV_D)
                VE.tensor_mul(rsigrep[0:1, 0:T], rowN, rowN)
                VE.tensor_sub(rowb, rowb, rsigrep[0:1, 0:T])
                VE.tensor_scalar_add(rowb, rowb, EPS).then_inc(ve.h, 1)
                d_v = ve.inc()
                SE.wait_ge(ve.h, d_v)
                SE.activation(rsigrep[0:1, 0:T], rowb, AF.Sqrt).then_inc(se.h, 1)
                s0_v = se.inc()
                VE.wait_ge(se.h, s0_v)
                VE.reciprocal(musigrep[0:1, 0:T], rsigrep[0:1, 0:T])
                VE.tensor_mul(rsigrep[0:1, 0:T], musigrep[0:1, 0:T],
                              musigrep[0:1, 0:T])
                VE.tensor_mul(rsigrep[0:1, 0:T], rsigrep[0:1, 0:T],
                              rowb)
                VE.tensor_scalar(rsigrep[0:1, 0:T], rsigrep[0:1, 0:T],
                                 -0.5, 1.5, op0=OP.mult, op1=OP.add)
                VE.tensor_mul(rsigrep[0:1, 0:T], musigrep[0:1, 0:T],
                              rsigrep[0:1, 0:T])
                VE.tensor_mul(musigrep[0:1, 0:T], rowN,
                              rsigrep[0:1, 0:T]).then_inc(ve.h, 1)
                rows_v = ve.inc()
                row_done[c] = rows_v
                VE.wait_ge(bc.h, 2 * (c + 1))
                VE.tensor_copy(musigrep[:, :], pTR[0][:, :])
                VE.tensor_copy(rsigrep[:, :], pTR[1][:, :]).then_inc(ve.h, 1)
                rep_v = ve.inc()
                GP.wait_ge(ve.h, rep_v)
                rs_ps = rsigrep.ap().ap[0][0]
                rs2 = bass.AP(rsigrep.ap().tensor, rsigrep.ap().offset,
                              [[rs_ps, 128], [0, 2], [1, T]])
                mu2b = bass.AP(musigrep.ap().tensor, musigrep.ap().offset,
                               [[musigrep.ap().ap[0][0], 128], [0, 2], [1, T]])
                for mp in range(NCH // 2):
                    GP.tensor_mul(gztmp[:, :],
                                  y_sb[:, 2 * mp * T:(2 * mp + 2) * T], rs2)
                    GP.tensor_add(gztmp[:, :], gztmp[:, :], mu2b).then_inc(ge.h, 1)
                    gv = ge.inc()
                    z1_pair[(c, mp)] = gv
                    for h2 in range(2):
                        m = 2 * mp + h2
                        GP.wait_ge(qz1[m % 2].h, qz1[m % 2].n)
                        GP.tensor_scalar(z1stg[m % 2][:, :],
                                         gztmp[:, h2 * T:(h2 + 1) * T],
                                         slng[:, m:m + 1], slnb[:, m:m + 1],
                                         op0=OP.mult, op1=OP.add).then_inc(ge.h, 1)
                        zv = ge.inc()
                        SY.wait_ge(ge.h, zv)
                        SY.dma_start(
                            z1_s[m * 128:(m + 1) * 128, c * T:(c + 1) * T],
                            z1stg[m % 2][:, :]).then_inc(qz1[m % 2].h, 16)
                        qz1[m % 2].inc(16)
                z1_done[c] = ge.n

            def c_bcast(c):
                TE.wait_ge(ve.h, row_done[c])
                TE.matmul(pTR[0][:, :], sonr[:, :], musigrep[0:1, 0:T],
                          start=True, stop=True).then_inc(bc.h, 1)
                bc.inc()
                TE.matmul(pTR[1][:, :], sonr[:, :], rsigrep[0:1, 0:T],
                          start=True, stop=True).then_inc(bc.h, 1)
                bc.inc()

            ld_vals = [pre_c[0], pre_c[1]]
            st_vals[0] = load_fm_chunk(stbuf[0], st_s, 0, qstl[0], eng=SEr)
            for c in range(NC_OWN):
                if c + 1 < NC_OWN:
                    dep2 = (ve, y_last[c - 1]) if c >= 1 else None
                    st_vals[c + 1] = load_fm_chunk(stbuf[(c + 1) % 2], st_s,
                                                   c + 1, qstl[(c + 1) % 2],
                                                   dep=dep2, eng=SEr)
                TE.wait_ge(ld_vals[c][0].h, ld_vals[c][1])
                hook = None
                if c >= 2:
                    def hook(m, _cc=c - 2):
                        if m == 4:
                            c_bcast(_cc)
                vals = emit_chunk_groups(c, xnbuf[c % 2], te, free_vals, ve,
                                         c_evac, hook=hook)
                grp_last[c] = vals[-1]
                if c + 2 < NC_OWN:
                    ld_vals.append((qxn[c % 2],
                                    load_fm_chunk(xnbuf[c % 2], xnbf_s,
                                                  NC_ALL - NC_OWN + c + 2,
                                                  qxn[c % 2],
                                                  dep=(te, grp_last[c]))))
                if c == NC_OWN - 1:
                    pre_d[0] = (qpre[0], load_fm_chunk(
                        xnbuf[0], xnbf_s, NC_ALL - NC_OWN, qpre[0],
                        dep=(te, grp_last[NC_OWN - 2])))
                if c >= 1:
                    c_stats_z1(c - 1)
            pre_d[1] = (qpre[1], load_fm_chunk(
                xnbuf[1], xnbf_s, NC_ALL - NC_OWN + 1, qpre[1],
                dep=(te, grp_last[NC_OWN - 1])))
            c_bcast(NC_OWN - 2)
            c_stats_z1(NC_OWN - 1)
            c_bcast(NC_OWN - 1)
            SEr.wait_ge(te.h, grp_last[NC_OWN - 1])
            load_weight_bf(1, qg_w[2], use_ge=False)
            barrier(queues=[(SY, qxn[0]), (SY, qxn[1]), (SEr, qstl[0]),
                            (SEr, qstl[1]), (SY, qz1[0]), (SY, qz1[1]),
                            (SEr, qg)])

        if _STOP == "C":
            return
        qg = qg_w[2]
        qg2 = qg_w2[2]
        qxn = dstream("qxn_d")
        qz1l = dstream("qz1l_d")
        qzs = dstream("qzs_d")
        te = sem("te_d")
        se = sem("se_d")
        ve = sem("ve_d")
        with ExitStack() as pd:
            dsb = lambda nm, sh, dt_: pd.enter_context(nc.sbuf_tensor(nm, sh, dt_))
            z1buf = [dsb(f"z1b{i}", [128, NCH * T], BF16) for i in range(2)]
            gstg = [dsb(f"ph_g{i}", [128, T], BF16) for i in range(2)]
            zstg = [dsb(f"ph_z{i}", [128, T], BF16) for i in range(2)]

            TE.wait_ge(qg.h, qg.n)
            TE.wait_ge(qg2.h, qg2.n)
            free_vals = []
            grp_last = [0] * NC_OWN
            z_last = [0] * NC_OWN
            z1_vals = [0] * NC_OWN
            z_of = {}
            zs_after = {}

            def d_evac(c, m, bank, v):
                SE.wait_ge(te.h, v)
                if (c * NCH + m) >= 2:
                    SE.wait_ge(ve.h, z_of.get(c * NCH + m - 2, 0))
                SE.activation(gstg[m % 2][:, :], bank[:, :], AF.Silu).then_inc(se.h, 1)
                free_vals.append(se.inc())
                VE.wait_ge(se.h, free_vals[-1])
                if m == 0:
                    VE.wait_ge(qz1l[c % 2].h, z1_vals[c])
                VE.wait_ge(qzs[m % 2].h, qzs[m % 2].n)
                VE.tensor_mul(zstg[m % 2][:, :],
                              z1buf[c % 2][:, m * T:(m + 1) * T],
                              gstg[m % 2][:, :]).then_inc(ve.h, 1)
                zv = ve.inc()
                z_of[c * NCH + m] = zv
                if m == NCH - 1:
                    z_last[c] = zv
                SY.wait_ge(ve.h, zv)
                SY.dma_start(z_s[m * 128:(m + 1) * 128, c * T:(c + 1) * T],
                             zstg[m % 2][:, :]).then_inc(qzs[m % 2].h, 16)
                qzs[m % 2].inc(16)

            ld_vals = [pre_d[0], pre_d[1]]
            z1_vals[0] = load_fm_chunk(z1buf[0], z1_s, 0, qz1l[0], eng=SEr)
            for c in range(NC_OWN):
                if c + 1 < NC_OWN:
                    dep2 = (ve, z_last[c - 1]) if c >= 1 else None
                    z1_vals[c + 1] = load_fm_chunk(z1buf[(c + 1) % 2], z1_s,
                                                   c + 1, qz1l[(c + 1) % 2],
                                                   dep=dep2, eng=SEr)
                TE.wait_ge(ld_vals[c][0].h, ld_vals[c][1])
                vals = emit_chunk_groups(c, xnbuf[c % 2], te, free_vals, se,
                                         d_evac)
                grp_last[c] = vals[-1]
                zs_after[c] = (qzs[0].n, qzs[1].n)
                if c + 2 < NC_OWN:
                    ld_vals.append((qxn[c % 2],
                                    load_fm_chunk(xnbuf[c % 2], xnbf_s,
                                                  NC_ALL - NC_OWN + c + 2,
                                                  qxn[c % 2],
                                                  dep=(te, grp_last[c]))))
            SEr.wait_ge(te.h, grp_last[NC_OWN - 1])
            GE.wait_ge(te.h, grp_last[NC_OWN - 1])
            load_weight_bf(2, qg_w[3])
            SY.wait_ge(qzs[0].h, zs_after[0][0])
            SY.wait_ge(qzs[1].h, zs_after[0][1])
            pre_e[0] = (qpre[0], load_fm_chunk(
                xnbuf[0], z_s, 0, qpre[0],
                dep=(te, grp_last[NC_OWN - 2])))
            SY.wait_ge(qzs[0].h, zs_after[1][0])
            SY.wait_ge(qzs[1].h, zs_after[1][1])
            pre_e[1] = (qpre[1], load_fm_chunk(
                xnbuf[1], z_s, 1, qpre[1],
                dep=(te, grp_last[NC_OWN - 1])))
            barrier(queues=[(SY, qxn[0]), (SY, qxn[1]), (SEr, qz1l[0]),
                            (SEr, qz1l[1]), (SY, qzs[0]), (SY, qzs[1]),
                            (SEr, qg)])

        if _STOP == "D":
            return
        qg = qg_w[3]
        qg2 = qg_w2[3]
        qzl = dstream("qzl_e")
        qxe = dstream("qxe_e")
        qpar = dstream("qpar_e")
        te = sem("te_e")
        ve = sem("ve_e")
        with ExitStack() as pe:
            esb = lambda nm, sh, dt_: pe.enter_context(nc.sbuf_tensor(nm, sh, dt_))
            pTR = [pe.enter_context(nc.psum_tensor(f"ptre{i}", [128, T], F32))
                   for i in range(4)]
            xbe = [esb(f"e_x{i}", [128, D], F32) for i in range(2)]
            oste = [esb(f"e_o{i}", [128, D], F32) for i in range(2)]
            attn_sb = v_sb

            TE.wait_ge(qg.h, qg.n)
            TE.wait_ge(qg2.h, qg2.n)
            free_vals = []
            grp_last = [0] * NC_OWN
            evv = {}
            x_vals = {}
            out_done = {}
            tr_free = {}

            def e_evac(c, m, bank, v):
                VE.wait_ge(te.h, v)
                VE.tensor_copy(attn_sb[:, m * T:(m + 1) * T], bank[:, :]).then_inc(ve.h, 1)
                free_vals.append(ve.inc())
                evv[(c, m)] = free_vals[-1]

            def emit_xe_load(i):
                if i >= 2:
                    SY.wait_ge(ve.h, out_done.get(i - 2, 0))
                SY.dma_start(xbe[i % 2][:, :],
                             xp[PRE + i * 128: PRE + (i + 1) * 128, :]
                             ).then_inc(qxe[i % 2].h, 16)
                x_vals[i] = qxe[i % 2].inc(16)

            ld_vals = [pre_e[0], pre_e[1]]
            emit_xe_load(0)
            emit_xe_load(1)
            for c in range(NC_OWN):
                TE.wait_ge(ld_vals[c][0].h, ld_vals[c][1])
                vals = emit_chunk_groups(c, xnbuf[c % 2], te, free_vals, ve,
                                         e_evac)
                grp_last[c] = vals[-1]
                if c + 2 < NC_OWN:
                    ld_vals.append((qzl[c % 2],
                                    load_fm_chunk(xnbuf[c % 2], z_s, c + 2,
                                                  qzl[c % 2],
                                                  dep=(te, grp_last[c]),
                                                  eng=SEr)))
                for s in range(4):
                    i = c * 4 + s
                    ob = oste[i % 2]
                    if i >= 2:
                        VE.wait_ge(qpar[i % 2].h, qpar[i % 2].n)
                    for mg in range(4):
                        bank = pTR[mg % 4]
                        fkey = (i - 1, mg)
                        if fkey in tr_free:
                            TE.wait_ge(ve.h, tr_free[fkey])
                        for mi in range(4):
                            m = mg * 4 + mi
                            TE.wait_ge(ve.h, evv[(c, m)])
                            _inst = TE.transpose(
                                bank[:, mi * 128:(mi + 1) * 128],
                                attn_sb[:, m * T + s * 128:
                                        m * T + (s + 1) * 128],
                                sid32[:, :])
                        _inst.then_inc(te.h, 1)
                        trv = te.inc()
                        VE.wait_ge(te.h, trv)
                        VE.wait_ge(qxe[i % 2].h, x_vals[i])
                        VE.tensor_add(ob[:, mg * 512:(mg + 1) * 512],
                                      bank[:, :],
                                      xbe[i % 2][:, mg * 512:(mg + 1) * 512]).then_inc(ve.h, 1)
                        tr_free[(i, mg)] = ve.inc()
                    out_done[i] = tr_free[(i, 3)]
                    SY.wait_ge(ve.h, out_done[i])
                    SY.dma_start(par_s[i * 128:(i + 1) * 128, :], ob[:, :]
                                 ).then_inc(qpar[i % 2].h, 16)
                    qpar[i % 2].inc(16)
                    if i + 2 < OWN // 128:
                        emit_xe_load(i + 2)
            barrier(queues=[(SEr, qzl[0]), (SEr, qzl[1]), (SY, qxe[0]),
                            (SY, qxe[1]), (SY, qpar[0]), (SY, qpar[1]),
                            (SEr, qg)])
    _emit_bcde()
    ws.close()

    if _STOP in ("B", "C", "D", "E"):
        emit_dbg_dumps()
        es.close()
        return nc

    DR = mybir.MatmulPerfMode.DoubleRow
    NJC = NCH // 2
    with ExitStack() as ph:
        fsb = lambda nm, sh, dt_: ph.enter_context(nc.sbuf_tensor(nm, sh, dt_))
        pFF = [ph.enter_context(nc.psum_tensor(f"pff{i}", [128, 1024], F32))
               for i in range(4)]
        xn8 = fsb("xn8", [128, NCH * OWN], F8)
        wg8a = [fsb(f"wg8a{i}", [128, NCH * 128], F8) for i in range(2)]
        wg8b = [fsb(f"wg8b{i}", [128, NCH * 128], F8) for i in range(2)]
        W2 = 1024
        ffg = [fsb(f"ffg{i}", [128, W2], BF16) for i in range(2)]
        fgel = [fsb(f"fgel{i}", [128, W2], BF16) for i in range(2)]
        gt1 = [fsb(f"gt1_{i}", [128, W2], BF16) for i in range(2)]
        gt2 = [fsb(f"gt2_{i}", [128, W2], BF16) for i in range(2)]
        ff1 = [fsb(f"ff1{i}", [128, W2], BF16) for i in range(2)]
        factb = [fsb(f"factb{i}", [128, W2], BF16) for i in range(2)]
        fact8 = [fsb(f"fact8{i}", [128, W2], F8) for i in range(2)]
        wlb = [fsb(f"wlb{i}", [128, D], F8) for i in range(2)]

        qx8 = sem("qx8_f1")
        qwg = dstream("qwg_f1")
        qa8 = dstream("qa8_f1")
        qwl = dstream("qwl_f1")
        qws = dstream("qws_f1")
        te = sem("te_f1")
        ve = sem("ve_f1")
        se = sem("se_f1")

        xn8_ps = xn8.ap().ap[0][0]
        for jc in range(NJC):
            src_ap = bass.AP(xn8_s.tensor, 256 * jc * OWN,
                             [[2 * OWN, 128], [OWN, 2], [1, OWN]])
            eng = SEr if jc % 2 == 0 else SY
            eng.dma_start(xn8[:, jc * 2 * OWN:(jc + 1) * 2 * OWN], src_ap
                          ).then_inc(qx8.h, 16)
            qx8.inc(16)

        def xn8_rhs(jc, tc):
            return bass.AP(xn8.ap().tensor,
                           xn8.ap().offset + jc * 2 * OWN + tc * T,
                           [[xn8_ps, 128], [OWN, 2], [1, T]])

        hc_grp_last = [0] * NHC

        def load_wg8(hc, dep_te):
            bi = hc % 2
            if dep_te:
                GE.wait_ge(te.h, dep_te)
            for j, wbuf in ((0, wg8a), (1, wg8b)):
                for i in range(2):
                    src_ap = bass.AP(wgt.tensor,
                                     j * D * HID + i * HID + hc * 128,
                                     [[2 * HID, 128], [256 * HID, NJC],
                                      [1, 128]])
                    dst_ap = bass.AP(wbuf[bi].ap().tensor,
                                     wbuf[bi].ap().offset + i * 128,
                                     [[wbuf[bi].ap().ap[0][0], 128],
                                      [256, NJC], [1, 128]])
                    GE.dma_start(dst_ap, src_ap).then_inc(qwg[bi].h, 16)
                    qwg[bi].inc(16)
            return qwg[bi].n

        def wg_lhsT(wbuf, jc):
            return bass.AP(wbuf.ap().tensor,
                           wbuf.ap().offset + jc * 256,
                           [[wbuf.ap().ap[0][0], 128], [128, 2], [1, 128]])

        wg_vals = [load_wg8(0, 0)]
        chain_used = [0, 0]
        pff_free = [(None, 0)] * 4
        TE.wait_ge(qx8.h, qx8.n)
        for hc in range(NHC):
            if hc + 1 < NHC:
                wg_vals.append(load_wg8(hc + 1,
                                        hc_grp_last[hc - 1] if hc >= 1 else 0))
            p = hc % 2
            GE.wait_ge(qws[p].h, qws[p].n)
            GE.dma_start(wlb[p][:, :], wlin[hc * 128:(hc + 1) * 128, :]
                         ).then_inc(qwl[p].h, 16)
            qwl[p].inc(16)
            GE.wait_ge(qwl[p].h, qwl[p].n)
            GE.dma_start(wl8_s[hc * 128:(hc + 1) * 128, :], wlb[p][:, :]
                         ).then_inc(qws[p].h, 16)
            qws[p].inc(16)

            TE.wait_ge(qwg[hc % 2].h, wg_vals[hc])
            for tcp in range(NC_OWN // 2):
                pp = tcp % 2
                pg = pFF[pp]
                pf = pFF[2 + pp]
                for bank, bank_i in ((pg, pp), (pf, 2 + pp)):
                    ctr, v = pff_free[bank_i]
                    if ctr is not None:
                        TE.wait_ge(ctr.h, v)
                for half in range(2):
                    tc = 2 * tcp + half
                    for bank, wbuf in ((pg, wg8a), (pf, wg8b)):
                        for jc in range(NJC):
                            _inst = TE.matmul(
                                bank[:, half * T:(half + 1) * T],
                                wg_lhsT(wbuf[hc % 2], jc),
                                xn8_rhs(jc, tc),
                                start=(jc == 0), stop=(jc == NJC - 1),
                                perf_mode=DR)
                _inst.then_inc(te.h, 1)
                grp_v = te.inc()
                hc_grp_last[hc] = grp_v
                SE.wait_ge(te.h, grp_v)
                SE.wait_ge(ve.h, chain_used[pp])
                SE.copy(ffg[pp][:, :], pg[:, :]).then_inc(se.h, 1)
                ffg_v = se.inc()
                pff_free[pp] = (se, ffg_v)
                SE.copy(ff1[pp][:, :], pf[:, :]).then_inc(se.h, 1)
                ff1_v = se.inc()
                pff_free[2 + pp] = (se, ff1_v)
                SE.activation(fgel[pp][:, :], ffg[pp][:, :],
                              AF.Gelu_apprx_tanh).then_inc(se.h, 1)
                gel_v = se.inc()
                VE.wait_ge(se.h, gel_v)
                VE.wait_ge(ve.h, chain_used[pp])
                VE.tensor_mul(factb[pp][:, :], fgel[pp][:, :], ff1[pp][:, :])
                VE.wait_ge(qa8[pp].h, qa8[pp].n)
                VE.tensor_copy(fact8[pp][:, :], factb[pp][:, :]).then_inc(ve.h, 1)
                a8v = ve.inc()
                chain_used[pp] = a8v
                SY.wait_ge(ve.h, a8v)
                SY.dma_start(
                    act8_s[hc * 128:(hc + 1) * 128,
                           2 * tcp * T:(2 * tcp + 2) * T],
                    fact8[pp][:, :]).then_inc(qa8[pp].h, 16)
                qa8[pp].inc(16)
        barrier(queues=[(SY, qx8), (SY, qa8[0]), (SY, qa8[1]),
                        (GE, qwg[0]), (GE, qwg[1]), (GE, qwl[0]),
                        (GE, qwl[1]), (GE, qws[0]), (GE, qws[1])])

    if _STOP == "F1":
        emit_dbg_dumps()
        es.close()
        return nc

    with ExitStack() as ph:
        fsb = lambda nm, sh, dt_: ph.enter_context(nc.sbuf_tensor(nm, sh, dt_))
        banks8 = [ph.enter_context(nc.psum_tensor(f"pex{i}", [128, T], F32))
                  for i in range(8)]
        a8 = [[fsb(f"a8_{p}_{i}", [128, (NHC // 2) * T], F8) for i in range(2)]
              for p in range(2)]
        wl2 = [fsb(f"wl2_{i}", [128, 2048], F8) for i in range(4)]
        ffsl = fsb("ffsl", [128, NCH * T], BF16)
        outtm = [fsb(f"outtm{i}", [128, D], BF16) for i in range(2)]
        parb = [fsb(f"parb{i}", [128, D], F32) for i in range(2)]
        outb = [fsb(f"outb{i}", [128, D], F32) for i in range(2)]

        qa8 = dstream("qa8_f2")
        qwl2 = dstream("qwl2_f2", 4)
        qpar2 = dstream("qpar_f2")
        qout = dstream("qout_f2")
        qfT = dstream("qfT_f2")
        qtt = dstream("qtt_f2")
        te = sem("te_f2")
        ve = sem("ve_f2")

        NJC2 = NHC // 2
        def load_a8(tc, dep_te):
            p = tc % 2
            if dep_te:
                GE.wait_ge(te.h, dep_te)
            for jc in range(NJC2):
                half = jc // (NJC2 // 2)
                jj = jc % (NJC2 // 2)
                buf = a8[p][half]
                ps = buf.ap().ap[0][0]
                for i in range(2):
                    src_ap = bass.AP(act8_s.tensor,
                                     (256 * jc + i) * OWN + tc * T,
                                     [[2 * OWN, 128], [1, T]])
                    dst_ap = bass.AP(buf.ap().tensor,
                                     buf.ap().offset + jj * 2 * T + i * T,
                                     [[ps, 128], [1, T]])
                    GE.dma_start(dst_ap, src_ap).then_inc(qa8[p].h, 16)
                    qa8[p].inc(16)
            return qa8[p].n

        def a8_rhs(tc, jc):
            half = jc // (NJC2 // 2)
            jj = jc % (NJC2 // 2)
            buf = a8[tc % 2][half]
            return bass.AP(buf.ap().tensor,
                           buf.ap().offset + jj * 2 * T,
                           [[buf.ap().ap[0][0], 128], [T, 2], [1, T]])

        a8_vals = [load_a8(0, 0)]
        par_vals = {}
        tr_free2 = {}
        bank_busy = [0] * 8
        mm_last = 0
        wl_buf_user = [0, 0, 0, 0]
        wl_vals = {}

        SEr = nc.scalar

        def emit_par_load(i):
            SEr.wait_ge(ve.h, tr_free2.get((i - 2, 0), 0))
            SEr.dma_start(parb[i % 2][:, :], par_s[i * 128:(i + 1) * 128, :]
                          ).then_inc(qpar2[i % 2].h, 16)
            par_vals[i] = qpar2[i % 2].inc(16)

        emit_par_load(0)
        emit_par_load(1)

        def load_wl2(key, buf):
            tck, mh, jc = key
            SY.wait_ge(te.h, wl_buf_user[buf])
            ps = wl2[buf].ap().ap[0][0]
            for i in range(2):
                src_ap = bass.AP(wl8_s.tensor,
                                 (256 * jc + i) * D + mh * 1024,
                                 [[2 * D, 128], [1, 1024]])
                dst_ap = bass.AP(wl2[buf].ap().tensor,
                                 wl2[buf].ap().offset + i * 1024,
                                 [[ps, 128], [1, 1024]])
                SY.dma_start(dst_ap, src_ap).then_inc(qwl2[buf].h, 16)
                qwl2[buf].inc(16)
            wl_vals[key] = qwl2[buf].n

        def wl2_lhsT(buf, mi):
            return bass.AP(wl2[buf].ap().tensor,
                           wl2[buf].ap().offset + mi * 128,
                           [[wl2[buf].ap().ap[0][0], 128], [1024, 2],
                            [1, 128]])

        all_keys = [(tc_, mh_, jc_) for tc_ in range(NC_OWN)
                    for mh_ in range(2) for jc_ in range(NJC2)]
        for idx0 in range(3):
            load_wl2(all_keys[idx0], idx0 % 4)
        gstep = 0
        for tc in range(NC_OWN):
            p = tc % 2
            TE.wait_ge(qa8[p].h, a8_vals[tc])
            if tc + 1 < NC_OWN:
                a8_vals.append(load_a8(tc + 1, mm_last))
            evac_v = {}
            for mh in range(2):
                for jc in range(NJC2):
                    buf = gstep % 4
                    if gstep + 3 < len(all_keys):
                        load_wl2(all_keys[gstep + 3], (gstep + 3) % 4)
                    gstep += 1
                    TE.wait_ge(qwl2[buf].h, wl_vals[(tc, mh, jc)])
                    if jc == 0:
                        for mi in range(8):
                            if bank_busy[mi]:
                                TE.wait_ge(ve.h, bank_busy[mi])
                    rhs = a8_rhs(tc, jc)
                    for mi in range(8):
                        _inst = TE.matmul(
                            banks8[mi][:, :],
                            wl2_lhsT(buf, mi),
                            rhs,
                            start=(jc == 0), stop=(jc == NJC2 - 1),
                            perf_mode=mybir.MatmulPerfMode.DoubleRow)
                    _inst.then_inc(te.h, 1)
                    mm_last = te.inc()
                    wl_buf_user[buf] = mm_last
                for mi in range(8):
                    m = mh * 8 + mi
                    VE.wait_ge(te.h, mm_last)
                    if tc > 0 and mi == 0:
                        VE.wait_ge(qfT[0].h, qfT[0].n)
                        VE.wait_ge(qfT[1].h, qfT[1].n)
                    VE.tensor_copy(ffsl[:, m * T:(m + 1) * T],
                                   banks8[mi][:, :]).then_inc(ve.h, 1)
                    evac_v[m] = ve.inc()
                    bank_busy[mi] = evac_v[m]
                    SY.wait_ge(ve.h, evac_v[m])
                    SY.dma_start(
                        ffnT_s[m * 128:(m + 1) * 128, tc * T:(tc + 1) * T],
                        ffsl[:, m * T:(m + 1) * T]).then_inc(qfT[m % 2].h, 16)
                    qfT[m % 2].inc(16)
            fT_done = (qfT[0].n, qfT[1].n)
            for s in range(4):
                i = tc * 4 + s
                ob = outb[i % 2]
                SEr.wait_ge(ve.h, tr_free2.get((i - 2, 0), 0))
                SEr.wait_ge(qfT[0].h, fT_done[0])
                SEr.wait_ge(qfT[1].h, fT_done[1])
                SEr.dma_start_transpose(
                    outtm[i % 2][:, :],
                    ffnT_s[:, i * 128:(i + 1) * 128]).then_inc(qtt[i % 2].h, 16)
                qtt[i % 2].inc(16)
                if i >= 2:
                    VE.wait_ge(qout[i % 2].h, qout[i % 2].n)
                VE.wait_ge(qtt[i % 2].h, qtt[i % 2].n)
                VE.wait_ge(qpar2[i % 2].h, par_vals[i])
                VE.tensor_add(ob[:, :], outtm[i % 2][:, :],
                              parb[i % 2][:, :]).then_inc(ve.h, 1)
                tr_free2[(i, 0)] = ve.inc()
                SEr.wait_ge(ve.h, tr_free2[(i, 0)])
                SEr.dma_start(out[i * 128:(i + 1) * 128, :], ob[:, :]
                              ).then_inc(qout[i % 2].h, 16)
                qout[i % 2].inc(16)
                if i + 2 < OWN // 128:
                    emit_par_load(i + 2)
        SEr.wait_ge(qout[0].h, qout[0].n)
        SEr.wait_ge(qout[1].h, qout[1].n)
        SY.wait_ge(qfT[0].h, qfT[0].n)
        SY.wait_ge(qfT[1].h, qfT[1].n)
        GE.wait_ge(qa8[0].h, qa8[0].n)
        GE.wait_ge(qa8[1].h, qa8[1].n)
        emit_dbg_dumps()

    es.close()
    return nc


_NC_CACHE = None


def _get_nc():
    global _NC_CACHE
    if _NC_CACHE is None:
        _NC_CACHE = build()
    return _NC_CACHE


def _make_in_maps(x, Wq, Wv, Wg, Wo, beta, ln_g, ln_b, wgt_, wlin_):
    import ml_dtypes
    beta = np.ascontiguousarray(
        np.broadcast_to(beta.reshape(1, H), (128, H)))
    ln_g = np.ascontiguousarray(ln_g.reshape(NCH, 128).T)
    ln_b = np.ascontiguousarray(ln_b.reshape(NCH, 128).T)
    cid32 = np.eye(128, dtype=np.float32)
    cid16 = np.eye(128, dtype=ml_dtypes.bfloat16)
    conc = np.ones((128, 1), np.float32)
    conr = np.ones((1, 128), np.float32)
    in_maps = []
    for c in range(8):
        b, h = c % 4, c // 4
        xpv = np.zeros((NT, D), np.float32)
        if h == 0:
            xpv[PRE:] = x[b, 0:OWN]
        else:
            xpv[:] = x[b]
        in_maps.append({
            "xp": xpv, "wq": Wq, "wv": Wv, "wg": Wg, "wo": Wo,
            "beta": beta, "lng": ln_g, "lnb": ln_b,
            "wgt": wgt_, "wlin": wlin_,
            "cid32": cid32, "cid16": np.asarray(cid16),
            "conc": conc, "conr": conr,
        })
    return in_maps


def kernel(L0_attn_x, L0_attn_Wq, L0_attn_Wv, L0_attn_Wg, L0_attn_Wo,
           L0_attn_beta, L0_attn_ln_g, L0_attn_ln_b,
           L0_ffn_w_gating, L0_ffn_w_linear):
    x = np.ascontiguousarray(np.asarray(L0_attn_x, dtype=np.float32))
    Wq = np.ascontiguousarray(np.asarray(L0_attn_Wq, dtype=np.float32))
    Wv = np.ascontiguousarray(np.asarray(L0_attn_Wv, dtype=np.float32))
    Wg = np.ascontiguousarray(np.asarray(L0_attn_Wg, dtype=np.float32))
    Wo = np.ascontiguousarray(np.asarray(L0_attn_Wo, dtype=np.float32))
    beta = np.asarray(L0_attn_beta, dtype=np.float32)
    ln_g = np.asarray(L0_attn_ln_g, dtype=np.float32)
    ln_b = np.asarray(L0_attn_ln_b, dtype=np.float32)
    wgt_ = np.ascontiguousarray(np.asarray(L0_ffn_w_gating, dtype=np.float32))
    wlin_ = np.ascontiguousarray(np.asarray(L0_ffn_w_linear, dtype=np.float32))

    nc = _get_nc()
    in_maps = _make_in_maps(x, Wq, Wv, Wg, Wo, beta, ln_g, ln_b, wgt_, wlin_)
    res = run_bass_kernel_spmd(nc, in_maps, core_ids=list(range(8)))
    outp = np.empty((B_, S_, D), np.float32)
    for c in range(8):
        b, h = c % 4, c // 4
        outp[b, h * OWN:(h + 1) * OWN] = res.results[c]["out"]
    return outp
